# version 09
# baseline (speedup 1.0000x reference)
import contextlib
import numpy as np

import concourse.bass as bass
import concourse.mybir as mybir
from concourse.bass_utils import run_bass_kernel_spmd
from concourse.alu_op_type import AluOpType as Op

F32 = mybir.dt.float32
BF16 = mybir.dt.bfloat16
AF = mybir.ActivationFunctionType

NCORE = 8
BB, LSEQ, DM = 2, 8192, 1024
HPC = 8
CPC = 512
FOP = 2176
OC = 512
QC = 128
TLOC = 2048
NE, HID = 64, 256
EPS = 1e-5


class Sem:
    def __init__(self, st, nc, name):
        self.s = st.enter_context(nc.semaphore(name))
        self.n = 0

    def inc(self, instr, k=1):
        instr.then_inc(self.s, k)
        self.n += k
        return instr


def build_nc(n_oc=16, tap=None):
    nc = bass.Bass("TRN2", target_bir_lowering=False, debug=False,
                   detect_race_conditions=False)
    L = n_oc * OC

    def din(name, shape):
        return nc.dram_tensor(name, shape, F32, kind="ExternalInput").ap()

    xf = din("xf", [DM, L])
    xres = din("xres", [DM, TLOC])
    w_in = din("w_in", [DM, FOP])
    w_out = din("w_out", [CPC, DM])
    convw = din("convw", [128, 4, 4])
    convb = din("convb", [128, 4])
    alog = din("alog", [128, 4])
    dvec = din("dvec", [128, 4])
    dtbias = din("dtbias", [128, 1])
    wr = din("wr", [DM, NE])
    brv = din("brv", [NE, 1])
    wup = din("wup", [NE, DM, HID])
    wgt = din("wgt", [NE, DM, HID])
    wdn = din("wdn", [NE, HID, DM])
    bupT = din("bupT", [128, 2, NE])
    bgtT = din("bgtT", [128, 2, NE])
    bdn = din("bdn", [NE, DM])
    eyec = din("eyec", [128, 128])
    hmk = din("hmk", [8, 4, 128])
    mones = din("mones", [128, 128])
    gmaskc = din("gmaskc", [4, 128, 128])
    bmask = din("bmask", [128, OC])
    sgnp = din("sgnp", [128, 7])
    iotag = din("iotag", [128, 128])
    selt = din("selt", [128, 4, 16])
    onescol = din("onescol", [128, 1])
    constv = din("constv", [128, 4])
    onesrow = din("onesrow", [1, 128])
    seltau = din("seltau", [64, 8])

    idram = lambda name, shape: nc.dram_tensor(name, shape, F32).ap()
    rs_in = idram("rs_in", [4, DM, TLOC])
    rs_out = idram("rs_out", [DM, TLOC])
    sc_in = idram("sc_in", [NE, TLOC])
    sc_out = idram("sc_out", [NCORE * NE, TLOC])
    gs_in = idram("gs_in", [8, 256])
    gs_out = idram("gs_out", [64, 256])
    tau_in = idram("tau_in", [8, 8])
    tau_out = idram("tau_out", [64, 8])
    x2_spill = idram("x2_spill", [DM, TLOC])
    sorstg = idram("sorstg", [16, OC])

    yout = nc.dram_tensor("yout", [DM, TLOC], F32, kind="ExternalOutput").ap()
    dbg_sort = nc.dram_tensor("dbg_sort", [128, 8, 128], F32, kind="ExternalOutput").ap()
    dbg_kth = nc.dram_tensor("dbg_kth", [64, 3], F32, kind="ExternalOutput").ap()
    dbg_tau = nc.dram_tensor("dbg_tau", [1, 3, 8], F32, kind="ExternalOutput").ap()
    dbg_gs = nc.dram_tensor("dbg_gs", [2, 1024], F32, kind="ExternalOutput").ap()
    dbg_hg = nc.dram_tensor("dbg_hg", [128, 8, 2, 1024], F32, kind="ExternalOutput").ap()
    dbg_ob = nc.dram_tensor("dbg_ob", [128, 8, 512], F32, kind="ExternalOutput").ap()
    tapt = None
    if tap is not None:
        tshape = dict(proj=[128, 4, OC], xconv=[128, 4, OC + 8], dt=[128, OC],
                      yg=[128, 4, OC], mixpart=[128, 8, OC], x2=[DM, TLOC],
                      scores=[NE, TLOC], sorted=[128, 8, 128], gs=[8, 256],
                      tau=[8, 8], fs=[NE, TLOC], ybd=[128, 4, OC],
                      cl4=[128, 4, OC], bl4=[128, 4, OC], cl0=[128, 4, OC],
                      bl0=[128, 4, OC], lam=[128, 4, 4], dsum=[8, 128],
                      daux=[8, 3, OC // 8, 8], rsout=[DM, TLOC], rsin=[4, DM, TLOC],
                      ffnout=[128, 8, TLOC])[tap]
        tapt = nc.dram_tensor("tap_" + tap, tshape, F32, kind="ExternalOutput").ap()

    st = contextlib.ExitStack()
    stB = contextlib.ExitStack()
    sbufA = lambda name, shape, dt=F32: st.enter_context(nc.sbuf_tensor(name, shape, dt))
    sbuf = lambda name, shape, dt=F32: stB.enter_context(nc.sbuf_tensor(name, shape, dt))
    psum = lambda name, shape, dt=F32: st.enter_context(nc.psum_tensor(name, shape, dt))

    with st:
        cwc = sbufA("cwc", [128, 4, 4])
        cbc = sbufA("cbc", [128, 4])
        acol = sbufA("acol", [128, 4])
        dcolc = sbufA("dcolc", [128, 4])
        dtbc = sbufA("dtbc", [128, 1])
        eyeb = sbufA("eyeb", [128, 128], BF16)
        eyef = sbufA("eyef", [128, 128])
        hmkc = sbufA("hmkc", [8, 4, 128])
        monesb = sbufA("monesb", [128, 128], BF16)
        gmb = sbufA("gmb", [128, 4, 128], BF16)
        bmkc = sbufA("bmkc", [128, OC], BF16)
        onc = sbufA("onc", [128, 1])
        oncb = sbufA("oncb", [128, 1], BF16)
        onrb = sbufA("onrb", [1, 128], BF16)
        cstv = sbufA("cstv", [128, 4])
        nc.const_aps.register(F32, EPS, cstv[:, 0:1])
        nc.const_aps.register(F32, 1.0, cstv[:, 1:2])
        nc.const_aps.register(F32, -1.0, cstv[:, 2:3])
        nc.const_aps.register(F32, 0.0, cstv[:, 3:4])
        winr = sbuf("winr", [128, 8, 17, 128], BF16)
        wotr = sbuf("wotr", [128, 4, 8, 128], BF16)
        xcf = sbuf("xcf", [128, 8, OC])
        xcb = sbuf("xcb", [128, 8, OC], BF16)
        sqb = sbuf("sqb", [128, 8, OC], BF16)
        rowf = sbuf("rowf", [1, 5, OC])
        rowb = sbuf("rowb", [1, 2, OC], BF16)
        xiw = sbuf("xiw", [128, 4, OC + 8], BF16)
        zbt = sbuf("zbt", [128, 4, OC], BF16)
        btt = sbuf("btt", [128, 4, OC], BF16)
        ctt = sbuf("ctt", [128, 4, OC], BF16)
        dtsb = sbuf("dtsb", [128, OC])
        dtaux = sbuf("dtaux", [8, 3, OC // 8, 8])
        dsum = sbuf("dsum", [8, 128])
        lamt = sbuf("lamt", [128, 4, 4])
        cfac = sbuf("cfac", [128, 4, 32], BF16)
        cfacB = sbuf("cfacB", [128, 4, 32], BF16)
        xch = sbuf("xch", [128, 4, OC + 8], BF16)
        bdt = sbuf("bdt", [128, 4, OC + 8], BF16)
        dmk = sbuf("dmk", [128, 4, OC + 8], BF16)
        ebd = sbuf("ebd", [128, 4, OC], BF16)
        wbd = sbuf("wbd", [128, 4, OC], BF16)
        gbs = sbuf("gbs", [128, OC], BF16)
        gtp = sbuf("gtp", [128, OC], BF16)
        ybd = sbuf("ybd", [128, 4, OC], BF16)
        cl = [sbuf(f"cl{m}", [128, 4, OC], BF16) for m in (8, 16, 32, 64, 128)]
        blv = [sbuf(f"bl{m}", [128, 4, OC], BF16) for m in (8, 16, 32, 64, 128)]
        xts = sbuf("xts", [128, 64], BF16)
        bts = sbuf("bts", [128, 64], BF16)
        gsbA = sbuf("gsbA", [128, 128], BF16)
        gsbB = sbuf("gsbB", [128, 128], BF16)
        hst = sbuf("hst", [128, 4, 64])
        hsb = sbuf("hsb", [128, 4, 64], BF16)
        ygt = sbuf("ygt", [128, 4, OC])
        rsst = sbuf("rsst", [128, 4, OC])
        xnb = rsst[:].bitcast(BF16).rearrange("p c t -> p (c t)").rearrange("p (c t) -> p c t", c=8)

        P0 = psum("P0", [128, 512])
        P1 = psum("P1", [128, 512])
        P2 = psum("P2", [128, 512])
        P3 = psum("P3", [128, 512])
        P4 = psum("P4", [128, 512])
        P5 = psum("P5", [128, 1024], BF16)
        P6 = psum("P6", [128, 512])
        P7 = psum("P7", [128, 512])

        sIN = Sem(st, nc, "sIN")
        sW = Sem(st, nc, "sW")
        sMM = Sem(st, nc, "sMM")
        sV = Sem(st, nc, "sV")
        sSC = Sem(st, nc, "sSC")
        sG = Sem(st, nc, "sG")
        sCC = Sem(st, nc, "sCC")
        sOUT = Sem(st, nc, "sOUT")
        sRS = Sem(st, nc, "sRS")

        blk = st.enter_context(nc.Block())

        def f(s):
            for o_, i_ in ((eyef, eyec), (hmkc, hmk), (cwc, convw), (cbc, convb),
                           (dcolc, dvec), (dtbc, dtbias), (onc, onescol), (acol, alog),
                           (cstv, constv)):
                sIN.inc(s.dma_start(out=o_[:], in_=i_[(slice(None),) * len(i_.shape)]), 16)
        blk.sync(f)

        def f(g):
            sW.inc(g.dma_start(out=winr[:], in_=w_in.rearrange("(c p) (t f) -> p c t f", p=128, f=128)), 16)
            sW.inc(g.dma_start(out=wotr[:], in_=w_out.rearrange("(c p) (t f) -> p c t f", p=128, f=128)), 16)
            sW.inc(g.dma_start(out=eyeb[:], in_=eyec[:, :]), 16)
            sW.inc(g.dma_start(out=monesb[:], in_=mones[:, :]), 16)
            sW.inc(g.dma_start(out=gmb[:], in_=gmaskc.rearrange("m p f -> p m f")), 16)
            sW.inc(g.dma_start(out=bmkc[:], in_=bmask[:, :]), 16)
            sW.inc(g.dma_start(out=oncb[:], in_=onescol[:, :]), 16)
            sW.inc(g.dma_start(out=onrb[:], in_=onesrow[:, :]), 16)
        blk.gpsimd(f)

        def f(s):
            s.wait_ge(sIN.s, sIN.n)
            s.activation(acol[:], acol[:], AF.Exp)
            sSC.inc(s.mul(acol[:], acol[:], -1.0))
        blk.scalar(f)

        def f(v):
            v.memset(hst[:], 0.0)
            sV.inc(v.memset(hsb[:], 0.0))
        blk.vector(f)

        scan_done_prev = sV.n
        rs_dma_prev = 0

        for oc in range(n_oc):
            t0 = oc * OC

            def f(s, t0=t0, oc=oc):
                if oc > 0:
                    s.wait_ge(sV.s, sV.n)
                sIN.inc(s.dma_start(
                    out=xcf[:], in_=xf.rearrange("(c p) t -> p c t", p=128)[:, :, t0:t0 + OC]), 16)
            blk.sync(f)
            ld32 = sIN.n

            def f(g, t0=t0, oc=oc):
                if oc > 0:
                    g.wait_ge(sSC.s, sSC.n)
                sW.inc(g.dma_start(
                    out=xcb[:], in_=xf.rearrange("(c p) t -> p c t", p=128)[:, :, t0:t0 + OC]), 16)
            blk.gpsimd(f)
            ldbf = sW.n

            def f(v, oc=oc, sdp=scan_done_prev):
                v.wait_ge(sG.s, sG.n)
                v.wait_ge(sV.s, sdp)
                if oc == 0:
                    v.memset(xiw[:, :, 0:8], 0.0)
                    v.memset(xch[:, :, 0:8], 0.0)
                    v.memset(bdt[:, :, 0:8], 0.0)
                    sV.inc(v.memset(dmk[:, :, 0:8], 0.0))
                else:
                    v.tensor_copy(xiw[:, :, 0:8], xiw[:, :, OC:OC + 8])
                    v.tensor_copy(xch[:, :, 0:8], xch[:, :, OC:OC + 8])
                    v.tensor_copy(bdt[:, :, 0:8], bdt[:, :, OC:OC + 8])
                    sV.inc(v.tensor_copy(dmk[:, :, 0:8], dmk[:, :, OC:OC + 8]))
            blk.vector(f)
            tails_done = sV.n

            def f(s, ldbf=ldbf):
                s.wait_ge(sW.s, ldbf)
                sSC.inc(s.activation(sqb[:], xcb[:], AF.Square))
            blk.scalar(f)
            sq_done = sSC.n

            def f(t, ldbf=ldbf, sq_done=sq_done):
                t.wait_ge(sW.s, ldbf)
                t.wait_ge(sSC.s, sq_done)
                t.wait_ge(sV.s, sV.n)
                for dc in range(8):
                    t.matmul(P1[0:1, 0:OC], oncb[:], xcb[:, dc, :], start=(dc == 0), stop=(dc == 7))
                for dc in range(8):
                    mi = t.matmul(P6[0:1, 0:OC], oncb[:], sqb[:, dc, :], start=(dc == 0), stop=(dc == 7))
                sMM.inc(mi)
            blk.tensor(f)
            stat_done = sMM.n

            def f(v, stat_done=stat_done):
                v.wait_ge(sMM.s, stat_done)
                v.tensor_scalar(rowf[:, 0, :], P1[0:1, 0:OC], 1.0 / DM, 0.0, Op.mult, Op.add)
                v.tensor_scalar(rowf[:, 1, :], P6[0:1, 0:OC], 1.0 / DM, 0.0, Op.mult, Op.add)
                v.tensor_tensor(rowf[:, 2, :], rowf[:, 0, :], rowf[:, 0, :], Op.mult)
                sV.inc(v.tensor_tensor(rowf[:, 2, :], rowf[:, 1, :], rowf[:, 2, :], Op.subtract))
            blk.vector(f)
            var_done = sV.n

            def f(s, var_done=var_done):
                s.wait_ge(sV.s, var_done)
                sSC.inc(s.activation(rowf[:, 3, :], rowf[:, 2, :], AF.Sqrt, bias=EPS))
            blk.scalar(f)
            std_done = sSC.n

            def f(v, std_done=std_done):
                v.wait_ge(sSC.s, std_done)
                v.reciprocal(rowf[:, 4, :], rowf[:, 3, :])
                v.tensor_copy(rowb[:, 0, :], rowf[:, 0, :])
                sV.inc(v.tensor_copy(rowb[:, 1, :], rowf[:, 4, :]))
            blk.vector(f)
            rows_done = sV.n

            def f(t, rows_done=rows_done):
                t.wait_ge(sV.s, rows_done)
                t.matmul(P1[:, 0:OC], onrb[:], rowb[:, 0, :], start=True, stop=True)
                sMM.inc(t.matmul(P6[:, 0:OC], onrb[:], rowb[:, 1, :], start=True, stop=True))
            blk.tensor(f)
            bc_done = sMM.n

            def f(v, bc_done=bc_done, ld32=ld32, rdp=rs_dma_prev):
                v.wait_ge(sMM.s, bc_done)
                v.wait_ge(sIN.s, ld32)
                v.wait_ge(sRS.s, rdp)
                mb = P1[:, 0:OC].unsqueeze(1).broadcast_to((128, 8, OC))
                rb = P6[:, 0:OC].unsqueeze(1).broadcast_to((128, 8, OC))
                v.tensor_tensor(xcf[:], xcf[:], mb, Op.subtract)
                sV.inc(v.tensor_tensor(xnb, xcf[:], rb, Op.mult))
            blk.vector(f)
            xn_done = sV.n

            evac_marks = {}
            for ft in range(17):
                def f(t, ft=ft, xn_done=xn_done, tails_done=tails_done):
                    if ft == 0:
                        t.wait_ge(sV.s, xn_done)
                        t.wait_ge(sV.s, tails_done)
                        t.wait_ge(sSC.s, sSC.n)
                    if ft >= 2:
                        t.wait_ge(sSC.s, evac_marks[ft - 2])
                    pb = P0 if ft % 2 == 0 else P2
                    for dc in range(8):
                        mi = t.matmul(pb[:, 0:OC], winr[:, dc, ft, :], xnb[:, dc:dc+1, :].opt(),
                                      start=(dc == 0), stop=(dc == 7))
                    sMM.inc(mi)
                blk.tensor(f)
                mm_mark = sMM.n

                def f(s, ft=ft, mm_mark=mm_mark):
                    s.wait_ge(sMM.s, mm_mark)
                    pb = P0 if ft % 2 == 0 else P2
                    if ft < 4:
                        out = xiw[:, ft, 8:8 + OC]
                    elif ft < 8:
                        out = zbt[:, ft - 4, :]
                    elif ft < 12:
                        out = btt[:, ft - 8, :]
                    elif ft < 16:
                        out = ctt[:, ft - 12, :]
                    else:
                        out = dtsb[0:128, :]
                    sSC.inc(s.copy(out, pb[:, 0:OC]))
                blk.scalar(f)
                evac_marks[ft] = sSC.n
            proj_done = sSC.n

            def f(v, proj_done=proj_done):
                v.wait_ge(sSC.s, proj_done)
                for b in range(4):
                    for k in range(4):
                        xv = xiw[:, b, 5 + k:5 + k + OC]
                        wcol = cwc[:, b:b + 1, k]
                        if k == 0:
                            mi = v.tensor_scalar(ygt[:, b, :], xv, wcol, 0.0, Op.mult, Op.add)
                        else:
                            mi = v.scalar_tensor_tensor(ygt[:, b, :], xv, wcol, ygt[:, b, :],
                                                        Op.mult, Op.add)
                sV.inc(mi)
            blk.vector(f)
            conv_done = sV.n

            def f(s, conv_done=conv_done):
                s.wait_ge(sV.s, conv_done)
                for b in range(4):
                    mi = s.activation(xch[:, b, 8:8 + OC], ygt[:, b, :], AF.Silu, bias=cbc[:, b:b + 1])
                sSC.inc(mi)
            blk.scalar(f)
            xconv_done = sSC.n

            def f(s, proj_done=proj_done):
                s.wait_ge(sSC.s, proj_done)
                s.activation(dtaux[:, 2, :, :].rearrange('h b e -> h (b e)'), dtsb[0:8, :], AF.Exp, bias=dtbc[0:8, :])
                sSC.inc(s.activation(dtsb[0:8, :], dtaux[:, 2, :, :].rearrange('h b e -> h (b e)'), AF.Ln, bias=1.0))
            blk.scalar(f)
            dt_done = sSC.n

            def f(v, dt_done=dt_done):
                v.wait_ge(sSC.s, dt_done)
                dt8 = dtsb[0:8, :].rearrange("h (b e) -> h b e", e=8)
                inc = dtaux[:, 0, :, :]
                rex = dtaux[:, 1, :, :]
                tmp = dtaux[:, 2, :, :]
                v.tensor_copy(inc[:], dt8)
                v.tensor_tensor(tmp[:, :, 1:8], inc[:, :, 0:7], inc[:, :, 1:8], Op.add)
                v.tensor_copy(tmp[:, :, 0:1], inc[:, :, 0:1])
                v.tensor_tensor(inc[:, :, 2:8], tmp[:, :, 0:6], tmp[:, :, 2:8], Op.add)
                v.tensor_copy(inc[:, :, 0:2], tmp[:, :, 0:2])
                v.tensor_tensor(rex[:, :, 4:8], inc[:, :, 0:4], inc[:, :, 4:8], Op.add)
                v.tensor_copy(rex[:, :, 0:4], inc[:, :, 0:4])
                v.tensor_copy(inc[:], rex[:])
                v.tensor_tensor(rex[:], inc[:, :, 7:8].broadcast_to((8, OC // 8, 8)), inc[:], Op.subtract)
                v.tensor_copy(dsum[:, 0:64], inc[:, :, 7])
                v.tensor_reduce(dsum[:, 64:96], dtsb[0:8, :].rearrange("h (b e) -> h b e", e=16),
                                mybir.AxisListType.X, Op.add)
                v.tensor_reduce(dsum[:, 96:112], dtsb[0:8, :].rearrange("h (b e) -> h b e", e=32),
                                mybir.AxisListType.X, Op.add)
                v.tensor_reduce(dsum[:, 112:120], dtsb[0:8, :].rearrange("h (b e) -> h b e", e=64),
                                mybir.AxisListType.X, Op.add)
                sV.inc(v.tensor_reduce(dsum[:, 120:124],
                                       dtsb[0:8, :].rearrange("h (b e) -> h b e", e=128),
                                       mybir.AxisListType.X, Op.add))
            blk.vector(f)
            aux_done = sV.n

            base_marks = {}
            for b in range(4):
                def f(t, b=b, aux_done=aux_done, tails_done=tails_done):
                    if b == 0:
                        t.wait_ge(sV.s, aux_done)
                        t.wait_ge(sSC.s, sSC.n)
                    if b >= 2:
                        t.wait_ge(sSC.s, base_marks[b - 2])
                    pd, pi, pr = (P4, P0, P1) if b % 2 == 0 else (P6, P2, P7)
                    t.matmul(pd[:, 0:OC], hmkc[:, b, :], dtsb[0:8, :], start=True, stop=True)
                    t.matmul(pi[:, 0:OC], hmkc[:, b, :], dtaux[:, 0, :, :].rearrange("h b e -> h (b e)"), start=True, stop=True)
                    sMM.inc(t.matmul(pr[:, 0:OC], hmkc[:, b, :], dtaux[:, 1, :, :].rearrange("h b e -> h (b e)"),
                                     start=True, stop=True))
                blk.tensor(f)
                bmm = sMM.n

                def f(s, b=b, bmm=bmm, tails_done=tails_done):
                    s.wait_ge(sMM.s, bmm)
                    pd, pi, pr = (P4, P0, P1) if b % 2 == 0 else (P6, P2, P7)
                    s.activation(dmk[:, b, 8:8 + OC], pd[:, 0:OC], AF.Exp, scale=acol[:, b:b + 1])
                    s.activation(cl[0][:, b, :], pi[:, 0:OC], AF.Exp, scale=acol[:, b:b + 1])
                    s.activation(blv[0][:, b, :], pr[:, 0:OC], AF.Exp, scale=acol[:, b:b + 1])
                    sSC.inc(s.copy(bdt[:, b, 8:8 + OC], pd[:, 0:OC]))
                blk.scalar(f)
                base_marks[b] = sSC.n
            base_done = sSC.n

            def f(v, base_done=base_done):
                v.wait_ge(sSC.s, base_done)
                v.tensor_tensor(bdt[:, :, 8:8 + OC], bdt[:, :, 8:8 + OC], btt[:], Op.mult)
                v.tensor_tensor(dmk[:, :, 8:8 + OC], dmk[:, :, 8:8 + OC],
                                bmkc[:].unsqueeze(1).broadcast_to((128, 4, OC)), Op.mult)
                v.tensor_tensor(cl[0][:], cl[0][:], ctt[:], Op.mult)
                sV.inc(v.tensor_tensor(blv[0][:], blv[0][:], bdt[:, :, 8:8 + OC], Op.mult))
            blk.vector(f)
            lvl_prev = sV.n

            lvl_cfg = [(0, 64, 8), (64, 32, 16), (96, 16, 32), (112, 8, 64)]
            for li, (off, nb, half) in enumerate(lvl_cfg):
                nb2 = nb // 2

                def f(t, off=off, nb=nb, nb2=nb2, lvl_prev=lvl_prev):
                    t.wait_ge(sV.s, lvl_prev)
                    t.wait_ge(sSC.s, sSC.n)
                    dsv = dsum[:, off:off + nb].rearrange("h (b two) -> h b two", two=2)
                    for b in range(4):
                        t.matmul(P4[:, b * 32:b * 32 + nb2], hmkc[:, b, :], dsv[:, :, 0],
                                 start=(b == 0), stop=(b == 3))
                        mi = t.matmul(P6[:, b * 32:b * 32 + nb2], hmkc[:, b, :], dsv[:, :, 1],
                                      start=(b == 0), stop=(b == 3))
                    sMM.inc(mi)
                blk.tensor(f)
                lmm = sMM.n

                def f(s, nb2=nb2, lmm=lmm):
                    s.wait_ge(sMM.s, lmm)
                    for b in range(4):
                        s.activation(cfac[:, b, 0:nb2], P4[:, b * 32:b * 32 + nb2],
                                     AF.Exp, scale=acol[:, b:b + 1])
                        mi = s.activation(cfacB[:, b, 0:nb2], P6[:, b * 32:b * 32 + nb2],
                                          AF.Exp, scale=acol[:, b:b + 1])
                    sSC.inc(mi)
                blk.scalar(f)
                lact = sSC.n

                def f(v, li=li, nb2=nb2, half=half, lact=lact):
                    v.wait_ge(sSC.s, lact)
                    v.tensor_copy(cl[li + 1][:], cl[li][:])
                    v.tensor_copy(blv[li + 1][:], blv[li][:])
                    cv = cl[li + 1][:].rearrange("p b (n w) -> p b n w", n=nb2)
                    bv = blv[li + 1][:].rearrange("p b (n w) -> p b n w", n=nb2)
                    for b in range(4):
                        cf = cfac[:, b, 0:nb2].unsqueeze(2)
                        bf = cfacB[:, b, 0:nb2].unsqueeze(2)
                        v.tensor_tensor(cv[:, b, :, half:], cv[:, b, :, half:],
                                        cf.broadcast_to((128, nb2, half)), Op.mult)
                        mi = v.tensor_tensor(bv[:, b, :, 0:half], bv[:, b, :, 0:half],
                                             bf.broadcast_to((128, nb2, half)), Op.mult)
                    sV.inc(mi)
                blk.vector(f)
                lvl_prev = sV.n
            chains_done = lvl_prev

            def f(t, chains_done=chains_done):
                t.wait_ge(sV.s, chains_done)
                t.wait_ge(sSC.s, sSC.n)
                for b in range(4):
                    mi = t.matmul(P4[:, b * 4:b * 4 + 4], hmkc[:, b, :], dsum[:, 120:124],
                                  start=(b == 0), stop=(b == 3))
                sMM.inc(mi)
            blk.tensor(f)
            lamm = sMM.n

            def f(s, lamm=lamm):
                s.wait_ge(sMM.s, lamm)
                for b in range(4):
                    mi = s.activation(lamt[:, b, :], P4[:, b * 4:b * 4 + 4], AF.Exp,
                                      scale=acol[:, b:b + 1])
                sSC.inc(mi)
            blk.scalar(f)
            lam_done = sSC.n

            band_mm_prev = sMM.n
            gb_mark = sSC.n
            gp_read_mark = sG.n
            for k in range(8):
                def f(v, k=k, chains_done=chains_done, xconv_done=xconv_done,
                      band_mm_prev=band_mm_prev):
                    if k == 0:
                        v.wait_ge(sV.s, chains_done)
                        v.wait_ge(sSC.s, xconv_done)
                    else:
                        v.wait_ge(sMM.s, band_mm_prev)
                    if k == 0:
                        mi = v.tensor_tensor(wbd[:], ctt[:], bdt[:, :, 8:8 + OC], Op.mult)
                    elif k == 1:
                        v.tensor_copy(ebd[:], dmk[:, :, 8:8 + OC])
                        v.tensor_tensor(wbd[:], ctt[:], ebd[:], Op.mult)
                        mi = v.tensor_tensor(wbd[:], wbd[:], bdt[:, :, 7:7 + OC], Op.mult)
                    else:
                        v.tensor_tensor(ebd[:], ebd[:], dmk[:, :, 9 - k:9 - k + OC], Op.mult)
                        v.tensor_tensor(wbd[:], ctt[:], ebd[:], Op.mult)
                        mi = v.tensor_tensor(wbd[:], wbd[:], bdt[:, :, 8 - k:8 - k + OC], Op.mult)
                    sV.inc(mi)
                blk.vector(f)
                wmark = sV.n

                for b in range(4):
                    def f(t, k=k, b=b, wmark=wmark, gb_mark=gb_mark):
                        t.wait_ge(sV.s, wmark)
                        if not (k == 0 and b == 0):
                            t.wait_ge(sSC.s, gb_mark)
                        sMM.inc(t.matmul(P7[:, 0:OC], monesb[:], wbd[:, b, :], start=True, stop=True))
                    blk.tensor(f)
                    gmmk = sMM.n

                    def f(s, gmmk=gmmk, gp_read_mark=gp_read_mark):
                        s.wait_ge(sMM.s, gmmk)
                        s.wait_ge(sG.s, gp_read_mark)
                        sSC.inc(s.copy(gbs[:], P7[:, 0:OC]))
                    blk.scalar(f)
                    gb_mark = sSC.n

                    def f(g, k=k, b=b, gb_mark=gb_mark, sdp=scan_done_prev):
                        g.wait_ge(sSC.s, gb_mark)
                        if k == 0 and b == 0:
                            g.wait_ge(sV.s, sdp)
                        sG.inc(g.tensor_tensor(gtp[:], gbs[:], xch[:, b, 8 - k:8 - k + OC], Op.mult))
                        if k == 0:
                            sG.inc(g.tensor_copy(ybd[:, b, :], gtp[:]))
                        else:
                            sG.inc(g.tensor_tensor(ybd[:, b, :], ybd[:, b, :], gtp[:], Op.add))
                    blk.gpsimd(f)
                    gp_read_mark = sG.n - 1
                band_mm_prev = sMM.n
            bands_done = sG.n

            for qc in range(OC // QC):
                q0 = qc * QC
                for h in range(HPC):
                    b = h // 2
                    r0 = 64 * (h % 2)

                    def f(t, b=b, r0=r0, q0=q0, qc=qc, h=h, chains_done=chains_done,
                          lam_done=lam_done, xconv_done=xconv_done):
                        if qc == 0 and h == 0:
                            t.wait_ge(sV.s, chains_done)
                            t.wait_ge(sSC.s, lam_done)
                            t.wait_ge(sSC.s, xconv_done)
                        t.wait_ge(sV.s, sV.n)
                        t.transpose(P5[:, 0:64], xch[r0:r0 + 64, b, 8 + q0:8 + q0 + QC],
                                    eyeb[r0:r0 + 64, r0:r0 + 64])
                        sMM.inc(t.transpose(P5[:, 64:128], blv[4][r0:r0 + 64, b, q0:q0 + QC],
                                            eyeb[r0:r0 + 64, r0:r0 + 64]))
                    blk.tensor(f)
                    tr_mark = sMM.n

                    def f(v, tr_mark=tr_mark):
                        v.wait_ge(sMM.s, tr_mark)
                        v.tensor_copy(xts[:], P5[:, 0:64])
                        sV.inc(v.tensor_copy(bts[:], P5[:, 64:128]))
                    blk.vector(f)
                    trc_mark = sV.n

                    def f(t, b=b, r0=r0, q0=q0, trc_mark=trc_mark):
                        t.wait_ge(sV.s, trc_mark)
                        t.matmul(P4[r0:r0 + 64, 0:QC], hsb[r0:r0 + 64, b, :],
                                 cl[4][r0:r0 + 64, b, q0:q0 + QC], start=True, stop=False)
                        t.matmul(P0[:, 0:QC], blv[0][r0:r0 + 64, b, q0:q0 + QC],
                                 cl[0][r0:r0 + 64, b, q0:q0 + QC], start=True, stop=True)
                        sMM.inc(t.matmul(P2[:, 0:QC], blv[1][r0:r0 + 64, b, q0:q0 + QC],
                                         cl[1][r0:r0 + 64, b, q0:q0 + QC], start=True, stop=True))
                    blk.tensor(f)
                    g01 = sMM.n

                    def f(v, g01=g01):
                        v.wait_ge(sMM.s, g01)
                        v.tensor_tensor(gsbA[:], P0[:, 0:QC], gmb[:, 0, :], Op.mult)
                        sV.inc(v.tensor_tensor(gsbB[:], P2[:, 0:QC], gmb[:, 1, :], Op.mult))
                    blk.vector(f)
                    ev01 = sV.n

                    def f(t, b=b, r0=r0, q0=q0, ev01=ev01):
                        t.wait_ge(sV.s, ev01)
                        t.matmul(P4[r0:r0 + 64, 0:QC], xts[:], gsbA[:], start=False, stop=False)
                        t.matmul(P4[r0:r0 + 64, 0:QC], xts[:], gsbB[:], start=False, stop=False)
                        t.matmul(P0[:, 0:QC], blv[2][r0:r0 + 64, b, q0:q0 + QC],
                                 cl[2][r0:r0 + 64, b, q0:q0 + QC], start=True, stop=True)
                        sMM.inc(t.matmul(P2[:, 0:QC], blv[3][r0:r0 + 64, b, q0:q0 + QC],
                                         cl[3][r0:r0 + 64, b, q0:q0 + QC], start=True, stop=True))
                    blk.tensor(f)
                    g23 = sMM.n

                    def f(v, g23=g23):
                        v.wait_ge(sMM.s, g23)
                        v.tensor_tensor(gsbA[:], P0[:, 0:QC], gmb[:, 2, :], Op.mult)
                        sV.inc(v.tensor_tensor(gsbB[:], P2[:, 0:QC], gmb[:, 3, :], Op.mult))
                    blk.vector(f)
                    ev23 = sV.n

                    def f(t, b=b, r0=r0, ev23=ev23):
                        t.wait_ge(sV.s, ev23)
                        t.matmul(P4[r0:r0 + 64, 0:QC], xts[:], gsbA[:], start=False, stop=False)
                        t.matmul(P4[r0:r0 + 64, 0:QC], xts[:], gsbB[:], start=False, stop=True)
                        sMM.inc(t.matmul(P6[r0:r0 + 64, 0:64], bts[:], xts[:], start=True, stop=True))
                    blk.tensor(f)
                    smm = sMM.n

                    def f(v, b=b, r0=r0, q0=q0, qc=qc, smm=smm, bands_done=bands_done):
                        v.wait_ge(sMM.s, smm)
                        v.wait_ge(sG.s, bands_done)
                        v.scalar_tensor_tensor(ygt[r0:r0 + 64, b, q0:q0 + QC],
                                               xch[r0:r0 + 64, b, 8 + q0:8 + q0 + QC],
                                               dcolc[r0:r0 + 64, b:b + 1], P4[r0:r0 + 64, 0:QC],
                                               Op.mult, Op.add)
                        v.tensor_tensor(ygt[r0:r0 + 64, b, q0:q0 + QC],
                                        ygt[r0:r0 + 64, b, q0:q0 + QC],
                                        ybd[r0:r0 + 64, b, q0:q0 + QC], Op.add)
                        v.scalar_tensor_tensor(hst[r0:r0 + 64, b, :], hst[r0:r0 + 64, b, :],
                                               lamt[r0:r0 + 64, b, qc:qc + 1], P6[r0:r0 + 64, 0:64],
                                               Op.mult, Op.add)
                        sV.inc(v.tensor_copy(hsb[r0:r0 + 64, b, :], hst[r0:r0 + 64, b, :]))
                    blk.vector(f)
            scan_done = sV.n
            scan_done_prev = scan_done

            def f(s):
                s.wait_ge(sSC.s, sSC.n)
                sSC.inc(s.activation(zbt[:], zbt[:], AF.Silu))
            blk.scalar(f)
            zs_done = sSC.n

            def f(v, zs_done=zs_done, scan_done=scan_done):
                v.wait_ge(sSC.s, zs_done)
                v.wait_ge(sV.s, scan_done)
                sV.inc(v.tensor_tensor(sqb[:, 0:4, :], ygt[:], zbt[:], Op.mult))
            blk.vector(f)
            yg_done = sV.n

            wo_marks = {}
            half_marks = {}
            for dt_ in range(8):
                def f(t, dt_=dt_, yg_done=yg_done, rdp=rs_dma_prev):
                    if dt_ == 0:
                        t.wait_ge(sV.s, yg_done)
                        t.wait_ge(sSC.s, sSC.n)
                        t.wait_ge(sRS.s, rdp)
                    if dt_ >= 2:
                        t.wait_ge(sSC.s, wo_marks[dt_ - 2])
                    pb = P0 if dt_ % 2 == 0 else P2
                    for cb in range(4):
                        mi = t.matmul(pb[:, 0:OC], wotr[:, cb, dt_, :], sqb[:, cb, :],
                                      start=(cb == 0), stop=(cb == 3))
                    sMM.inc(mi)
                blk.tensor(f)
                womm = sMM.n

                def f(s, dt_=dt_, womm=womm, oc=oc):
                    s.wait_ge(sMM.s, womm)
                    if dt_ >= 4:
                        s.wait_ge(sRS.s, half_marks[0])
                    pb = P0 if dt_ % 2 == 0 else P2
                    sSC.inc(s.copy(rsst[:, dt_ % 4, :], pb[:, 0:OC]))
                blk.scalar(f)
                wo_marks[dt_] = sSC.n
                if dt_ in (3, 7):
                    def f(s, oc=oc, dt_=dt_, wo_done=sSC.n):
                        s.wait_ge(sSC.s, wo_done)
                        shard, col = oc // 4, (oc % 4) * OC
                        d0 = 0 if dt_ == 3 else 4
                        sRS.inc(s.dma_start(
                            out=rs_in.rearrange("s (c p) t -> s p c t", p=128)[shard, :, d0:d0 + 4, col:col + OC],
                            in_=rsst[:]), 16)
                    blk.sync(f)
                    half_marks[0 if dt_ == 3 else 1] = sRS.n
            rs_dma_prev = sRS.n

            if tap in ("proj", "xconv", "dt", "yg", "mixpart", "ybd", "cl4", "bl4",
                       "cl0", "bl0", "lam", "dsum", "daux") and oc == 0:
                def f(g, tap=tap):
                    g.wait_ge(sSC.s, sSC.n)
                    g.wait_ge(sV.s, sV.n)
                    g.wait_ge(sMM.s, sMM.n)
                    srcv = dict(proj=btt[:], xconv=xch[:], dt=dtsb[:], yg=ygt[:],
                                mixpart=rsst[:], ybd=ybd[:], cl4=cl[4][:],
                                bl4=blv[4][:], cl0=cl[0][:], bl0=blv[0][:],
                                lam=lamt[:], dsum=dsum[:], daux=dtaux[:])[tap]
                    sW.inc(g.dma_start(out=tapt, in_=srcv), 16)
                blk.gpsimd(f)

        def f(g):
            g.wait_ge(sRS.s, sRS.n)
            sCC.inc(g.collective_compute(
                "ReduceScatter", Op.add,
                replica_groups=[[0, 1, 2, 3], [4, 5, 6, 7]],
                ins=[rs_in.opt()], outs=[rs_out.opt()]))
        blk.gpsimd(f)
        rs_done = sCC.n

        if tap in ("rsout", "rsin"):
            def f(g, tap=tap):
                g.wait_ge(sCC.s, sCC.n)
                srcv = rs_out[:, :] if tap == "rsout" else rs_in[:, :, :]
                sW.inc(g.dma_start(out=tapt, in_=srcv), 16)
            blk.gpsimd(f)

        for pn in range(4):
            c0 = pn * OC

            def f(s, pn=pn, c0=c0, rs_done=rs_done, outp=sOUT.n):
                if pn == 0:
                    s.wait_ge(sCC.s, rs_done)
                else:
                    s.wait_ge(sV.s, sV.n)
                    s.wait_ge(sOUT.s, outp)
                sIN.inc(s.dma_start(out=xcf[:], in_=rs_out.rearrange("(c p) t -> p c t", p=128)[:, :, c0:c0 + OC]), 16)
                sIN.inc(s.dma_start(out=ygt[:],
                                    in_=xres.rearrange("(c p) t -> p c t", p=128)[:, 0:4, c0:c0 + OC]), 16)
            blk.sync(f)
            ld2 = sIN.n

            def f(v, ld2=ld2):
                v.wait_ge(sIN.s, ld2)
                sV.inc(v.tensor_copy(xcb[:], xcf[:]))
            blk.vector(f)
            cb2 = sV.n

            def f(s, cb2=cb2):
                s.wait_ge(sV.s, cb2)
                sSC.inc(s.activation(sqb[:], xcb[:], AF.Square))
            blk.scalar(f)
            sq2 = sSC.n

            def f(t, cb2=cb2, sq2=sq2):
                t.wait_ge(sV.s, cb2)
                t.wait_ge(sSC.s, sq2)
                for dc in range(8):
                    t.matmul(P1[0:1, 0:OC], oncb[:], xcb[:, dc, :], start=(dc == 0), stop=(dc == 7))
                for dc in range(8):
                    mi = t.matmul(P6[0:1, 0:OC], oncb[:], sqb[:, dc, :], start=(dc == 0), stop=(dc == 7))
                sMM.inc(mi)
            blk.tensor(f)
            st2 = sMM.n

            def f(v, st2=st2):
                v.wait_ge(sMM.s, st2)
                v.tensor_scalar(rowf[:, 0, :], P1[0:1, 0:OC], 1.0 / DM, 0.0, Op.mult, Op.add)
                v.tensor_scalar(rowf[:, 1, :], P6[0:1, 0:OC], 1.0 / DM, 0.0, Op.mult, Op.add)
                v.tensor_tensor(rowf[:, 2, :], rowf[:, 0, :], rowf[:, 0, :], Op.mult)
                sV.inc(v.tensor_tensor(rowf[:, 2, :], rowf[:, 1, :], rowf[:, 2, :], Op.subtract))
            blk.vector(f)
            var2 = sV.n

            def f(s, var2=var2):
                s.wait_ge(sV.s, var2)
                sSC.inc(s.activation(rowf[:, 3, :], rowf[:, 2, :], AF.Sqrt, bias=EPS))
            blk.scalar(f)
            std2 = sSC.n

            def f(v, std2=std2):
                v.wait_ge(sSC.s, std2)
                v.reciprocal(rowf[:, 4, :], rowf[:, 3, :])
                v.tensor_copy(rowb[:, 0, :], rowf[:, 0, :])
                sV.inc(v.tensor_copy(rowb[:, 1, :], rowf[:, 4, :]))
            blk.vector(f)
            rw2 = sV.n

            def f(t, rw2=rw2):
                t.wait_ge(sV.s, rw2)
                t.matmul(P1[:, 0:OC], onrb[:], rowb[:, 0, :], start=True, stop=True)
                sMM.inc(t.matmul(P6[:, 0:OC], onrb[:], rowb[:, 1, :], start=True, stop=True))
            blk.tensor(f)
            bc2 = sMM.n

            def f(v, bc2=bc2):
                v.wait_ge(sMM.s, bc2)
                mb = P1[:, 0:OC].unsqueeze(1).broadcast_to((128, 8, OC))
                rb = P6[:, 0:OC].unsqueeze(1).broadcast_to((128, 8, OC))
                v.tensor_tensor(xcf[:], xcf[:], mb, Op.subtract)
                v.tensor_tensor(xcf[:], xcf[:], rb, Op.mult)
                sV.inc(v.tensor_tensor(xcf[:, 0:4, :], xcf[:, 0:4, :], ygt[:], Op.add))
            blk.vector(f)
            h1 = sV.n

            def f(s, c0=c0, h1=h1):
                s.wait_ge(sV.s, h1)
                sIN.inc(s.dma_start(out=ygt[:],
                                    in_=xres.rearrange("(c p) t -> p c t", p=128)[:, 4:8, c0:c0 + OC]), 16)
            blk.sync(f)
            ld3 = sIN.n

            def f(v, ld3=ld3):
                v.wait_ge(sIN.s, ld3)
                sV.inc(v.tensor_tensor(xcf[:, 4:8, :], xcf[:, 4:8, :], ygt[:], Op.add))
            blk.vector(f)
            x2done = sV.n

            def f(s, c0=c0, x2done=x2done):
                s.wait_ge(sV.s, x2done)
                sOUT.inc(s.dma_start(
                    out=x2_spill.rearrange("(c p) t -> p c t", p=128)[:, :, c0:c0 + OC],
                    in_=xcf[:]), 16)
            blk.scalar(f)

        stB.close()

        stC = contextlib.ExitStack()
        sbufC = lambda name, shape, dt=F32: stC.enter_context(nc.sbuf_tensor(name, shape, dt))
        x2b = sbufC("x2b", [128, 8, TLOC], BF16)
        outb = sbufC("outb", [128, 8, TLOC], BF16)
        fsem = sbufC("fsem", [NE, TLOC])
        fseb = sbufC("fseb", [NE, TLOC], BF16)
        stmt = sbufC("stmt", [128, 16, NE])
        tausb = sbufC("tausb", [1, NE])
        onrf = sbufC("onrf", [1, 128])

        stD = contextlib.ExitStack()
        sbufD = lambda name, shape, dt=F32: stD.enter_context(nc.sbuf_tensor(name, shape, dt))
        x2p = sbufD("x2p", [128, 8, OC])
        sqp = sbufD("sqp", [128, 8, OC], BF16)
        x2pb = sbufD("x2pb", [128, 8, OC], BF16)
        rwf = sbufD("rwf", [1, 5, OC])
        rwb = sbufD("rwb", [1, 2, OC], BF16)
        wrb = sbufD("wrb", [128, 8, NE])
        brc = sbufD("brc", [NE, 1])
        lgt = sbufD("lgt", [NE, TLOC])
        ltm = sbufD("ltm", [128, 16, NE])
        lrow = sbufD("lrow", [128, 16, 2])
        semj = sbufD("semj", [NE, TLOC])
        gath = sbufD("gath", [128, 4, TLOC])
        selb = sbufD("selb", [128, 4, 16])
        stg = sbufD("stg", [16, OC])
        sorc = sbufD("sorc", [128, 8, 128])
        sortmp = sbufD("sortmp", [128, 512])
        sgnc = sbufD("sgnc", [128, 7])
        gssb = sbufD("gssb", [2, 1024])
        hmsk = sbufD("hmsk", [128, 2])
        gso = sbufD("gso", [64, 256])
        sor2 = sbufD("sor2", [128, 128])
        sor2t = sbufD("sor2t", [128, 64])
        iotc = sbufD("iotc", [128, 128])
        seltc = sbufD("seltc", [64, 8])
        kth = sbufD("kth", [64, 3])
        eqt = sbufD("eqt", [128, 128])
        rb128 = sbufD("rb128", [128, 8])
        tfix = sbufD("tfix", [1, 3, 8])

        for pn in range(4):
            c0 = pn * OC

            def f(s, pn=pn, c0=c0):
                if pn == 0:
                    s.wait_ge(sOUT.s, sOUT.n)
                    sIN.inc(s.dma_start(out=sgnc[:], in_=sgnp[:, :]), 16)
                    sIN.inc(s.dma_start(out=iotc[:], in_=iotag[:, :]), 16)
                    sIN.inc(s.dma_start(out=selb[:], in_=selt[:, :, :]), 16)
                    sIN.inc(s.dma_start(out=brc[:], in_=brv[:, :]), 16)
                    sIN.inc(s.dma_start(out=onrf[:], in_=onesrow[:, :]), 16)
                else:
                    s.wait_ge(sV.s, sV.n)
                sIN.inc(s.dma_start(out=x2p[:], in_=x2_spill.rearrange("(c p) t -> p c t", p=128)[:, :, c0:c0 + OC]), 16)
            blk.sync(f)
            ld = sIN.n

            def f(g, pn=pn, outp=sOUT.n):
                if pn == 0:
                    g.wait_ge(sOUT.s, outp)
                    sW.inc(g.dma_start(out=wrb[:], in_=wr.rearrange("(c p) e -> p c e", p=128)), 16)
            blk.gpsimd(f)
            wrld = sW.n

            def f(v, ld=ld):
                v.wait_ge(sIN.s, ld)
                sV.inc(v.tensor_copy(x2pb[:], x2p[:]))
            blk.vector(f)
            cb = sV.n

            def f(s, cb=cb):
                s.wait_ge(sV.s, cb)
                sSC.inc(s.activation(sqp[:], x2pb[:], AF.Square))
            blk.scalar(f)
            sq = sSC.n

            def f(t, cb=cb, sq=sq, pn=pn):
                t.wait_ge(sV.s, cb)
                t.wait_ge(sSC.s, sq)
                if pn > 0:
                    t.wait_ge(sV.s, sV.n)
                for dc in range(8):
                    t.matmul(P1[0:1, 0:OC], oncb[:], x2pb[:, dc, :], start=(dc == 0), stop=(dc == 7))
                for dc in range(8):
                    mi = t.matmul(P6[0:1, 0:OC], oncb[:], sqp[:, dc, :], start=(dc == 0), stop=(dc == 7))
                sMM.inc(mi)
            blk.tensor(f)
            stt = sMM.n

            def f(v, stt=stt):
                v.wait_ge(sMM.s, stt)
                v.tensor_scalar(rwf[:, 0, :], P1[0:1, 0:OC], 1.0 / DM, 0.0, Op.mult, Op.add)
                v.tensor_scalar(rwf[:, 1, :], P6[0:1, 0:OC], 1.0 / DM, 0.0, Op.mult, Op.add)
                v.tensor_tensor(rwf[:, 2, :], rwf[:, 0, :], rwf[:, 0, :], Op.mult)
                sV.inc(v.tensor_tensor(rwf[:, 2, :], rwf[:, 1, :], rwf[:, 2, :], Op.subtract))
            blk.vector(f)
            vr = sV.n

            def f(s, vr=vr):
                s.wait_ge(sV.s, vr)
                sSC.inc(s.activation(rwf[:, 3, :], rwf[:, 2, :], AF.Sqrt, bias=EPS))
            blk.scalar(f)
            sd = sSC.n

            def f(v, sd=sd):
                v.wait_ge(sSC.s, sd)
                v.reciprocal(rwf[:, 4, :], rwf[:, 3, :])
                v.tensor_copy(rwb[:, 0, :], rwf[:, 0, :])
                sV.inc(v.tensor_copy(rwb[:, 1, :], rwf[:, 4, :]))
            blk.vector(f)
            rw = sV.n

            def f(t, rw=rw):
                t.wait_ge(sV.s, rw)
                t.matmul(P1[:, 0:OC], onrb[:], rwb[:, 0, :], start=True, stop=True)
                sMM.inc(t.matmul(P6[:, 0:OC], onrb[:], rwb[:, 1, :], start=True, stop=True))
            blk.tensor(f)
            bc = sMM.n

            def f(v, bc=bc, pn=pn, c0=c0):
                v.wait_ge(sMM.s, bc)
                mb = P1[:, 0:OC].unsqueeze(1).broadcast_to((128, 8, OC))
                rb = P6[:, 0:OC].unsqueeze(1).broadcast_to((128, 8, OC))
                v.tensor_tensor(x2p[:], x2p[:], mb, Op.subtract)
                v.tensor_tensor(x2p[:], x2p[:], rb, Op.mult)
                sV.inc(v.tensor_copy(x2b[:, :, c0:c0 + OC], x2p[:]))
            blk.vector(f)
            xb_done = sV.n

            def f(t, c0=c0, xb_done=xb_done, wrld=wrld):
                t.wait_ge(sV.s, xb_done)
                t.wait_ge(sW.s, wrld)
                t.wait_ge(sSC.s, sSC.n)
                t.wait_ge(sV.s, sV.n)
                for dc in range(8):
                    mi = t.matmul(P0[0:64, 0:OC], wrb[:, dc, :], x2p[:, dc, :],
                                  start=(dc == 0), stop=(dc == 7))
                sMM.inc(mi)
            blk.tensor(f)
            lmm = sMM.n

            def f(v, c0=c0, lmm=lmm):
                v.wait_ge(sMM.s, lmm)
                sV.inc(v.tensor_scalar(lgt[0:64, c0:c0 + OC], P0[0:64, 0:OC], brc[:, 0:1], 0.0,
                                       Op.add, Op.add))
            blk.vector(f)
        logits_done = sV.n

        for tc in range(16):
            def f(t, tc=tc, logits_done=logits_done):
                if tc == 0:
                    t.wait_ge(sV.s, logits_done)
                else:
                    t.wait_ge(sV.s, sV.n)
                sMM.inc(t.transpose(P1[:, 0:64], lgt[0:64, tc * 128:tc * 128 + 128],
                                    eyef[0:64, 0:64]))
            blk.tensor(f)
            trm = sMM.n

            def f(v, tc=tc, trm=trm):
                v.wait_ge(sMM.s, trm)
                v.tensor_copy(ltm[:, tc, :], P1[:, 0:64])
                sV.inc(v.tensor_reduce(lrow[:, tc, 0:1], ltm[:, tc, :], mybir.AxisListType.X, Op.max))
            blk.vector(f)

            def f(v, tc=tc, m=sV.n):
                v.wait_ge(sV.s, m)
                sV.inc(v.tensor_scalar(lrow[:, tc, 1:2], lrow[:, tc, 0:1], -1.0, 0.0, Op.mult, Op.add))
            blk.vector(f)
            mxd = sV.n

            def f(s, tc=tc, mxd=mxd):
                s.wait_ge(sV.s, mxd)
                sSC.inc(s.activation(ltm[:, tc, :], ltm[:, tc, :], AF.Exp, bias=lrow[:, tc, 1:2]))
            blk.scalar(f)
            exd = sSC.n

            def f(v, tc=tc, exd=exd):
                v.wait_ge(sSC.s, exd)
                sV.inc(v.tensor_reduce(lrow[:, tc, 0:1], ltm[:, tc, :], mybir.AxisListType.X, Op.add))
            blk.vector(f)

            def f(v, tc=tc, m=sV.n):
                v.wait_ge(sV.s, m)
                sV.inc(v.reciprocal(lrow[:, tc, 1:2], lrow[:, tc, 0:1]))
            blk.vector(f)

            def f(v, tc=tc, m=sV.n):
                v.wait_ge(sV.s, m)
                sV.inc(v.tensor_tensor(stmt[:, tc, :], ltm[:, tc, :],
                                       lrow[:, tc, 1:2].broadcast_to((128, NE)), Op.mult))
            blk.vector(f)
        smax_done = sV.n

        em_marks = {}
        for tc in range(16):
            def f(t, tc=tc, smax_done=smax_done):
                if tc == 0:
                    t.wait_ge(sV.s, smax_done)
                    t.wait_ge(sSC.s, sSC.n)
                if tc >= 2:
                    t.wait_ge(sSC.s, em_marks[tc - 2])
                pb = P0 if tc % 2 == 0 else P2
                sMM.inc(t.transpose(pb[0:64, 0:128], stmt[:, tc, :], eyef[:, :]))
            blk.tensor(f)
            trm = sMM.n

            def f(s, tc=tc, trm=trm):
                s.wait_ge(sMM.s, trm)
                pb = P0 if tc % 2 == 0 else P2
                sSC.inc(s.copy(semj[:, tc * 128:tc * 128 + 128], pb[0:64, 0:128]))
            blk.scalar(f)
            em_marks[tc] = sSC.n
        sem_done = sSC.n

        def f(s, sem_done=sem_done):
            s.wait_ge(sSC.s, sem_done)
            sRS.inc(s.dma_start(out=sc_in[:, :], in_=semj[:]), 16)
        blk.sync(f)
        scw = sRS.n

        def f(g, scw=scw):
            g.wait_ge(sRS.s, scw)
            sCC.inc(g.collective_compute(
                "AllGather", Op.bypass,
                replica_groups=[list(range(NCORE))],
                ins=[sc_in.opt()], outs=[sc_out.opt()]))
        blk.gpsimd(f)
        ag1 = sCC.n

        def f(s, ag1=ag1):
            s.wait_ge(sCC.s, ag1)
            sIN.inc(s.dma_start(out=gath[:], in_=sc_out.rearrange("(c p) t -> p c t", p=128)), 16)
        blk.sync(f)
        gld = sIN.n

        sel_marks = {}
        for i in range(4):
            for tc4 in range(4):
                def f(t, i=i, tc4=tc4, gld=gld):
                    if i == 0 and tc4 == 0:
                        t.wait_ge(sIN.s, gld)
                        t.wait_ge(sSC.s, sSC.n)
                    if (i * 4 + tc4) >= 2:
                        t.wait_ge(sSC.s, sel_marks[(i * 4 + tc4) - 2])
                    pb = P0 if (i * 4 + tc4) % 2 == 0 else P2
                    sMM.inc(t.matmul(pb[0:16, 0:OC], selb[:, i, :],
                                     gath[:, i, tc4 * OC:tc4 * OC + OC], start=True, stop=True))
                blk.tensor(f)
                smm = sMM.n

                def f(s, i=i, tc4=tc4, smm=smm):
                    s.wait_ge(sMM.s, smm)
                    s.wait_ge(sRS.s, sRS.n)
                    pb = P0 if (i * 4 + tc4) % 2 == 0 else P2
                    sSC.inc(s.copy(stg[:], pb[0:16, 0:OC]))
                blk.scalar(f)
                sel_marks[i * 4 + tc4] = sSC.n

                def f(s, i=i, tc4=tc4, mark=sSC.n):
                    s.wait_ge(sSC.s, mark)
                    sRS.inc(s.dma_start(out=sorstg[:, :], in_=stg[:]), 16)
                blk.sync(f)
                stgw = sRS.n

                def f(s, i=i, tc4=tc4, stgw=stgw):
                    s.wait_ge(sRS.s, stgw)
                    for jj in range(2):
                        j = 2 * i + jj
                        sRS.inc(s.dma_start(
                            out=sorc[j * 16 + 4 * tc4: j * 16 + 4 * tc4 + 4, :, :],
                            in_=sorstg.rearrange("(jj s) (p f) -> jj p s f", jj=2, f=128)[jj]), 16)
                blk.sync(f)
        sort_ready = sRS.n

        def negate_partitions(v, bbit):
            v.tensor_scalar(sorc[:], sorc[:], sgnc[:, bbit:bbit + 1], 0.0, Op.mult, Op.add)

        def negate_free(v, period_bit):
            half = 1 << period_bit
            vw = sorc[:].rearrange("p c (n w) -> p (c n) w", w=2 * half)
            v.tensor_scalar(vw[:, :, half:], vw[:, :, half:], -1.0, 0.0, Op.mult, Op.add)

        def cmpex(v, dfree):
            vw = sorc[:].rearrange("p c (n two d) -> p (c n) two d", two=2, d=dfree)
            ncn = vw.shape[1]
            tm = sortmp[:].rearrange("p (cn d) -> p cn d", d=dfree)[:, 0:ncn, :]
            v.tensor_tensor(tm, vw[:, :, 0, :], vw[:, :, 1, :], Op.min)
            v.tensor_tensor(vw[:, :, 1, :], vw[:, :, 0, :], vw[:, :, 1, :], Op.max)
            v.tensor_copy(vw[:, :, 0, :], tm)

        def pb_sel(c):
            return P0 if c % 2 == 0 else P2

        def emit_transpose_all(dep_v):
            for c in range(8):
                def f(t, c=c, dep_v=dep_v):
                    if c == 0:
                        t.wait_ge(sV.s, dep_v)
                        t.wait_ge(sSC.s, sSC.n)
                    else:
                        t.wait_ge(sV.s, sV.n)
                    sMM.inc(t.transpose(pb_sel(c)[:, 0:128], sorc[:, c, :], eyef[:, :]))
                blk.tensor(f)
                trm = sMM.n

                def f(v, c=c, trm=trm):
                    v.wait_ge(sMM.s, trm)
                    sV.inc(v.tensor_copy(sorc[:, c, :], pb_sel(c)[:, 0:128]))
                blk.vector(f)

        first_sort_op = [True]

        def emit_sweeps(ks, js, form_now):
            def f(v, ks=ks, js=js, form_now=form_now):
                if first_sort_op[0]:
                    v.wait_ge(sRS.s, sort_ready)
                    first_sort_op[0] = False
                else:
                    v.wait_ge(sMM.s, sMM.n)
                k = ks
                pp_bit, fs_bit = None, None
                if k < 14:
                    if form_now == "N":
                        if k >= 7:
                            fs_bit = k - 7
                        else:
                            pp_bit = k
                    else:
                        if k >= 7:
                            pp_bit = k - 7
                        else:
                            fs_bit = k
                if pp_bit is not None:
                    negate_partitions(v, pp_bit)
                if fs_bit is not None:
                    negate_free(v, fs_bit)
                for j in js:
                    d = (1 << (j - 7)) if form_now == "N" else (1 << j)
                    cmpex(v, d)
                if pp_bit is not None:
                    negate_partitions(v, pp_bit)
                if fs_bit is not None:
                    negate_free(v, fs_bit)
                sV.inc(v.engine_nop())
            blk.vector(f)

        form = "T"
        for k in range(1, 15):
            sweeps = list(range(k - 1, -1, -1))
            hi = [j for j in sweeps if j >= 7]
            lo = [j for j in sweeps if j < 7]
            if hi:
                if form != "N":
                    emit_transpose_all(sV.n)
                    form = "N"
                emit_sweeps(k, hi, "N")
            if lo:
                if form != "T":
                    emit_transpose_all(sV.n)
                    form = "T"
                emit_sweeps(k, lo, "T")
        if form != "N":
            emit_transpose_all(sV.n)
            form = "N"
        sorted_done = sV.n

        def f(g, sorted_done=sorted_done):
            g.wait_ge(sV.s, sorted_done)
            sW.inc(g.dma_start(out=dbg_sort, in_=sorc[:]), 16)
        blk.gpsimd(f)

        def f(v, sorted_done=sorted_done):
            v.wait_ge(sV.s, sorted_done)
            v.memset(hmsk[:, :], 0.0)
            v.memset(hmsk[0:64, 0:1], 1.0)
            sV.inc(v.memset(hmsk[64:128, 1:2], 1.0))
        blk.vector(f)
        hm = sV.n

        def f(t, hm=hm):
            t.wait_ge(sV.s, hm)
            t.wait_ge(sSC.s, sSC.n)
            t.matmul(P1[0:2, 0:512], hmsk[:], sorc[:].rearrange("p c f -> p (c f)")[:, 0:512],
                     start=True, stop=True)
            sMM.inc(t.matmul(P6[0:2, 0:512], hmsk[:],
                             sorc[:].rearrange("p c f -> p (c f)")[:, 512:1024], start=True, stop=True))
        blk.tensor(f)
        gsm = sMM.n

        def f(v, gsm=gsm):
            v.wait_ge(sMM.s, gsm)
            v.tensor_copy(gssb[:, 0:512], P1[0:2, 0:512])
            sV.inc(v.tensor_copy(gssb[:, 512:1024], P6[0:2, 0:512]))
        blk.vector(f)
        gse = sV.n

        def f(g, gse=gse):
            g.wait_ge(sV.s, gse)
            sW.inc(g.dma_start(out=dbg_gs, in_=gssb[:]), 16)
        blk.gpsimd(f)

        def f(s, gse=gse):
            s.wait_ge(sV.s, gse)
            sRS.inc(s.dma_start(
                out=gs_in.rearrange("c (two f) -> two c f", two=2),
                in_=gssb[:].rearrange("h (c f) -> h c f", f=128)), 16)
        blk.sync(f)
        gsw = sRS.n

        def f(g, gsw=gsw):
            g.wait_ge(sRS.s, gsw)
            sCC.inc(g.collective_compute(
                "AllGather", Op.bypass,
                replica_groups=[list(range(NCORE))],
                ins=[gs_in.opt()], outs=[gs_out.opt()]))
        blk.gpsimd(f)
        ag2 = sCC.n

        def f(s, ag2=ag2):
            s.wait_ge(sCC.s, ag2)
            sIN.inc(s.dma_start(out=gso[:], in_=gs_out[:, :]), 16)
            sIN.inc(s.dma_start(out=sor2[:],
                                in_=gs_out.rearrange("e q -> (e q)").rearrange("(p f) -> p f", f=128)), 16)
        blk.sync(f)
        g2ld = sIN.n

        def sor2_negate_partitions(v, bbit):
            v.tensor_scalar(sor2[:], sor2[:], sgnc[:, bbit:bbit + 1], 0.0, Op.mult, Op.add)

        def sor2_negate_free(v, period_bit):
            half = 1 << period_bit
            vw = sor2[:].rearrange("p (n w) -> p n w", w=2 * half)
            v.tensor_scalar(vw[:, :, half:], vw[:, :, half:], -1.0, 0.0, Op.mult, Op.add)

        def sor2_cmpex(v, dfree):
            vw = sor2[:].rearrange("p (n two d) -> p n two d", two=2, d=dfree)
            ncn = vw.shape[1]
            tm = sor2t[:].rearrange("p (n d) -> p n d", d=dfree)[:, 0:ncn, :]
            v.tensor_tensor(tm, vw[:, :, 0, :], vw[:, :, 1, :], Op.min)
            v.tensor_tensor(vw[:, :, 1, :], vw[:, :, 0, :], vw[:, :, 1, :], Op.max)
            v.tensor_copy(vw[:, :, 0, :], tm)

        first2 = [True]

        def emit2_transpose(dep):
            def f(t, dep=dep):
                t.wait_ge(sV.s, dep)
                t.wait_ge(sSC.s, sSC.n)
                sMM.inc(t.transpose(P1[:, 0:128], sor2[:], eyef[:, :]))
            blk.tensor(f)
            trm = sMM.n

            def f(v, trm=trm):
                v.wait_ge(sMM.s, trm)
                sV.inc(v.tensor_copy(sor2[:], P1[:, 0:128]))
            blk.vector(f)

        def emit2_sweeps(k, js, form_now):
            def f(v, k=k, js=js, form_now=form_now):
                if first2[0]:
                    v.wait_ge(sIN.s, g2ld)
                    first2[0] = False
                else:
                    v.wait_ge(sMM.s, sMM.n)
                pp_bit, fs_bit = None, None
                if k < 14:
                    if form_now == "N":
                        if k >= 7:
                            fs_bit = k - 7
                        else:
                            pp_bit = k
                    else:
                        if k >= 7:
                            pp_bit = k - 7
                        else:
                            fs_bit = k
                if pp_bit is not None:
                    sor2_negate_partitions(v, pp_bit)
                if fs_bit is not None:
                    sor2_negate_free(v, fs_bit)
                for j in js:
                    d = (1 << (j - 7)) if form_now == "N" else (1 << j)
                    sor2_cmpex(v, d)
                if pp_bit is not None:
                    sor2_negate_partitions(v, pp_bit)
                if fs_bit is not None:
                    sor2_negate_free(v, fs_bit)
                sV.inc(v.engine_nop())
            blk.vector(f)

        form2 = "T"
        for k in range(1, 15):
            sweeps = list(range(k - 1, -1, -1))
            hi = [j for j in sweeps if j >= 7]
            lo = [j for j in sweeps if j < 7]
            if hi:
                if form2 != "N":
                    emit2_transpose(sV.n)
                    form2 = "N"
                emit2_sweeps(k, hi, "N")
            if lo:
                if form2 != "T":
                    emit2_transpose(sV.n)
                    form2 = "T"
                emit2_sweeps(k, lo, "T")
        if form2 != "N":
            emit2_transpose(sV.n)
            form2 = "N"
        g2sorted = sV.n

        def f(t, g2sorted=g2sorted):
            t.wait_ge(sV.s, g2sorted)
            t.wait_ge(sSC.s, sSC.n)
            sMM.inc(t.matmul(P1[0:64, 0:1], onrf[0:1, 0:64], sor2[0:1, 96:97], start=True, stop=True))
        blk.tensor(f)
        thm = sMM.n

        def f(v, thm=thm, g2ld=g2ld):
            v.wait_ge(sMM.s, thm)
            v.wait_ge(sIN.s, g2ld)
            sV.inc(v.tensor_copy(kth[:, 0:1], P1[0:64, 0:1]))
        blk.vector(f)

        def f(v, m=sV.n):
            v.wait_ge(sV.s, m)
            sV.inc(v.tensor_scalar(gso[:], gso[:], kth[:, 0:1], 0.0, Op.is_ge, Op.add))
        blk.vector(f)

        def f(v, m=sV.n):
            v.wait_ge(sV.s, m)
            sV.inc(v.tensor_reduce(kth[:, 1:2], gso[:], mybir.AxisListType.X, Op.add))
        blk.vector(f)

        def f(v, m=sV.n):
            v.wait_ge(sV.s, m)
            sV.inc(v.tensor_scalar(kth[:, 2:3], kth[:, 1:2], -64.0, 16384.0, Op.mult, Op.add))
        blk.vector(f)
        kd = sV.n

        def f(g, kd=kd):
            g.wait_ge(sV.s, kd)
            sW.inc(g.dma_start(out=dbg_kth, in_=kth[:]), 16)
        blk.gpsimd(f)

        def f(s):
            sIN.inc(s.dma_start(out=seltc[:], in_=seltau[:, :]), 16)
        blk.sync(f)
        stau_ld = sIN.n

        def f(t, kd=kd, stau_ld=stau_ld):
            t.wait_ge(sV.s, kd)
            t.wait_ge(sIN.s, stau_ld)
            t.matmul(P6[0:1, 0:8], kth[:, 2:3], seltc[:], start=True, stop=False)
            sMM.inc(t.matmul(P6[0:1, 8:16], kth[:, 1:2], seltc[:], start=False, stop=True))
        blk.tensor(f)
        rm = sMM.n

        def f(v, rm=rm):
            v.wait_ge(sMM.s, rm)
            v.tensor_copy(tfix[:, 0, :], P6[0:1, 0:8])
            sV.inc(v.tensor_copy(tfix[:, 1, :], P6[0:1, 8:16]))
        blk.vector(f)
        rrow = sV.n

        def f(t, rrow=rrow):
            t.wait_ge(sV.s, rrow)
            sMM.inc(t.matmul(P6[:, 16:24], onrf[0:1, :], tfix[:, 0, :], start=True, stop=True))
        blk.tensor(f)
        rbm = sMM.n

        def f(v, rbm=rbm):
            v.wait_ge(sMM.s, rbm)
            sV.inc(v.tensor_copy(rb128[:], P6[:, 16:24]))
        blk.vector(f)
        rbd = sV.n

        for s_ in range(8):
            def f(v, s_=s_, rbd=rbd):
                if s_ == 0:
                    v.wait_ge(sV.s, rbd)
                else:
                    v.wait_ge(sMM.s, sMM.n)
                sV.inc(v.tensor_scalar(eqt[:], iotc[:], rb128[:, s_:s_ + 1], 0.0, Op.is_equal, Op.add))
            blk.vector(f)

            def f(v, s_=s_, m=sV.n):
                v.wait_ge(sV.s, m)
                sV.inc(v.tensor_tensor(eqt[:], eqt[:], sorc[:, s_, :], Op.mult))
            blk.vector(f)
            eqd = sV.n

            def f(t, s_=s_, eqd=eqd):
                t.wait_ge(sV.s, eqd)
                sMM.inc(t.matmul(P7[0:1, 0:128], onc[:, 0:1], eqt[:], start=True, stop=True))
            blk.tensor(f)
            tmm = sMM.n

            def f(v, s_=s_, tmm=tmm):
                v.wait_ge(sMM.s, tmm)
                sV.inc(v.tensor_reduce(tfix[:, 2, s_:s_ + 1], P7[0:1, 0:128], mybir.AxisListType.X, Op.add))
            blk.vector(f)
        taud = sV.n

        def f(v, taud=taud):
            v.wait_ge(sV.s, taud)
            sV.inc(v.tensor_scalar(tfix[:, 1, :], tfix[:, 1, :], 0.5, 0.0, Op.is_ge, Op.add))
        blk.vector(f)

        def f(v, m=sV.n):
            v.wait_ge(sV.s, m)
            sV.inc(v.tensor_tensor(tfix[:, 2, :], tfix[:, 2, :], tfix[:, 1, :], Op.mult))
        blk.vector(f)

        def f(v, m=sV.n):
            v.wait_ge(sV.s, m)
            sV.inc(v.tensor_scalar(tfix[:, 1, :], tfix[:, 1, :], -1e30, 1e30, Op.mult, Op.add))
        blk.vector(f)

        def f(v, m=sV.n):
            v.wait_ge(sV.s, m)
            sV.inc(v.tensor_tensor(tfix[:, 2, :], tfix[:, 2, :], tfix[:, 1, :], Op.add))
        blk.vector(f)
        tfd = sV.n

        def f(g, tfd=tfd):
            g.wait_ge(sV.s, tfd)
            sW.inc(g.dma_start(out=dbg_tau, in_=tfix[:]), 16)
        blk.gpsimd(f)

        def f(s, tfd=tfd):
            s.wait_ge(sV.s, tfd)
            sRS.inc(s.dma_start(out=tau_in[0:1, 0:8], in_=tfix[:, 2, :]), 16)
        blk.sync(f)
        tw = sRS.n

        def f(g, tw=tw):
            g.wait_ge(sRS.s, tw)
            sCC.inc(g.collective_compute(
                "AllGather", Op.bypass,
                replica_groups=[list(range(NCORE))],
                ins=[tau_in.opt()], outs=[tau_out.opt()]))
        blk.gpsimd(f)
        ag3 = sCC.n

        def f(s, ag3=ag3):
            s.wait_ge(sCC.s, ag3)
            sIN.inc(s.dma_start(out=tausb[:].rearrange("o (c s) -> o c s", c=8),
                                in_=tau_out.rearrange("(c r) s -> r c s", r=8)[0:1]), 16)
        blk.sync(f)
        tld = sIN.n

        def f(t, tld=tld):
            t.wait_ge(sIN.s, tld)
            t.wait_ge(sSC.s, sSC.n)
            sMM.inc(t.matmul(P1[:, 0:64], onrf[:], tausb[:], start=True, stop=True))
        blk.tensor(f)
        tbm = sMM.n

        def f(v, tbm=tbm):
            v.wait_ge(sMM.s, tbm)
            v.tensor_tensor(ltm[:], stmt[:], P1[:, 0:64].unsqueeze(1).broadcast_to((128, 16, NE)), Op.is_ge)
            v.tensor_tensor(ltm[:], ltm[:], stmt[:], Op.mult)
            sV.inc(v.tensor_reduce(lrow[:, :, 0], ltm[:], mybir.AxisListType.X, Op.add))
        blk.vector(f)

        def f(v, m=sV.n):
            v.wait_ge(sV.s, m)
            sV.inc(v.tensor_scalar(lrow[:, :, 0], lrow[:, :, 0], 1.0, 1e-6, Op.mult, Op.add))
        blk.vector(f)

        def f(v, m=sV.n):
            v.wait_ge(sV.s, m)
            sV.inc(v.reciprocal(lrow[:, :, 1], lrow[:, :, 0]))
        blk.vector(f)

        def f(v, m=sV.n):
            v.wait_ge(sV.s, m)
            sV.inc(v.tensor_tensor(ltm[:], ltm[:],
                                   lrow[:, :, 1:2].broadcast_to((128, 16, NE)), Op.mult))
        blk.vector(f)
        fsd = sV.n

        fse_marks = {}
        for tc in range(16):
            def f(t, tc=tc, fsd=fsd):
                if tc == 0:
                    t.wait_ge(sV.s, fsd)
                    t.wait_ge(sSC.s, sSC.n)
                if tc >= 2:
                    t.wait_ge(sSC.s, fse_marks[tc - 2])
                pb = P0 if tc % 2 == 0 else P2
                sMM.inc(t.transpose(pb[0:64, 0:128], ltm[:, tc, :], eyef[:, :]))
            blk.tensor(f)
            trm = sMM.n

            def f(s, tc=tc, trm=trm):
                s.wait_ge(sMM.s, trm)
                pb = P0 if tc % 2 == 0 else P2
                s.copy(fsem[:, tc * 128:tc * 128 + 128], pb[0:64, 0:128])
                sSC.inc(s.copy(fseb[:, tc * 128:tc * 128 + 128], pb[0:64, 0:128]))
            blk.scalar(f)
            fse_marks[tc] = sSC.n
        fs_done = sSC.n

        if tap == "scores":
            def f(g):
                g.wait_ge(sSC.s, sSC.n)
                g.wait_ge(sV.s, sV.n)
                sW.inc(g.dma_start(out=tapt, in_=semj[:]), 16)
            blk.gpsimd(f)
        if tap == "sorted":
            def f(g):
                g.wait_ge(sV.s, sV.n)
                sW.inc(g.dma_start(out=tapt, in_=sorc[:]), 16)
            blk.gpsimd(f)
        if tap == "gs":
            def f(g):
                g.wait_ge(sV.s, sV.n)
                g.wait_ge(sRS.s, sRS.n)
                sW.inc(g.dma_start(out=tapt, in_=gs_in[:, :]), 16)
            blk.gpsimd(f)
        if tap == "tau":
            def f(g):
                g.wait_ge(sV.s, sV.n)
                g.wait_ge(sIN.s, sIN.n)
                sW.inc(g.dma_start(out=tapt, in_=tau_out[0:8, :]), 16)
            blk.gpsimd(f)
        if tap == "fs":
            def f(g):
                g.wait_ge(sSC.s, sSC.n)
                sW.inc(g.dma_start(out=tapt, in_=fsem[:]), 16)
            blk.gpsimd(f)

        stD.close()

        stE = contextlib.ExitStack()
        sbufE = lambda name, shape, dt=F32: stE.enter_context(nc.sbuf_tensor(name, shape, dt))
        hg = sbufE("hg", [128, 8, 2, 1024], BF16)
        wdnb = sbufE("wdnb", [128, 8, 2, 8, 128], BF16)
        wupb = [sbufE(f"wupb{i}", [128, 8, HID], BF16) for i in range(2)]
        wgtb = [sbufE(f"wgtb{i}", [128, 8, HID], BF16) for i in range(2)]
        fsbb = sbufE("fsbb", [128, 1024], BF16)
        bdnb = sbufE("bdnb", [64, 8, 128], BF16)
        bupc = sbufE("bupc", [128, 2, NE])
        bgtc = sbufE("bgtc", [128, 2, NE])
        upt = sbufE("upt", [128, 512], BF16)
        gtt = sbufE("gtt", [128, 512], BF16)
        fsrow = sbufE("fsrow", [1, 1024], BF16)

        def f(s, fs_done=fs_done):
            s.wait_ge(sSC.s, fs_done)
            sIN.inc(s.dma_start(out=bupc[:], in_=bupT[:, :, :]), 16)
            sIN.inc(s.dma_start(out=bgtc[:], in_=bgtT[:, :, :]), 16)
        blk.sync(f)

        def f(g, fs_done=fs_done):
            g.wait_ge(sSC.s, fs_done)
            sW.inc(g.dma_start(out=bdnb[:], in_=bdn.rearrange("e (d f) -> e d f", f=128)), 16)
        blk.gpsimd(f)
        bldw = sW.n

        def f(v):
            v.wait_ge(sV.s, sV.n)
            sV.inc(v.memset(outb[:], 0.0))
        blk.vector(f)
        ob0 = sV.n

        wld_marks = {}
        upgate_marks = {}
        for tp in range(2):
            toff = tp * 1024
            for gr in range(8):
                for e8 in range(8):
                    e = gr * 8 + e8
                    slot = e8 % 2

                    def f(g, e=e, slot=slot, tp=tp, gr=gr, e8=e8, vmark=sV.n):
                        g.wait_ge(sV.s, vmark)
                        if tp == 0 and gr == 0 and e8 == 0:
                            g.wait_ge(sV.s, ob0)
                            g.wait_ge(sW.s, bldw)
                            g.wait_ge(sSC.s, sSC.n)
                            g.wait_ge(sMM.s, sMM.n)
                        prev = (tp, gr, e8 - 2)
                        if prev in upgate_marks:
                            g.wait_ge(sMM.s, upgate_marks[prev])
                        elif e8 < 2 and (tp, gr) != (0, 0):
                            pg = (tp, gr - 1) if gr > 0 else (tp - 1, 7)
                            g.wait_ge(sMM.s, upgate_marks[(pg[0], pg[1], 6 + e8)])
                        sW.inc(g.dma_start(out=wupb[slot][:],
                                           in_=wup.rearrange("e (c p) h -> e p c h", p=128)[e]), 16)
                        sW.inc(g.dma_start(out=wgtb[slot][:],
                                           in_=wgt.rearrange("e (c p) h -> e p c h", p=128)[e]), 16)
                        sW.inc(g.dma_start(out=wdnb[:, e8, :, :, :],
                                           in_=wdn.rearrange("e (c p) (d f) -> e p c d f", p=128, f=128)[e]), 16)
                    blk.gpsimd(f)
                    wld_marks[(tp, e)] = sW.n

                    def f(t, e=e, tp=tp, toff=toff):
                        t.wait_ge(sSC.s, sSC.n)
                        t.wait_ge(sV.s, sV.n)
                        t.matmul(P6[0:1, 0:512], eyeb[0:64, e:e + 1], fseb[:, toff:toff + 512],
                                 start=True, stop=True)
                        sMM.inc(t.matmul(P7[0:1, 0:512], eyeb[0:64, e:e + 1],
                                         fseb[:, toff + 512:toff + 1024], start=True, stop=True))
                    blk.tensor(f)
                    fbm = sMM.n

                    def f(s, fbm=fbm):
                        s.wait_ge(sMM.s, fbm)
                        s.copy(fsrow[:, 0:512], P6[0:1, 0:512])
                        sSC.inc(s.copy(fsrow[:, 512:1024], P7[0:1, 0:512]))
                    blk.scalar(f)
                    fbr = sSC.n

                    def f(t, fbr=fbr):
                        t.wait_ge(sSC.s, fbr)
                        t.matmul(P6[:, 0:512], onrb[:], fsrow[:, 0:512], start=True, stop=True)
                        sMM.inc(t.matmul(P7[:, 0:512], onrb[:], fsrow[:, 512:1024], start=True, stop=True))
                    blk.tensor(f)
                    fbm2 = sMM.n

                    def f(s, fbm2=fbm2):
                        s.wait_ge(sMM.s, fbm2)
                        s.copy(fsbb[:, 0:512], P6[:, 0:512])
                        sSC.inc(s.copy(fsbb[:, 512:1024], P7[:, 0:512]))
                    blk.scalar(f)
                    fbe = sSC.n

                    for h2 in range(2):
                        for sub in range(2):
                            def f(t, e=e, slot=slot, h2=h2, sub=sub, toff=toff,
                                  wldm=wld_marks[(tp, e)], fbe=fbe):
                                t.wait_ge(sW.s, wldm)
                                t.wait_ge(sSC.s, sSC.n)
                                t.wait_ge(sV.s, sV.n)
                                for dc in range(8):
                                    t.matmul(P0[:, 0:512], wupb[slot][:, dc, h2 * 128:h2 * 128 + 128],
                                             x2b[:, dc, toff + sub * 512:toff + sub * 512 + 512],
                                             start=(dc == 0), stop=(dc == 7))
                                for dc in range(8):
                                    mi = t.matmul(P2[:, 0:512], wgtb[slot][:, dc, h2 * 128:h2 * 128 + 128],
                                                  x2b[:, dc, toff + sub * 512:toff + sub * 512 + 512],
                                                  start=(dc == 0), stop=(dc == 7))
                                sMM.inc(mi)
                            blk.tensor(f)
                            ugm = sMM.n

                            def f(s, e=e, h2=h2, ugm=ugm):
                                s.wait_ge(sMM.s, ugm)
                                sSC.inc(s.activation(upt[:], P0[:, 0:512], AF.Silu, bias=bupc[:, h2, e:e + 1]))
                            blk.scalar(f)
                            uge = sSC.n

                            def f(v, e=e, e8=e8, h2=h2, sub=sub, uge=uge, ugm=ugm):
                                v.wait_ge(sSC.s, uge)
                                v.wait_ge(sMM.s, ugm)
                                v.tensor_scalar(gtt[:], P2[:, 0:512], bgtc[:, h2, e:e + 1], 0.0,
                                                Op.add, Op.add)
                                v.tensor_tensor(upt[:], upt[:], gtt[:], Op.mult)
                                sV.inc(v.tensor_tensor(hg[:, e8, h2, sub * 512:sub * 512 + 512],
                                                       upt[:], fsbb[:, sub * 512:sub * 512 + 512], Op.mult))
                            blk.vector(f)
                    upgate_marks[(tp, gr, e8)] = sMM.n
                hg_done = sV.n

                if tp == 1 and gr == 0:
                    def f(g, hg_done=hg_done):
                        g.wait_ge(sV.s, hg_done)
                        sW.inc(g.dma_start(out=dbg_hg, in_=hg[:]), 16)
                    blk.gpsimd(f)

                dn_marks = {}
                for dt_ in range(8):
                    for sub in range(2):
                        def f(t, gr=gr, dt_=dt_, sub=sub, tp=tp, toff=toff, hg_done=hg_done):
                            if dt_ == 0 and sub == 0:
                                t.wait_ge(sV.s, hg_done)
                            if (dt_ * 2 + sub) >= 2:
                                t.wait_ge(sV.s, dn_marks[dt_ * 2 + sub - 2])
                            pb = P4 if (dt_ * 2 + sub) % 2 == 0 else P6
                            for e8 in range(8):
                                for h2 in range(2):
                                    last = (e8 == 7 and h2 == 1 and gr != 7)
                                    mi = t.matmul(pb[:, 0:512], wdnb[:, e8, h2, dt_, :],
                                                  hg[:, e8, h2, sub * 512:sub * 512 + 512],
                                                  start=(e8 == 0 and h2 == 0), stop=last)
                            if gr == 7:
                                mi = t.matmul(pb[:, 0:512], bdnb[:, dt_, :],
                                              fseb[:, toff + sub * 512:toff + sub * 512 + 512],
                                              start=False, stop=True)
                            sMM.inc(mi)
                        blk.tensor(f)
                        dnm = sMM.n

                        def f(v, dt_=dt_, sub=sub, tp=tp, toff=toff, dnm=dnm):
                            v.wait_ge(sMM.s, dnm)
                            pb = P4 if (dt_ * 2 + sub) % 2 == 0 else P6
                            sV.inc(v.tensor_tensor(outb[:, dt_, toff + sub * 512:toff + sub * 512 + 512],
                                                   outb[:, dt_, toff + sub * 512:toff + sub * 512 + 512],
                                                   pb[:, 0:512], Op.add))
                        blk.vector(f)
                        dn_marks[dt_ * 2 + sub] = sV.n
                if tp == 1 and gr == 0:
                    def f(g, m=sV.n):
                        g.wait_ge(sV.s, m)
                        sW.inc(g.dma_start(out=dbg_ob, in_=outb[:, :, 1024:1536]), 16)
                    blk.gpsimd(f)
        ffn_done = sV.n

        if tap == "ffnout":
            def f(g, ffn_done=ffn_done):
                g.wait_ge(sV.s, ffn_done)
                sW.inc(g.dma_start(out=tapt, in_=outb[:]), 16)
            blk.gpsimd(f)

        stE.close()

        stF = contextlib.ExitStack()
        sbufF = lambda name, shape, dt=F32: stF.enter_context(nc.sbuf_tensor(name, shape, dt))
        fout = sbufF("fout", [128, 8, OC])
        xf2 = sbufF("xf2", [128, 8, OC])
        sqf = sbufF("sqf", [128, 8, OC], BF16)
        rwf2 = sbufF("rwf2", [1, 5, OC])
        rwb2 = sbufF("rwb2", [1, 2, OC], BF16)

        for pn in range(4):
            c0 = pn * OC

            def f(s, pn=pn, c0=c0, ffn_done=ffn_done, outp=sOUT.n):
                if pn == 0:
                    s.wait_ge(sV.s, ffn_done)
                else:
                    s.wait_ge(sV.s, sV.n)
                    s.wait_ge(sOUT.s, outp)
                sIN.inc(s.dma_start(out=xf2[:], in_=x2_spill.rearrange("(c p) t -> p c t", p=128)[:, :, c0:c0 + OC]), 16)
            blk.sync(f)
            ld = sIN.n

            def f(s, pn=pn, c0=c0, ffn_done=ffn_done):
                s.wait_ge(sSC.s, sSC.n)
                if pn == 0:
                    s.wait_ge(sV.s, ffn_done)
                sSC.inc(s.activation(sqf[:], outb[:, :, c0:c0 + OC], AF.Square))
            blk.scalar(f)
            sq = sSC.n

            def f(t, sq=sq, c0=c0, pn=pn):
                t.wait_ge(sSC.s, sq)
                if pn > 0:
                    t.wait_ge(sV.s, sV.n)
                for dc in range(8):
                    t.matmul(P1[0:1, 0:OC], oncb[:], outb[:, dc, c0:c0 + OC], start=(dc == 0), stop=(dc == 7))
                for dc in range(8):
                    mi = t.matmul(P6[0:1, 0:OC], oncb[:], sqf[:, dc, :], start=(dc == 0), stop=(dc == 7))
                sMM.inc(mi)
            blk.tensor(f)
            stt = sMM.n

            def f(v, stt=stt):
                v.wait_ge(sMM.s, stt)
                v.tensor_scalar(rwf2[:, 0, :], P1[0:1, 0:OC], 1.0 / DM, 0.0, Op.mult, Op.add)
                v.tensor_scalar(rwf2[:, 1, :], P6[0:1, 0:OC], 1.0 / DM, 0.0, Op.mult, Op.add)
                v.tensor_tensor(rwf2[:, 2, :], rwf2[:, 0, :], rwf2[:, 0, :], Op.mult)
                sV.inc(v.tensor_tensor(rwf2[:, 2, :], rwf2[:, 1, :], rwf2[:, 2, :], Op.subtract))
            blk.vector(f)
            vr = sV.n

            def f(s, vr=vr):
                s.wait_ge(sV.s, vr)
                sSC.inc(s.activation(rwf2[:, 3, :], rwf2[:, 2, :], AF.Sqrt, bias=EPS))
            blk.scalar(f)
            sd = sSC.n

            def f(v, sd=sd):
                v.wait_ge(sSC.s, sd)
                v.reciprocal(rwf2[:, 4, :], rwf2[:, 3, :])
                v.tensor_copy(rwb2[:, 0, :], rwf2[:, 0, :])
                sV.inc(v.tensor_copy(rwb2[:, 1, :], rwf2[:, 4, :]))
            blk.vector(f)
            rw = sV.n

            def f(t, rw=rw):
                t.wait_ge(sV.s, rw)
                t.matmul(P1[:, 0:OC], onrb[:], rwb2[:, 0, :], start=True, stop=True)
                sMM.inc(t.matmul(P6[:, 0:OC], onrb[:], rwb2[:, 1, :], start=True, stop=True))
            blk.tensor(f)
            bc = sMM.n

            def f(v, bc=bc, ld=ld, c0=c0):
                v.wait_ge(sMM.s, bc)
                v.wait_ge(sIN.s, ld)
                mb = P1[:, 0:OC].unsqueeze(1).broadcast_to((128, 8, OC))
                rb = P6[:, 0:OC].unsqueeze(1).broadcast_to((128, 8, OC))
                v.tensor_tensor(fout[:], outb[:, :, c0:c0 + OC], mb, Op.subtract)
                v.tensor_tensor(fout[:], fout[:], rb, Op.mult)
                sV.inc(v.tensor_tensor(fout[:], fout[:], xf2[:], Op.add))
            blk.vector(f)
            fd = sV.n

            def f(s, c0=c0, fd=fd):
                s.wait_ge(sV.s, fd)
                sOUT.inc(s.dma_start(
                    out=yout.rearrange("(c p) t -> p c t", p=128)[:, :, c0:c0 + OC],
                    in_=fout[:]), 16)
            blk.scalar(f)

        def f(s):
            s.wait_ge(sOUT.s, sOUT.n)
        blk.scalar(f)

        stF.close()
        stC.close()

    return nc



def _hostprep(inputs, n_oc=16):
    L = n_oc * OC
    x = np.asarray(inputs["L0_attn_x"], dtype=np.float32)
    W_in = np.asarray(inputs["L0_attn_W_in"], dtype=np.float32)
    conv_w = np.asarray(inputs["L0_attn_conv_w"], dtype=np.float32)
    conv_b = np.asarray(inputs["L0_attn_conv_b"], dtype=np.float32)
    A_log = np.asarray(inputs["L0_attn_A_log"], dtype=np.float32)
    Dv = np.asarray(inputs["L0_attn_D"], dtype=np.float32)
    dt_b = np.asarray(inputs["L0_attn_dt_bias"], dtype=np.float32)
    W_out = np.asarray(inputs["L0_attn_W_out"], dtype=np.float32)

    eye = np.eye(128, dtype=np.float32)
    hmk = np.zeros((8, 4, 128), np.float32)
    for b in range(4):
        for pp in range(128):
            hmk[2 * b + pp // 64, b, pp] = 1.0
    mones = np.zeros((128, 128), np.float32)
    mones[0:64, 0:64] = 1.0
    mones[64:128, 64:128] = 1.0
    gm = np.zeros((4, 128, 128), np.float32)
    for li, mh in enumerate((8, 16, 32, 64)):
        bu = np.arange(128) // mh
        gm[li] = ((bu[:, None] % 2 == 0) & (bu[None, :] == bu[:, None] + 1)).astype(np.float32)
    bm = np.ones((128, OC), np.float32)
    bm[:, ::8] = 0.0
    sgn = np.ones((128, 7), np.float32)
    for bbit in range(7):
        sgn[:, bbit] = np.where(((np.arange(128) >> bbit) & 1) == 1, -1.0, 1.0)
    iog = (np.arange(128)[None, :] * 128 + np.arange(128)[:, None]).astype(np.float32)

    in_maps = []
    for c in range(NCORE):
        b, g = c // 4, c % 4
        hs = slice(8 * g, 8 * g + 8)
        ch = slice(512 * g, 512 * g + 512)
        xb = np.ascontiguousarray(x[b].T[:, :L])
        tok = slice(2048 * g, 2048 * g + 2048)
        xr = np.ascontiguousarray(x[b].T[:, tok])
        rows = [W_in[ch], W_in[2048 + 512 * g: 2048 + 512 * (g + 1)]]
        for h in range(8 * g, 8 * g + 8):
            rows.append(W_in[4096 + 129 * h: 4096 + 129 * h + 64])
        for h in range(8 * g, 8 * g + 8):
            rows.append(W_in[4096 + 129 * h + 64: 4096 + 129 * h + 128])
        rows.append(np.stack([W_in[4096 + 129 * h + 128] for h in range(8 * g, 8 * g + 8)]))
        wcat = np.concatenate(rows, axis=0)
        wpad = np.zeros((FOP, DM), np.float32)
        wpad[:wcat.shape[0]] = wcat
        w_in_c = np.ascontiguousarray(wpad.T)
        w_out_c = np.ascontiguousarray(W_out[:, ch].T)
        pidx = np.arange(128)
        cw = np.zeros((128, 4, 4), np.float32)
        cb = np.zeros((128, 4), np.float32)
        al = np.zeros((128, 4), np.float32)
        dc = np.zeros((128, 4), np.float32)
        for bi in range(4):
            chan = 512 * g + 128 * bi + pidx
            cw[:, bi] = conv_w[chan, 0]
            cb[:, bi] = conv_b[chan]
            dc[:, bi] = Dv[chan // 64]
            al[:, bi] = A_log[chan // 64, pidx % 64]
        dtb = np.zeros((128, 1), np.float32)
        dtb[0:8, 0] = dt_b[hs]
        sel = np.zeros((128, 4, 16), np.float32)
        for i in range(4):
            for jj in range(2):
                src_core = 2 * i + jj
                for s_ in range(8):
                    e_global = 8 * c + s_
                    sel[64 * jj + e_global, i, jj * 8 + s_] = 1.0
        m = dict(
            xf=xb, xres=xr, w_in=w_in_c, w_out=w_out_c, convw=cw, convb=cb,
            alog=al, dvec=dc, dtbias=dtb,
            wr=np.ascontiguousarray(np.asarray(inputs["L0_ffn_Wr"], dtype=np.float32)),
            brv=np.ascontiguousarray(np.asarray(inputs["L0_ffn_br"], dtype=np.float32).reshape(NE, 1)),
            wup=np.ascontiguousarray(np.asarray(inputs["L0_ffn_Wup"], dtype=np.float32)),
            wgt=np.ascontiguousarray(np.asarray(inputs["L0_ffn_Wgate"], dtype=np.float32)),
            wdn=np.ascontiguousarray(np.asarray(inputs["L0_ffn_Wdown"], dtype=np.float32)),
            bupT=np.ascontiguousarray(
                np.asarray(inputs["L0_ffn_bup"], dtype=np.float32).T.reshape(2, 128, NE).transpose(1, 0, 2)),
            bgtT=np.ascontiguousarray(
                np.asarray(inputs["L0_ffn_bgate"], dtype=np.float32).T.reshape(2, 128, NE).transpose(1, 0, 2)),
            bdn=np.ascontiguousarray(np.asarray(inputs["L0_ffn_bdown"], dtype=np.float32)),
            eyec=eye, hmk=hmk, mones=mones, gmaskc=gm, bmask=bm, sgnp=sgn,
            iotag=iog, selt=sel, seltau=np.eye(64, dtype=np.float32)[:, 8 * c:8 * c + 8].copy(),
            onescol=np.ones((128, 1), np.float32),
            onesrow=np.ones((1, 128), np.float32),
            constv=np.tile(np.array([[1e-5, 1.0, -1.0, 0.0]], np.float32), (128, 1)),
        )
        in_maps.append(m)
    return in_maps


_NC_CACHE = {}


def kernel(**inputs):
    in_maps = _hostprep(inputs)
    key = ("full", 16)
    if key not in _NC_CACHE:
        _NC_CACHE[key] = build_nc(n_oc=16, tap=None)
    nc = _NC_CACHE[key]
    res = run_bass_kernel_spmd(nc, in_maps, core_ids=list(range(NCORE)))
    out = np.zeros((BB, LSEQ, DM), np.float32)
    for c in range(NCORE):
        b, g = c // 4, c % 4
        out[b, 2048 * g:2048 * (g + 1), :] = res.results[c]["yout"].T
    return out


# version 10
# speedup vs baseline: 1.1340x; 1.1340x over previous
import contextlib
import numpy as np

import concourse.bass as bass
import concourse.mybir as mybir
from concourse.bass_utils import run_bass_kernel_spmd
from concourse.alu_op_type import AluOpType as Op

F32 = mybir.dt.float32
BF16 = mybir.dt.bfloat16
AF = mybir.ActivationFunctionType

NCORE = 8
BB, LSEQ, DM = 2, 8192, 1024
HPC = 8
CPC = 512
FOP = 2176
OC = 512
QC = 128
TLOC = 2048
NE, HID = 64, 256
EPS = 1e-5


class Sem:
    def __init__(self, st, nc, name):
        self.s = st.enter_context(nc.semaphore(name))
        self.n = 0

    def inc(self, instr, k=1):
        instr.then_inc(self.s, k)
        self.n += k
        return instr


def build_nc(n_oc=16, tap=None):
    nc = bass.Bass("TRN2", target_bir_lowering=False, debug=False,
                   detect_race_conditions=False)
    L = n_oc * OC

    def din(name, shape):
        return nc.dram_tensor(name, shape, F32, kind="ExternalInput").ap()

    xf = din("xf", [DM, L])
    xres = din("xres", [DM, TLOC])
    w_in = din("w_in", [DM, FOP])
    w_out = din("w_out", [CPC, DM])
    convw = din("convw", [128, 4, 4])
    convb = din("convb", [128, 4])
    alog = din("alog", [128, 4])
    dvec = din("dvec", [128, 4])
    dtbias = din("dtbias", [128, 1])
    wr = din("wr", [DM, NE])
    brv = din("brv", [NE, 1])
    wup = din("wup", [NE, DM, HID])
    wgt = din("wgt", [NE, DM, HID])
    wdn = din("wdn", [NE, HID, DM])
    bupT = din("bupT", [128, 2, NE])
    bgtT = din("bgtT", [128, 2, NE])
    bdn = din("bdn", [NE, DM])
    eyec = din("eyec", [128, 128])
    hmk = din("hmk", [8, 4, 128])
    mones = din("mones", [128, 128])
    gmaskc = din("gmaskc", [4, 128, 128])
    bmask = din("bmask", [128, OC])
    sgnp = din("sgnp", [128, 7])
    iotag = din("iotag", [128, 128])
    selt = din("selt", [128, 4, 16])
    onescol = din("onescol", [128, 1])
    constv = din("constv", [128, 4])
    onesrow = din("onesrow", [1, 128])
    seltau = din("seltau", [64, 8])

    idram = lambda name, shape: nc.dram_tensor(name, shape, F32).ap()
    rs_in = idram("rs_in", [4, DM, TLOC])
    rs_out = idram("rs_out", [DM, TLOC])
    sc_in = idram("sc_in", [NE, TLOC])
    sc_out = idram("sc_out", [NCORE * NE, TLOC])
    gs_in = idram("gs_in", [8, 256])
    gs_out = idram("gs_out", [64, 256])
    tau_in = idram("tau_in", [8, 8])
    tau_out = idram("tau_out", [64, 8])
    x2_spill = idram("x2_spill", [DM, TLOC])
    sorstg = idram("sorstg", [16, OC])

    yout = nc.dram_tensor("yout", [DM, TLOC], F32, kind="ExternalOutput").ap()
    dbg_sort = nc.dram_tensor("dbg_sort", [128, 8, 128], F32, kind="ExternalOutput").ap()
    dbg_kth = nc.dram_tensor("dbg_kth", [64, 3], F32, kind="ExternalOutput").ap()
    dbg_tau = nc.dram_tensor("dbg_tau", [1, 3, 8], F32, kind="ExternalOutput").ap()
    dbg_gs = nc.dram_tensor("dbg_gs", [2, 1024], F32, kind="ExternalOutput").ap()
    dbg_hg = nc.dram_tensor("dbg_hg", [128, 8, 2, 1024], F32, kind="ExternalOutput").ap()
    dbg_ob = nc.dram_tensor("dbg_ob", [128, 8, 512], F32, kind="ExternalOutput").ap()
    tapt = None
    if tap is not None:
        tshape = dict(proj=[128, 4, OC], xconv=[128, 4, OC + 8], dt=[128, OC],
                      yg=[128, 4, OC], mixpart=[128, 8, OC], x2=[DM, TLOC],
                      scores=[NE, TLOC], sorted=[128, 8, 128], gs=[8, 256],
                      tau=[8, 8], fs=[NE, TLOC], ybd=[128, 4, OC],
                      cl4=[128, 4, OC], bl4=[128, 4, OC], cl0=[128, 4, OC],
                      bl0=[128, 4, OC], lam=[128, 4, 4], dsum=[8, 128],
                      daux=[8, 3, OC // 8, 8], rsout=[DM, TLOC], rsin=[4, DM, TLOC],
                      ffnout=[128, 8, TLOC])[tap]
        tapt = nc.dram_tensor("tap_" + tap, tshape, F32, kind="ExternalOutput").ap()

    st = contextlib.ExitStack()
    stB = contextlib.ExitStack()
    sbufA = lambda name, shape, dt=F32: st.enter_context(nc.sbuf_tensor(name, shape, dt))
    sbuf = lambda name, shape, dt=F32: stB.enter_context(nc.sbuf_tensor(name, shape, dt))
    psum = lambda name, shape, dt=F32: st.enter_context(nc.psum_tensor(name, shape, dt))

    with st:
        cwc = sbufA("cwc", [128, 4, 4])
        cbc = sbufA("cbc", [128, 4])
        acol = sbufA("acol", [128, 4])
        dcolc = sbufA("dcolc", [128, 4])
        dtbc = sbufA("dtbc", [128, 1])
        eyeb = sbufA("eyeb", [128, 128], BF16)
        eyef = sbufA("eyef", [128, 128])
        hmkc = sbufA("hmkc", [8, 4, 128])
        monesb = sbufA("monesb", [128, 128], BF16)
        gmb = sbufA("gmb", [128, 4, 128], BF16)
        bmkc = sbufA("bmkc", [128, OC], BF16)
        onc = sbufA("onc", [128, 1])
        oncb = sbufA("oncb", [128, 1], BF16)
        onrb = sbufA("onrb", [1, 128], BF16)
        cstv = sbufA("cstv", [128, 4])
        nc.const_aps.register(F32, EPS, cstv[:, 0:1])
        nc.const_aps.register(F32, 1.0, cstv[:, 1:2])
        nc.const_aps.register(F32, -1.0, cstv[:, 2:3])
        nc.const_aps.register(F32, 0.0, cstv[:, 3:4])
        winr = sbuf("winr", [128, 8, 17, 128], BF16)
        wotr = sbuf("wotr", [128, 4, 8, 128], BF16)
        xcf = sbuf("xcf", [128, 8, OC])
        xcb = sbuf("xcb", [128, 8, OC], BF16)
        sqb = sbuf("sqb", [128, 8, OC], BF16)
        rowf = sbuf("rowf", [1, 5, OC])
        rowb = sbuf("rowb", [1, 2, OC], BF16)
        xiw = sbuf("xiw", [128, 4, OC + 8], BF16)
        zbt = sbuf("zbt", [128, 4, OC], BF16)
        btt = sbuf("btt", [128, 4, OC], BF16)
        ctt = sbuf("ctt", [128, 4, OC], BF16)
        dtsb = sbuf("dtsb", [128, OC])
        dtaux = sbuf("dtaux", [8, 3, OC // 8, 8])
        dsum = sbuf("dsum", [8, 128])
        lamt = sbuf("lamt", [128, 4, 4])
        cfac = sbuf("cfac", [128, 4, 32], BF16)
        cfacB = sbuf("cfacB", [128, 4, 32], BF16)
        xch = sbuf("xch", [128, 4, OC + 8], BF16)
        bdt = sbuf("bdt", [128, 4, OC + 8], BF16)
        dmk = sbuf("dmk", [128, 4, OC + 8], BF16)
        ebd = sbuf("ebd", [128, 4, OC], BF16)
        wbd = sbuf("wbd", [128, 4, OC], BF16)
        gbs = sbuf("gbs", [128, OC], BF16)
        gtp = sbuf("gtp", [128, OC], BF16)
        ybd = sbuf("ybd", [128, 4, OC], BF16)
        cl = [sbuf(f"cl{m}", [128, 4, OC], BF16) for m in (8, 16, 32, 64, 128)]
        blv = [sbuf(f"bl{m}", [128, 4, OC], BF16) for m in (8, 16, 32, 64, 128)]
        xts = sbuf("xts", [128, 64], BF16)
        bts = sbuf("bts", [128, 64], BF16)
        gsbA = sbuf("gsbA", [128, 128], BF16)
        gsbB = sbuf("gsbB", [128, 128], BF16)
        hst = sbuf("hst", [128, 4, 64])
        hsb = sbuf("hsb", [128, 4, 64], BF16)
        ygt = sbuf("ygt", [128, 4, OC])
        rsst = sbuf("rsst", [128, 4, OC])
        xnb = rsst[:].bitcast(BF16).rearrange("p c t -> p (c t)").rearrange("p (c t) -> p c t", c=8)

        P0 = psum("P0", [128, 512])
        P1 = psum("P1", [128, 512])
        P2 = psum("P2", [128, 512])
        P3 = psum("P3", [128, 512])
        P4 = psum("P4", [128, 512])
        P5 = psum("P5", [128, 1024], BF16)
        P6 = psum("P6", [128, 512])
        P7 = psum("P7", [128, 512])

        sIN = Sem(st, nc, "sIN")
        sW = Sem(st, nc, "sW")
        sMM = Sem(st, nc, "sMM")
        sV = Sem(st, nc, "sV")
        sSC = Sem(st, nc, "sSC")
        sG = Sem(st, nc, "sG")
        sCC = Sem(st, nc, "sCC")
        sOUT = Sem(st, nc, "sOUT")
        sRS = Sem(st, nc, "sRS")

        blk = st.enter_context(nc.Block())

        def f(s):
            for o_, i_ in ((eyef, eyec), (hmkc, hmk), (cwc, convw), (cbc, convb),
                           (dcolc, dvec), (dtbc, dtbias), (onc, onescol), (acol, alog),
                           (cstv, constv)):
                sIN.inc(s.dma_start(out=o_[:], in_=i_[(slice(None),) * len(i_.shape)]), 16)
        blk.sync(f)

        def f(g):
            sW.inc(g.dma_start(out=winr[:], in_=w_in.rearrange("(c p) (t f) -> p c t f", p=128, f=128)), 16)
            sW.inc(g.dma_start(out=wotr[:], in_=w_out.rearrange("(c p) (t f) -> p c t f", p=128, f=128)), 16)
            sW.inc(g.dma_start(out=eyeb[:], in_=eyec[:, :]), 16)
            sW.inc(g.dma_start(out=monesb[:], in_=mones[:, :]), 16)
            sW.inc(g.dma_start(out=gmb[:], in_=gmaskc.rearrange("m p f -> p m f")), 16)
            sW.inc(g.dma_start(out=bmkc[:], in_=bmask[:, :]), 16)
            sW.inc(g.dma_start(out=oncb[:], in_=onescol[:, :]), 16)
            sW.inc(g.dma_start(out=onrb[:], in_=onesrow[:, :]), 16)
        blk.gpsimd(f)

        def f(s):
            s.wait_ge(sIN.s, sIN.n)
            s.activation(acol[:], acol[:], AF.Exp)
            sSC.inc(s.mul(acol[:], acol[:], -1.0))
        blk.scalar(f)

        def f(v):
            v.memset(hst[:], 0.0)
            sV.inc(v.memset(hsb[:], 0.0))
        blk.vector(f)

        scan_done_prev = sV.n
        rs_dma_prev = 0

        for oc in range(n_oc):
            t0 = oc * OC

            def f(s, t0=t0, oc=oc):
                if oc > 0:
                    s.wait_ge(sV.s, sV.n)
                sIN.inc(s.dma_start(
                    out=xcf[:], in_=xf.rearrange("(c p) t -> p c t", p=128)[:, :, t0:t0 + OC]), 16)
            blk.sync(f)
            ld32 = sIN.n

            def f(g, t0=t0, oc=oc):
                if oc > 0:
                    g.wait_ge(sSC.s, sSC.n)
                sW.inc(g.dma_start(
                    out=xcb[:], in_=xf.rearrange("(c p) t -> p c t", p=128)[:, :, t0:t0 + OC]), 16)
            blk.gpsimd(f)
            ldbf = sW.n

            def f(v, oc=oc, sdp=scan_done_prev):
                v.wait_ge(sG.s, sG.n)
                v.wait_ge(sV.s, sdp)
                if oc == 0:
                    v.memset(xiw[:, :, 0:8], 0.0)
                    v.memset(xch[:, :, 0:8], 0.0)
                    v.memset(bdt[:, :, 0:8], 0.0)
                    sV.inc(v.memset(dmk[:, :, 0:8], 0.0))
                else:
                    v.tensor_copy(xiw[:, :, 0:8], xiw[:, :, OC:OC + 8])
                    v.tensor_copy(xch[:, :, 0:8], xch[:, :, OC:OC + 8])
                    v.tensor_copy(bdt[:, :, 0:8], bdt[:, :, OC:OC + 8])
                    sV.inc(v.tensor_copy(dmk[:, :, 0:8], dmk[:, :, OC:OC + 8]))
            blk.vector(f)
            tails_done = sV.n

            def f(s, ldbf=ldbf):
                s.wait_ge(sW.s, ldbf)
                sSC.inc(s.activation(sqb[:], xcb[:], AF.Square))
            blk.scalar(f)
            sq_done = sSC.n

            def f(t, ldbf=ldbf, sq_done=sq_done):
                t.wait_ge(sW.s, ldbf)
                t.wait_ge(sSC.s, sq_done)
                t.wait_ge(sV.s, sV.n)
                for dc in range(8):
                    t.matmul(P1[0:1, 0:OC], oncb[:], xcb[:, dc, :], start=(dc == 0), stop=(dc == 7))
                for dc in range(8):
                    mi = t.matmul(P6[0:1, 0:OC], oncb[:], sqb[:, dc, :], start=(dc == 0), stop=(dc == 7))
                sMM.inc(mi)
            blk.tensor(f)
            stat_done = sMM.n

            def f(v, stat_done=stat_done):
                v.wait_ge(sMM.s, stat_done)
                v.tensor_scalar(rowf[:, 0, :], P1[0:1, 0:OC], 1.0 / DM, 0.0, Op.mult, Op.add)
                v.tensor_scalar(rowf[:, 1, :], P6[0:1, 0:OC], 1.0 / DM, 0.0, Op.mult, Op.add)
                v.tensor_tensor(rowf[:, 2, :], rowf[:, 0, :], rowf[:, 0, :], Op.mult)
                sV.inc(v.tensor_tensor(rowf[:, 2, :], rowf[:, 1, :], rowf[:, 2, :], Op.subtract))
            blk.vector(f)
            var_done = sV.n

            def f(s, var_done=var_done):
                s.wait_ge(sV.s, var_done)
                sSC.inc(s.activation(rowf[:, 3, :], rowf[:, 2, :], AF.Sqrt, bias=EPS))
            blk.scalar(f)
            std_done = sSC.n

            def f(v, std_done=std_done):
                v.wait_ge(sSC.s, std_done)
                v.reciprocal(rowf[:, 4, :], rowf[:, 3, :])
                v.tensor_copy(rowb[:, 0, :], rowf[:, 0, :])
                sV.inc(v.tensor_copy(rowb[:, 1, :], rowf[:, 4, :]))
            blk.vector(f)
            rows_done = sV.n

            def f(t, rows_done=rows_done):
                t.wait_ge(sV.s, rows_done)
                t.matmul(P1[:, 0:OC], onrb[:], rowb[:, 0, :], start=True, stop=True)
                sMM.inc(t.matmul(P6[:, 0:OC], onrb[:], rowb[:, 1, :], start=True, stop=True))
            blk.tensor(f)
            bc_done = sMM.n

            def f(v, bc_done=bc_done, ld32=ld32, rdp=rs_dma_prev):
                v.wait_ge(sMM.s, bc_done)
                v.wait_ge(sIN.s, ld32)
                v.wait_ge(sRS.s, rdp)
                mb = P1[:, 0:OC].unsqueeze(1).broadcast_to((128, 8, OC))
                rb = P6[:, 0:OC].unsqueeze(1).broadcast_to((128, 8, OC))
                v.tensor_tensor(xcf[:], xcf[:], mb, Op.subtract)
                sV.inc(v.tensor_tensor(xnb, xcf[:], rb, Op.mult))
            blk.vector(f)
            xn_done = sV.n

            evac_marks = {}
            for ft in range(17):
                def f(t, ft=ft, xn_done=xn_done, tails_done=tails_done):
                    if ft == 0:
                        t.wait_ge(sV.s, xn_done)
                        t.wait_ge(sV.s, tails_done)
                        t.wait_ge(sSC.s, sSC.n)
                    if ft >= 2:
                        t.wait_ge(sSC.s, evac_marks[ft - 2])
                    pb = P0 if ft % 2 == 0 else P2
                    for dc in range(8):
                        mi = t.matmul(pb[:, 0:OC], winr[:, dc, ft, :], xnb[:, dc:dc+1, :].opt(),
                                      start=(dc == 0), stop=(dc == 7))
                    sMM.inc(mi)
                blk.tensor(f)
                mm_mark = sMM.n

                def f(s, ft=ft, mm_mark=mm_mark):
                    s.wait_ge(sMM.s, mm_mark)
                    pb = P0 if ft % 2 == 0 else P2
                    if ft < 4:
                        out = xiw[:, ft, 8:8 + OC]
                    elif ft < 8:
                        out = zbt[:, ft - 4, :]
                    elif ft < 12:
                        out = btt[:, ft - 8, :]
                    elif ft < 16:
                        out = ctt[:, ft - 12, :]
                    else:
                        out = dtsb[0:128, :]
                    sSC.inc(s.copy(out, pb[:, 0:OC]))
                blk.scalar(f)
                evac_marks[ft] = sSC.n
            proj_done = sSC.n

            def f(v, proj_done=proj_done):
                v.wait_ge(sSC.s, proj_done)
                for b in range(4):
                    for k in range(4):
                        xv = xiw[:, b, 5 + k:5 + k + OC]
                        wcol = cwc[:, b:b + 1, k]
                        if k == 0:
                            mi = v.tensor_scalar(ygt[:, b, :], xv, wcol, 0.0, Op.mult, Op.add)
                        else:
                            mi = v.scalar_tensor_tensor(ygt[:, b, :], xv, wcol, ygt[:, b, :],
                                                        Op.mult, Op.add)
                sV.inc(mi)
            blk.vector(f)
            conv_done = sV.n

            def f(s, conv_done=conv_done):
                s.wait_ge(sV.s, conv_done)
                for b in range(4):
                    mi = s.activation(xch[:, b, 8:8 + OC], ygt[:, b, :], AF.Silu, bias=cbc[:, b:b + 1])
                sSC.inc(mi)
            blk.scalar(f)
            xconv_done = sSC.n

            def f(s, proj_done=proj_done):
                s.wait_ge(sSC.s, proj_done)
                s.activation(dtaux[:, 2, :, :].rearrange('h b e -> h (b e)'), dtsb[0:8, :], AF.Exp, bias=dtbc[0:8, :])
                sSC.inc(s.activation(dtsb[0:8, :], dtaux[:, 2, :, :].rearrange('h b e -> h (b e)'), AF.Ln, bias=1.0))
            blk.scalar(f)
            dt_done = sSC.n

            def f(v, dt_done=dt_done):
                v.wait_ge(sSC.s, dt_done)
                dt8 = dtsb[0:8, :].rearrange("h (b e) -> h b e", e=8)
                inc = dtaux[:, 0, :, :]
                rex = dtaux[:, 1, :, :]
                tmp = dtaux[:, 2, :, :]
                v.tensor_copy(inc[:], dt8)
                v.tensor_tensor(tmp[:, :, 1:8], inc[:, :, 0:7], inc[:, :, 1:8], Op.add)
                v.tensor_copy(tmp[:, :, 0:1], inc[:, :, 0:1])
                v.tensor_tensor(inc[:, :, 2:8], tmp[:, :, 0:6], tmp[:, :, 2:8], Op.add)
                v.tensor_copy(inc[:, :, 0:2], tmp[:, :, 0:2])
                v.tensor_tensor(rex[:, :, 4:8], inc[:, :, 0:4], inc[:, :, 4:8], Op.add)
                v.tensor_copy(rex[:, :, 0:4], inc[:, :, 0:4])
                v.tensor_copy(inc[:], rex[:])
                v.tensor_tensor(rex[:], inc[:, :, 7:8].broadcast_to((8, OC // 8, 8)), inc[:], Op.subtract)
                v.tensor_copy(dsum[:, 0:64], inc[:, :, 7])
                v.tensor_reduce(dsum[:, 64:96], dtsb[0:8, :].rearrange("h (b e) -> h b e", e=16),
                                mybir.AxisListType.X, Op.add)
                v.tensor_reduce(dsum[:, 96:112], dtsb[0:8, :].rearrange("h (b e) -> h b e", e=32),
                                mybir.AxisListType.X, Op.add)
                v.tensor_reduce(dsum[:, 112:120], dtsb[0:8, :].rearrange("h (b e) -> h b e", e=64),
                                mybir.AxisListType.X, Op.add)
                sV.inc(v.tensor_reduce(dsum[:, 120:124],
                                       dtsb[0:8, :].rearrange("h (b e) -> h b e", e=128),
                                       mybir.AxisListType.X, Op.add))
            blk.vector(f)
            aux_done = sV.n

            base_marks = {}
            for b in range(4):
                def f(t, b=b, aux_done=aux_done, tails_done=tails_done):
                    if b == 0:
                        t.wait_ge(sV.s, aux_done)
                        t.wait_ge(sSC.s, sSC.n)
                    if b >= 2:
                        t.wait_ge(sSC.s, base_marks[b - 2])
                    pd, pi, pr = (P4, P0, P1) if b % 2 == 0 else (P6, P2, P7)
                    t.matmul(pd[:, 0:OC], hmkc[:, b, :], dtsb[0:8, :], start=True, stop=True)
                    t.matmul(pi[:, 0:OC], hmkc[:, b, :], dtaux[:, 0, :, :].rearrange("h b e -> h (b e)"), start=True, stop=True)
                    sMM.inc(t.matmul(pr[:, 0:OC], hmkc[:, b, :], dtaux[:, 1, :, :].rearrange("h b e -> h (b e)"),
                                     start=True, stop=True))
                blk.tensor(f)
                bmm = sMM.n

                def f(s, b=b, bmm=bmm, tails_done=tails_done):
                    s.wait_ge(sMM.s, bmm)
                    pd, pi, pr = (P4, P0, P1) if b % 2 == 0 else (P6, P2, P7)
                    s.activation(dmk[:, b, 8:8 + OC], pd[:, 0:OC], AF.Exp, scale=acol[:, b:b + 1])
                    s.activation(cl[0][:, b, :], pi[:, 0:OC], AF.Exp, scale=acol[:, b:b + 1])
                    s.activation(blv[0][:, b, :], pr[:, 0:OC], AF.Exp, scale=acol[:, b:b + 1])
                    sSC.inc(s.copy(bdt[:, b, 8:8 + OC], pd[:, 0:OC]))
                blk.scalar(f)
                base_marks[b] = sSC.n
            base_done = sSC.n

            def f(v, base_done=base_done):
                v.wait_ge(sSC.s, base_done)
                v.tensor_tensor(bdt[:, :, 8:8 + OC], bdt[:, :, 8:8 + OC], btt[:], Op.mult)
                v.tensor_tensor(dmk[:, :, 8:8 + OC], dmk[:, :, 8:8 + OC],
                                bmkc[:].unsqueeze(1).broadcast_to((128, 4, OC)), Op.mult)
                v.tensor_tensor(cl[0][:], cl[0][:], ctt[:], Op.mult)
                sV.inc(v.tensor_tensor(blv[0][:], blv[0][:], bdt[:, :, 8:8 + OC], Op.mult))
            blk.vector(f)
            lvl_prev = sV.n

            lvl_cfg = [(0, 64, 8), (64, 32, 16), (96, 16, 32), (112, 8, 64)]
            for li, (off, nb, half) in enumerate(lvl_cfg):
                nb2 = nb // 2

                def f(t, off=off, nb=nb, nb2=nb2, lvl_prev=lvl_prev):
                    t.wait_ge(sV.s, lvl_prev)
                    t.wait_ge(sSC.s, sSC.n)
                    dsv = dsum[:, off:off + nb].rearrange("h (b two) -> h b two", two=2)
                    for b in range(4):
                        t.matmul(P4[:, b * 32:b * 32 + nb2], hmkc[:, b, :], dsv[:, :, 0],
                                 start=(b == 0), stop=(b == 3))
                        mi = t.matmul(P6[:, b * 32:b * 32 + nb2], hmkc[:, b, :], dsv[:, :, 1],
                                      start=(b == 0), stop=(b == 3))
                    sMM.inc(mi)
                blk.tensor(f)
                lmm = sMM.n

                def f(s, nb2=nb2, lmm=lmm):
                    s.wait_ge(sMM.s, lmm)
                    for b in range(4):
                        s.activation(cfac[:, b, 0:nb2], P4[:, b * 32:b * 32 + nb2],
                                     AF.Exp, scale=acol[:, b:b + 1])
                        mi = s.activation(cfacB[:, b, 0:nb2], P6[:, b * 32:b * 32 + nb2],
                                          AF.Exp, scale=acol[:, b:b + 1])
                    sSC.inc(mi)
                blk.scalar(f)
                lact = sSC.n

                def f(v, li=li, nb2=nb2, half=half, lact=lact):
                    v.wait_ge(sSC.s, lact)
                    v.tensor_copy(cl[li + 1][:], cl[li][:])
                    v.tensor_copy(blv[li + 1][:], blv[li][:])
                    cv = cl[li + 1][:].rearrange("p b (n w) -> p b n w", n=nb2)
                    bv = blv[li + 1][:].rearrange("p b (n w) -> p b n w", n=nb2)
                    for b in range(4):
                        cf = cfac[:, b, 0:nb2].unsqueeze(2)
                        bf = cfacB[:, b, 0:nb2].unsqueeze(2)
                        v.tensor_tensor(cv[:, b, :, half:], cv[:, b, :, half:],
                                        cf.broadcast_to((128, nb2, half)), Op.mult)
                        mi = v.tensor_tensor(bv[:, b, :, 0:half], bv[:, b, :, 0:half],
                                             bf.broadcast_to((128, nb2, half)), Op.mult)
                    sV.inc(mi)
                blk.vector(f)
                lvl_prev = sV.n
            chains_done = lvl_prev

            def f(t, chains_done=chains_done):
                t.wait_ge(sV.s, chains_done)
                t.wait_ge(sSC.s, sSC.n)
                for b in range(4):
                    mi = t.matmul(P4[:, b * 4:b * 4 + 4], hmkc[:, b, :], dsum[:, 120:124],
                                  start=(b == 0), stop=(b == 3))
                sMM.inc(mi)
            blk.tensor(f)
            lamm = sMM.n

            def f(s, lamm=lamm):
                s.wait_ge(sMM.s, lamm)
                for b in range(4):
                    mi = s.activation(lamt[:, b, :], P4[:, b * 4:b * 4 + 4], AF.Exp,
                                      scale=acol[:, b:b + 1])
                sSC.inc(mi)
            blk.scalar(f)
            lam_done = sSC.n

            band_mm_prev = sMM.n
            gb_mark = sSC.n
            gp_read_mark = sG.n
            for k in range(8):
                def f(v, k=k, chains_done=chains_done, xconv_done=xconv_done,
                      band_mm_prev=band_mm_prev):
                    if k == 0:
                        v.wait_ge(sV.s, chains_done)
                        v.wait_ge(sSC.s, xconv_done)
                    else:
                        v.wait_ge(sMM.s, band_mm_prev)
                    if k == 0:
                        mi = v.tensor_tensor(wbd[:], ctt[:], bdt[:, :, 8:8 + OC], Op.mult)
                    elif k == 1:
                        v.tensor_copy(ebd[:], dmk[:, :, 8:8 + OC])
                        v.tensor_tensor(wbd[:], ctt[:], ebd[:], Op.mult)
                        mi = v.tensor_tensor(wbd[:], wbd[:], bdt[:, :, 7:7 + OC], Op.mult)
                    else:
                        v.tensor_tensor(ebd[:], ebd[:], dmk[:, :, 9 - k:9 - k + OC], Op.mult)
                        v.tensor_tensor(wbd[:], ctt[:], ebd[:], Op.mult)
                        mi = v.tensor_tensor(wbd[:], wbd[:], bdt[:, :, 8 - k:8 - k + OC], Op.mult)
                    sV.inc(mi)
                blk.vector(f)
                wmark = sV.n

                for b in range(4):
                    def f(t, k=k, b=b, wmark=wmark, gb_mark=gb_mark):
                        t.wait_ge(sV.s, wmark)
                        if not (k == 0 and b == 0):
                            t.wait_ge(sSC.s, gb_mark)
                        sMM.inc(t.matmul(P7[:, 0:OC], monesb[:], wbd[:, b, :], start=True, stop=True))
                    blk.tensor(f)
                    gmmk = sMM.n

                    def f(s, gmmk=gmmk, gp_read_mark=gp_read_mark):
                        s.wait_ge(sMM.s, gmmk)
                        s.wait_ge(sG.s, gp_read_mark)
                        sSC.inc(s.copy(gbs[:], P7[:, 0:OC]))
                    blk.scalar(f)
                    gb_mark = sSC.n

                    def f(g, k=k, b=b, gb_mark=gb_mark, sdp=scan_done_prev):
                        g.wait_ge(sSC.s, gb_mark)
                        if k == 0 and b == 0:
                            g.wait_ge(sV.s, sdp)
                        sG.inc(g.tensor_tensor(gtp[:], gbs[:], xch[:, b, 8 - k:8 - k + OC], Op.mult))
                        if k == 0:
                            sG.inc(g.tensor_copy(ybd[:, b, :], gtp[:]))
                        else:
                            sG.inc(g.tensor_tensor(ybd[:, b, :], ybd[:, b, :], gtp[:], Op.add))
                    blk.gpsimd(f)
                    gp_read_mark = sG.n - 1
                band_mm_prev = sMM.n
            bands_done = sG.n

            for qc in range(OC // QC):
                q0 = qc * QC
                for h in range(HPC):
                    b = h // 2
                    r0 = 64 * (h % 2)

                    def f(t, b=b, r0=r0, q0=q0, qc=qc, h=h, chains_done=chains_done,
                          lam_done=lam_done, xconv_done=xconv_done):
                        if qc == 0 and h == 0:
                            t.wait_ge(sV.s, chains_done)
                            t.wait_ge(sSC.s, lam_done)
                            t.wait_ge(sSC.s, xconv_done)
                        t.wait_ge(sV.s, sV.n)
                        t.transpose(P5[:, 0:64], xch[r0:r0 + 64, b, 8 + q0:8 + q0 + QC],
                                    eyeb[r0:r0 + 64, r0:r0 + 64])
                        sMM.inc(t.transpose(P5[:, 64:128], blv[4][r0:r0 + 64, b, q0:q0 + QC],
                                            eyeb[r0:r0 + 64, r0:r0 + 64]))
                    blk.tensor(f)
                    tr_mark = sMM.n

                    def f(v, tr_mark=tr_mark):
                        v.wait_ge(sMM.s, tr_mark)
                        v.tensor_copy(xts[:], P5[:, 0:64])
                        sV.inc(v.tensor_copy(bts[:], P5[:, 64:128]))
                    blk.vector(f)
                    trc_mark = sV.n

                    def f(t, b=b, r0=r0, q0=q0, trc_mark=trc_mark):
                        t.wait_ge(sV.s, trc_mark)
                        t.matmul(P4[r0:r0 + 64, 0:QC], hsb[r0:r0 + 64, b, :],
                                 cl[4][r0:r0 + 64, b, q0:q0 + QC], start=True, stop=False)
                        t.matmul(P0[:, 0:QC], blv[0][r0:r0 + 64, b, q0:q0 + QC],
                                 cl[0][r0:r0 + 64, b, q0:q0 + QC], start=True, stop=True)
                        sMM.inc(t.matmul(P2[:, 0:QC], blv[1][r0:r0 + 64, b, q0:q0 + QC],
                                         cl[1][r0:r0 + 64, b, q0:q0 + QC], start=True, stop=True))
                    blk.tensor(f)
                    g01 = sMM.n

                    def f(v, g01=g01):
                        v.wait_ge(sMM.s, g01)
                        v.tensor_tensor(gsbA[:], P0[:, 0:QC], gmb[:, 0, :], Op.mult)
                        sV.inc(v.tensor_tensor(gsbB[:], P2[:, 0:QC], gmb[:, 1, :], Op.mult))
                    blk.vector(f)
                    ev01 = sV.n

                    def f(t, b=b, r0=r0, q0=q0, ev01=ev01):
                        t.wait_ge(sV.s, ev01)
                        t.matmul(P4[r0:r0 + 64, 0:QC], xts[:], gsbA[:], start=False, stop=False)
                        t.matmul(P4[r0:r0 + 64, 0:QC], xts[:], gsbB[:], start=False, stop=False)
                        t.matmul(P0[:, 0:QC], blv[2][r0:r0 + 64, b, q0:q0 + QC],
                                 cl[2][r0:r0 + 64, b, q0:q0 + QC], start=True, stop=True)
                        sMM.inc(t.matmul(P2[:, 0:QC], blv[3][r0:r0 + 64, b, q0:q0 + QC],
                                         cl[3][r0:r0 + 64, b, q0:q0 + QC], start=True, stop=True))
                    blk.tensor(f)
                    g23 = sMM.n

                    def f(v, g23=g23):
                        v.wait_ge(sMM.s, g23)
                        v.tensor_tensor(gsbA[:], P0[:, 0:QC], gmb[:, 2, :], Op.mult)
                        sV.inc(v.tensor_tensor(gsbB[:], P2[:, 0:QC], gmb[:, 3, :], Op.mult))
                    blk.vector(f)
                    ev23 = sV.n

                    def f(t, b=b, r0=r0, ev23=ev23):
                        t.wait_ge(sV.s, ev23)
                        t.matmul(P4[r0:r0 + 64, 0:QC], xts[:], gsbA[:], start=False, stop=False)
                        t.matmul(P4[r0:r0 + 64, 0:QC], xts[:], gsbB[:], start=False, stop=True)
                        sMM.inc(t.matmul(P6[r0:r0 + 64, 0:64], bts[:], xts[:], start=True, stop=True))
                    blk.tensor(f)
                    smm = sMM.n

                    def f(v, b=b, r0=r0, q0=q0, qc=qc, smm=smm, bands_done=bands_done):
                        v.wait_ge(sMM.s, smm)
                        v.wait_ge(sG.s, bands_done)
                        v.scalar_tensor_tensor(ygt[r0:r0 + 64, b, q0:q0 + QC],
                                               xch[r0:r0 + 64, b, 8 + q0:8 + q0 + QC],
                                               dcolc[r0:r0 + 64, b:b + 1], P4[r0:r0 + 64, 0:QC],
                                               Op.mult, Op.add)
                        v.tensor_tensor(ygt[r0:r0 + 64, b, q0:q0 + QC],
                                        ygt[r0:r0 + 64, b, q0:q0 + QC],
                                        ybd[r0:r0 + 64, b, q0:q0 + QC], Op.add)
                        v.scalar_tensor_tensor(hst[r0:r0 + 64, b, :], hst[r0:r0 + 64, b, :],
                                               lamt[r0:r0 + 64, b, qc:qc + 1], P6[r0:r0 + 64, 0:64],
                                               Op.mult, Op.add)
                        sV.inc(v.tensor_copy(hsb[r0:r0 + 64, b, :], hst[r0:r0 + 64, b, :]))
                    blk.vector(f)
            scan_done = sV.n
            scan_done_prev = scan_done

            def f(s):
                s.wait_ge(sSC.s, sSC.n)
                sSC.inc(s.activation(zbt[:], zbt[:], AF.Silu))
            blk.scalar(f)
            zs_done = sSC.n

            def f(v, zs_done=zs_done, scan_done=scan_done):
                v.wait_ge(sSC.s, zs_done)
                v.wait_ge(sV.s, scan_done)
                sV.inc(v.tensor_tensor(sqb[:, 0:4, :], ygt[:], zbt[:], Op.mult))
            blk.vector(f)
            yg_done = sV.n

            wo_marks = {}
            half_marks = {}
            for dt_ in range(8):
                def f(t, dt_=dt_, yg_done=yg_done, rdp=rs_dma_prev):
                    if dt_ == 0:
                        t.wait_ge(sV.s, yg_done)
                        t.wait_ge(sSC.s, sSC.n)
                        t.wait_ge(sRS.s, rdp)
                    if dt_ >= 2:
                        t.wait_ge(sSC.s, wo_marks[dt_ - 2])
                    pb = P0 if dt_ % 2 == 0 else P2
                    for cb in range(4):
                        mi = t.matmul(pb[:, 0:OC], wotr[:, cb, dt_, :], sqb[:, cb, :],
                                      start=(cb == 0), stop=(cb == 3))
                    sMM.inc(mi)
                blk.tensor(f)
                womm = sMM.n

                def f(s, dt_=dt_, womm=womm, oc=oc):
                    s.wait_ge(sMM.s, womm)
                    if dt_ >= 4:
                        s.wait_ge(sRS.s, half_marks[0])
                    pb = P0 if dt_ % 2 == 0 else P2
                    sSC.inc(s.copy(rsst[:, dt_ % 4, :], pb[:, 0:OC]))
                blk.scalar(f)
                wo_marks[dt_] = sSC.n
                if dt_ in (3, 7):
                    def f(s, oc=oc, dt_=dt_, wo_done=sSC.n):
                        s.wait_ge(sSC.s, wo_done)
                        shard, col = oc // 4, (oc % 4) * OC
                        d0 = 0 if dt_ == 3 else 4
                        sRS.inc(s.dma_start(
                            out=rs_in.rearrange("s (c p) t -> s p c t", p=128)[shard, :, d0:d0 + 4, col:col + OC],
                            in_=rsst[:]), 16)
                    blk.sync(f)
                    half_marks[0 if dt_ == 3 else 1] = sRS.n
            rs_dma_prev = sRS.n

            if tap in ("proj", "xconv", "dt", "yg", "mixpart", "ybd", "cl4", "bl4",
                       "cl0", "bl0", "lam", "dsum", "daux") and oc == 0:
                def f(g, tap=tap):
                    g.wait_ge(sSC.s, sSC.n)
                    g.wait_ge(sV.s, sV.n)
                    g.wait_ge(sMM.s, sMM.n)
                    srcv = dict(proj=btt[:], xconv=xch[:], dt=dtsb[:], yg=ygt[:],
                                mixpart=rsst[:], ybd=ybd[:], cl4=cl[4][:],
                                bl4=blv[4][:], cl0=cl[0][:], bl0=blv[0][:],
                                lam=lamt[:], dsum=dsum[:], daux=dtaux[:])[tap]
                    sW.inc(g.dma_start(out=tapt, in_=srcv), 16)
                blk.gpsimd(f)

        def f(g):
            g.wait_ge(sRS.s, sRS.n)
            sCC.inc(g.collective_compute(
                "ReduceScatter", Op.add,
                replica_groups=[[0, 1, 2, 3], [4, 5, 6, 7]],
                ins=[rs_in.opt()], outs=[rs_out.opt()]))
        blk.gpsimd(f)
        rs_done = sCC.n

        if tap in ("rsout", "rsin"):
            def f(g, tap=tap):
                g.wait_ge(sCC.s, sCC.n)
                srcv = rs_out[:, :] if tap == "rsout" else rs_in[:, :, :]
                sW.inc(g.dma_start(out=tapt, in_=srcv), 16)
            blk.gpsimd(f)

        for pn in range(4):
            c0 = pn * OC

            def f(s, pn=pn, c0=c0, rs_done=rs_done, outp=sOUT.n):
                if pn == 0:
                    s.wait_ge(sCC.s, rs_done)
                else:
                    s.wait_ge(sV.s, sV.n)
                    s.wait_ge(sOUT.s, outp)
                sIN.inc(s.dma_start(out=xcf[:], in_=rs_out.rearrange("(c p) t -> p c t", p=128)[:, :, c0:c0 + OC]), 16)
                sIN.inc(s.dma_start(out=ygt[:],
                                    in_=xres.rearrange("(c p) t -> p c t", p=128)[:, 0:4, c0:c0 + OC]), 16)
            blk.sync(f)
            ld2 = sIN.n

            def f(v, ld2=ld2):
                v.wait_ge(sIN.s, ld2)
                sV.inc(v.tensor_copy(xcb[:], xcf[:]))
            blk.vector(f)
            cb2 = sV.n

            def f(s, cb2=cb2):
                s.wait_ge(sV.s, cb2)
                sSC.inc(s.activation(sqb[:], xcb[:], AF.Square))
            blk.scalar(f)
            sq2 = sSC.n

            def f(t, cb2=cb2, sq2=sq2):
                t.wait_ge(sV.s, cb2)
                t.wait_ge(sSC.s, sq2)
                for dc in range(8):
                    t.matmul(P1[0:1, 0:OC], oncb[:], xcb[:, dc, :], start=(dc == 0), stop=(dc == 7))
                for dc in range(8):
                    mi = t.matmul(P6[0:1, 0:OC], oncb[:], sqb[:, dc, :], start=(dc == 0), stop=(dc == 7))
                sMM.inc(mi)
            blk.tensor(f)
            st2 = sMM.n

            def f(v, st2=st2):
                v.wait_ge(sMM.s, st2)
                v.tensor_scalar(rowf[:, 0, :], P1[0:1, 0:OC], 1.0 / DM, 0.0, Op.mult, Op.add)
                v.tensor_scalar(rowf[:, 1, :], P6[0:1, 0:OC], 1.0 / DM, 0.0, Op.mult, Op.add)
                v.tensor_tensor(rowf[:, 2, :], rowf[:, 0, :], rowf[:, 0, :], Op.mult)
                sV.inc(v.tensor_tensor(rowf[:, 2, :], rowf[:, 1, :], rowf[:, 2, :], Op.subtract))
            blk.vector(f)
            var2 = sV.n

            def f(s, var2=var2):
                s.wait_ge(sV.s, var2)
                sSC.inc(s.activation(rowf[:, 3, :], rowf[:, 2, :], AF.Sqrt, bias=EPS))
            blk.scalar(f)
            std2 = sSC.n

            def f(v, std2=std2):
                v.wait_ge(sSC.s, std2)
                v.reciprocal(rowf[:, 4, :], rowf[:, 3, :])
                v.tensor_copy(rowb[:, 0, :], rowf[:, 0, :])
                sV.inc(v.tensor_copy(rowb[:, 1, :], rowf[:, 4, :]))
            blk.vector(f)
            rw2 = sV.n

            def f(t, rw2=rw2):
                t.wait_ge(sV.s, rw2)
                t.matmul(P1[:, 0:OC], onrb[:], rowb[:, 0, :], start=True, stop=True)
                sMM.inc(t.matmul(P6[:, 0:OC], onrb[:], rowb[:, 1, :], start=True, stop=True))
            blk.tensor(f)
            bc2 = sMM.n

            def f(v, bc2=bc2):
                v.wait_ge(sMM.s, bc2)
                mb = P1[:, 0:OC].unsqueeze(1).broadcast_to((128, 8, OC))
                rb = P6[:, 0:OC].unsqueeze(1).broadcast_to((128, 8, OC))
                v.tensor_tensor(xcf[:], xcf[:], mb, Op.subtract)
                v.tensor_tensor(xcf[:], xcf[:], rb, Op.mult)
                sV.inc(v.tensor_tensor(xcf[:, 0:4, :], xcf[:, 0:4, :], ygt[:], Op.add))
            blk.vector(f)
            h1 = sV.n

            def f(s, c0=c0, h1=h1):
                s.wait_ge(sV.s, h1)
                sIN.inc(s.dma_start(out=ygt[:],
                                    in_=xres.rearrange("(c p) t -> p c t", p=128)[:, 4:8, c0:c0 + OC]), 16)
            blk.sync(f)
            ld3 = sIN.n

            def f(v, ld3=ld3):
                v.wait_ge(sIN.s, ld3)
                sV.inc(v.tensor_tensor(xcf[:, 4:8, :], xcf[:, 4:8, :], ygt[:], Op.add))
            blk.vector(f)
            x2done = sV.n

            def f(s, c0=c0, x2done=x2done):
                s.wait_ge(sV.s, x2done)
                sOUT.inc(s.dma_start(
                    out=x2_spill.rearrange("(c p) t -> p c t", p=128)[:, :, c0:c0 + OC],
                    in_=xcf[:]), 16)
            blk.scalar(f)

        stB.close()

        stC = contextlib.ExitStack()
        sbufC = lambda name, shape, dt=F32: stC.enter_context(nc.sbuf_tensor(name, shape, dt))
        x2b = sbufC("x2b", [128, 8, TLOC], BF16)
        outb = sbufC("outb", [128, 8, TLOC], BF16)
        fsem = sbufC("fsem", [NE, TLOC])
        fseb = sbufC("fseb", [NE, TLOC], BF16)
        stmt = sbufC("stmt", [128, 16, NE])
        tausb = sbufC("tausb", [1, NE])
        onrf = sbufC("onrf", [1, 128])

        stD = contextlib.ExitStack()
        sbufD = lambda name, shape, dt=F32: stD.enter_context(nc.sbuf_tensor(name, shape, dt))
        x2p = sbufD("x2p", [128, 8, OC])
        sqp = sbufD("sqp", [128, 8, OC], BF16)
        x2pb = sbufD("x2pb", [128, 8, OC], BF16)
        rwf = sbufD("rwf", [1, 5, OC])
        rwb = sbufD("rwb", [1, 2, OC], BF16)
        wrb = sbufD("wrb", [128, 8, NE])
        brc = sbufD("brc", [NE, 1])
        lgt = sbufD("lgt", [NE, TLOC])
        ltm = sbufD("ltm", [128, 16, NE])
        lrow = sbufD("lrow", [128, 16, 2])
        semj = sbufD("semj", [NE, TLOC])
        gath = sbufD("gath", [128, 4, TLOC])
        selb = sbufD("selb", [128, 4, 16])
        stg = sbufD("stg", [16, OC])
        sorc = sbufD("sorc", [128, 8, 128])
        sortmp = sbufD("sortmp", [128, 512])
        sgnc = sbufD("sgnc", [128, 7])
        gssb = sbufD("gssb", [2, 1024])
        hmsk = sbufD("hmsk", [128, 2])
        gso = sbufD("gso", [64, 256])
        sor2 = sbufD("sor2", [128, 128])
        sor2t = sbufD("sor2t", [128, 64])
        iotc = sbufD("iotc", [128, 128])
        seltc = sbufD("seltc", [64, 8])
        kth = sbufD("kth", [64, 3])
        eqt = sbufD("eqt", [128, 128])
        rb128 = sbufD("rb128", [128, 8])
        tfix = sbufD("tfix", [1, 3, 8])

        for pn in range(4):
            c0 = pn * OC

            def f(s, pn=pn, c0=c0):
                if pn == 0:
                    s.wait_ge(sOUT.s, sOUT.n)
                    sIN.inc(s.dma_start(out=sgnc[:], in_=sgnp[:, :]), 16)
                    sIN.inc(s.dma_start(out=iotc[:], in_=iotag[:, :]), 16)
                    sIN.inc(s.dma_start(out=selb[:], in_=selt[:, :, :]), 16)
                    sIN.inc(s.dma_start(out=brc[:], in_=brv[:, :]), 16)
                    sIN.inc(s.dma_start(out=onrf[:], in_=onesrow[:, :]), 16)
                else:
                    s.wait_ge(sV.s, sV.n)
                sIN.inc(s.dma_start(out=x2p[:], in_=x2_spill.rearrange("(c p) t -> p c t", p=128)[:, :, c0:c0 + OC]), 16)
            blk.sync(f)
            ld = sIN.n

            def f(g, pn=pn, outp=sOUT.n):
                if pn == 0:
                    g.wait_ge(sOUT.s, outp)
                    sW.inc(g.dma_start(out=wrb[:], in_=wr.rearrange("(c p) e -> p c e", p=128)), 16)
            blk.gpsimd(f)
            wrld = sW.n

            def f(v, ld=ld):
                v.wait_ge(sIN.s, ld)
                sV.inc(v.tensor_copy(x2pb[:], x2p[:]))
            blk.vector(f)
            cb = sV.n

            def f(s, cb=cb):
                s.wait_ge(sV.s, cb)
                sSC.inc(s.activation(sqp[:], x2pb[:], AF.Square))
            blk.scalar(f)
            sq = sSC.n

            def f(t, cb=cb, sq=sq, pn=pn):
                t.wait_ge(sV.s, cb)
                t.wait_ge(sSC.s, sq)
                if pn > 0:
                    t.wait_ge(sV.s, sV.n)
                for dc in range(8):
                    t.matmul(P1[0:1, 0:OC], oncb[:], x2pb[:, dc, :], start=(dc == 0), stop=(dc == 7))
                for dc in range(8):
                    mi = t.matmul(P6[0:1, 0:OC], oncb[:], sqp[:, dc, :], start=(dc == 0), stop=(dc == 7))
                sMM.inc(mi)
            blk.tensor(f)
            stt = sMM.n

            def f(v, stt=stt):
                v.wait_ge(sMM.s, stt)
                v.tensor_scalar(rwf[:, 0, :], P1[0:1, 0:OC], 1.0 / DM, 0.0, Op.mult, Op.add)
                v.tensor_scalar(rwf[:, 1, :], P6[0:1, 0:OC], 1.0 / DM, 0.0, Op.mult, Op.add)
                v.tensor_tensor(rwf[:, 2, :], rwf[:, 0, :], rwf[:, 0, :], Op.mult)
                sV.inc(v.tensor_tensor(rwf[:, 2, :], rwf[:, 1, :], rwf[:, 2, :], Op.subtract))
            blk.vector(f)
            vr = sV.n

            def f(s, vr=vr):
                s.wait_ge(sV.s, vr)
                sSC.inc(s.activation(rwf[:, 3, :], rwf[:, 2, :], AF.Sqrt, bias=EPS))
            blk.scalar(f)
            sd = sSC.n

            def f(v, sd=sd):
                v.wait_ge(sSC.s, sd)
                v.reciprocal(rwf[:, 4, :], rwf[:, 3, :])
                v.tensor_copy(rwb[:, 0, :], rwf[:, 0, :])
                sV.inc(v.tensor_copy(rwb[:, 1, :], rwf[:, 4, :]))
            blk.vector(f)
            rw = sV.n

            def f(t, rw=rw):
                t.wait_ge(sV.s, rw)
                t.matmul(P1[:, 0:OC], onrb[:], rwb[:, 0, :], start=True, stop=True)
                sMM.inc(t.matmul(P6[:, 0:OC], onrb[:], rwb[:, 1, :], start=True, stop=True))
            blk.tensor(f)
            bc = sMM.n

            def f(v, bc=bc, pn=pn, c0=c0):
                v.wait_ge(sMM.s, bc)
                mb = P1[:, 0:OC].unsqueeze(1).broadcast_to((128, 8, OC))
                rb = P6[:, 0:OC].unsqueeze(1).broadcast_to((128, 8, OC))
                v.tensor_tensor(x2p[:], x2p[:], mb, Op.subtract)
                v.tensor_tensor(x2p[:], x2p[:], rb, Op.mult)
                sV.inc(v.tensor_copy(x2b[:, :, c0:c0 + OC], x2p[:]))
            blk.vector(f)
            xb_done = sV.n

            def f(t, c0=c0, xb_done=xb_done, wrld=wrld):
                t.wait_ge(sV.s, xb_done)
                t.wait_ge(sW.s, wrld)
                t.wait_ge(sSC.s, sSC.n)
                t.wait_ge(sV.s, sV.n)
                for dc in range(8):
                    mi = t.matmul(P0[0:64, 0:OC], wrb[:, dc, :], x2p[:, dc, :],
                                  start=(dc == 0), stop=(dc == 7))
                sMM.inc(mi)
            blk.tensor(f)
            lmm = sMM.n

            def f(v, c0=c0, lmm=lmm):
                v.wait_ge(sMM.s, lmm)
                sV.inc(v.tensor_scalar(lgt[0:64, c0:c0 + OC], P0[0:64, 0:OC], brc[:, 0:1], 0.0,
                                       Op.add, Op.add))
            blk.vector(f)
        logits_done = sV.n

        for tc in range(16):
            def f(t, tc=tc, logits_done=logits_done):
                if tc == 0:
                    t.wait_ge(sV.s, logits_done)
                else:
                    t.wait_ge(sV.s, sV.n)
                sMM.inc(t.transpose(P1[:, 0:64], lgt[0:64, tc * 128:tc * 128 + 128],
                                    eyef[0:64, 0:64]))
            blk.tensor(f)
            trm = sMM.n

            def f(v, tc=tc, trm=trm):
                v.wait_ge(sMM.s, trm)
                v.tensor_copy(ltm[:, tc, :], P1[:, 0:64])
                sV.inc(v.tensor_reduce(lrow[:, tc, 0:1], ltm[:, tc, :], mybir.AxisListType.X, Op.max))
            blk.vector(f)

            def f(v, tc=tc, m=sV.n):
                v.wait_ge(sV.s, m)
                sV.inc(v.tensor_scalar(lrow[:, tc, 1:2], lrow[:, tc, 0:1], -1.0, 0.0, Op.mult, Op.add))
            blk.vector(f)
            mxd = sV.n

            def f(s, tc=tc, mxd=mxd):
                s.wait_ge(sV.s, mxd)
                sSC.inc(s.activation(ltm[:, tc, :], ltm[:, tc, :], AF.Exp, bias=lrow[:, tc, 1:2]))
            blk.scalar(f)
            exd = sSC.n

            def f(v, tc=tc, exd=exd):
                v.wait_ge(sSC.s, exd)
                sV.inc(v.tensor_reduce(lrow[:, tc, 0:1], ltm[:, tc, :], mybir.AxisListType.X, Op.add))
            blk.vector(f)

            def f(v, tc=tc, m=sV.n):
                v.wait_ge(sV.s, m)
                sV.inc(v.reciprocal(lrow[:, tc, 1:2], lrow[:, tc, 0:1]))
            blk.vector(f)

            def f(v, tc=tc, m=sV.n):
                v.wait_ge(sV.s, m)
                sV.inc(v.tensor_tensor(stmt[:, tc, :], ltm[:, tc, :],
                                       lrow[:, tc, 1:2].broadcast_to((128, NE)), Op.mult))
            blk.vector(f)
        smax_done = sV.n

        em_marks = {}
        for tc in range(16):
            def f(t, tc=tc, smax_done=smax_done):
                if tc == 0:
                    t.wait_ge(sV.s, smax_done)
                    t.wait_ge(sSC.s, sSC.n)
                if tc >= 2:
                    t.wait_ge(sSC.s, em_marks[tc - 2])
                pb = P0 if tc % 2 == 0 else P2
                sMM.inc(t.transpose(pb[0:64, 0:128], stmt[:, tc, :], eyef[:, :]))
            blk.tensor(f)
            trm = sMM.n

            def f(s, tc=tc, trm=trm):
                s.wait_ge(sMM.s, trm)
                pb = P0 if tc % 2 == 0 else P2
                sSC.inc(s.copy(semj[:, tc * 128:tc * 128 + 128], pb[0:64, 0:128]))
            blk.scalar(f)
            em_marks[tc] = sSC.n
        sem_done = sSC.n

        def f(s, sem_done=sem_done):
            s.wait_ge(sSC.s, sem_done)
            sRS.inc(s.dma_start(out=sc_in[:, :], in_=semj[:]), 16)
        blk.sync(f)
        scw = sRS.n

        def f(g, scw=scw):
            g.wait_ge(sRS.s, scw)
            sCC.inc(g.collective_compute(
                "AllGather", Op.bypass,
                replica_groups=[list(range(NCORE))],
                ins=[sc_in.opt()], outs=[sc_out.opt()]))
        blk.gpsimd(f)
        ag1 = sCC.n

        def f(s, ag1=ag1):
            s.wait_ge(sCC.s, ag1)
            sIN.inc(s.dma_start(out=gath[:], in_=sc_out.rearrange("(c p) t -> p c t", p=128)), 16)
        blk.sync(f)
        gld = sIN.n

        sel_marks = {}
        for i in range(4):
            for tc4 in range(4):
                def f(t, i=i, tc4=tc4, gld=gld):
                    if i == 0 and tc4 == 0:
                        t.wait_ge(sIN.s, gld)
                        t.wait_ge(sSC.s, sSC.n)
                    if (i * 4 + tc4) >= 2:
                        t.wait_ge(sSC.s, sel_marks[(i * 4 + tc4) - 2])
                    pb = P0 if (i * 4 + tc4) % 2 == 0 else P2
                    sMM.inc(t.matmul(pb[0:16, 0:OC], selb[:, i, :],
                                     gath[:, i, tc4 * OC:tc4 * OC + OC], start=True, stop=True))
                blk.tensor(f)
                smm = sMM.n

                def f(s, i=i, tc4=tc4, smm=smm):
                    s.wait_ge(sMM.s, smm)
                    s.wait_ge(sRS.s, sRS.n)
                    pb = P0 if (i * 4 + tc4) % 2 == 0 else P2
                    sSC.inc(s.copy(stg[:], pb[0:16, 0:OC]))
                blk.scalar(f)
                sel_marks[i * 4 + tc4] = sSC.n

                def f(s, i=i, tc4=tc4, mark=sSC.n):
                    s.wait_ge(sSC.s, mark)
                    sRS.inc(s.dma_start(out=sorstg[:, :], in_=stg[:]), 16)
                blk.sync(f)
                stgw = sRS.n

                def f(s, i=i, tc4=tc4, stgw=stgw):
                    s.wait_ge(sRS.s, stgw)
                    for jj in range(2):
                        j = 2 * i + jj
                        sRS.inc(s.dma_start(
                            out=sorc[j * 16 + 4 * tc4: j * 16 + 4 * tc4 + 4, :, :],
                            in_=sorstg.rearrange("(jj s) (p f) -> jj p s f", jj=2, f=128)[jj]), 16)
                blk.sync(f)
        sort_ready = sRS.n

        def negate_partitions(v, bbit):
            v.tensor_scalar(sorc[:], sorc[:], sgnc[:, bbit:bbit + 1], 0.0, Op.mult, Op.add)

        def negate_free(v, period_bit):
            half = 1 << period_bit
            vw = sorc[:].rearrange("p c (n w) -> p (c n) w", w=2 * half)
            v.tensor_scalar(vw[:, :, half:], vw[:, :, half:], -1.0, 0.0, Op.mult, Op.add)

        def cmpex(v, dfree):
            vw = sorc[:].rearrange("p c (n two d) -> p (c n) two d", two=2, d=dfree)
            ncn = vw.shape[1]
            tm = sortmp[:].rearrange("p (cn d) -> p cn d", d=dfree)[:, 0:ncn, :]
            v.tensor_tensor(tm, vw[:, :, 0, :], vw[:, :, 1, :], Op.min)
            v.tensor_tensor(vw[:, :, 1, :], vw[:, :, 0, :], vw[:, :, 1, :], Op.max)
            v.tensor_copy(vw[:, :, 0, :], tm)

        def pb_sel(c):
            return P0 if c % 2 == 0 else P2

        def emit_transpose_all(dep_v):
            for c in range(8):
                def f(t, c=c, dep_v=dep_v):
                    if c == 0:
                        t.wait_ge(sV.s, dep_v)
                        t.wait_ge(sSC.s, sSC.n)
                    else:
                        t.wait_ge(sV.s, sV.n)
                    sMM.inc(t.transpose(pb_sel(c)[:, 0:128], sorc[:, c, :], eyef[:, :]))
                blk.tensor(f)
                trm = sMM.n

                def f(v, c=c, trm=trm):
                    v.wait_ge(sMM.s, trm)
                    sV.inc(v.tensor_copy(sorc[:, c, :], pb_sel(c)[:, 0:128]))
                blk.vector(f)

        first_sort_op = [True]

        def emit_sweeps(ks, js, form_now):
            def f(v, ks=ks, js=js, form_now=form_now):
                if first_sort_op[0]:
                    v.wait_ge(sRS.s, sort_ready)
                    first_sort_op[0] = False
                else:
                    v.wait_ge(sMM.s, sMM.n)
                k = ks
                pp_bit, fs_bit = None, None
                if k < 14:
                    if form_now == "N":
                        if k >= 7:
                            fs_bit = k - 7
                        else:
                            pp_bit = k
                    else:
                        if k >= 7:
                            pp_bit = k - 7
                        else:
                            fs_bit = k
                if pp_bit is not None:
                    negate_partitions(v, pp_bit)
                if fs_bit is not None:
                    negate_free(v, fs_bit)
                for j in js:
                    d = (1 << (j - 7)) if form_now == "N" else (1 << j)
                    cmpex(v, d)
                if pp_bit is not None:
                    negate_partitions(v, pp_bit)
                if fs_bit is not None:
                    negate_free(v, fs_bit)
                sV.inc(v.engine_nop())
            blk.vector(f)

        form = "T"
        for k in range(1, 15):
            sweeps = list(range(k - 1, -1, -1))
            hi = [j for j in sweeps if j >= 7]
            lo = [j for j in sweeps if j < 7]
            if hi:
                if form != "N":
                    emit_transpose_all(sV.n)
                    form = "N"
                emit_sweeps(k, hi, "N")
            if lo:
                if form != "T":
                    emit_transpose_all(sV.n)
                    form = "T"
                emit_sweeps(k, lo, "T")
        if form != "N":
            emit_transpose_all(sV.n)
            form = "N"
        sorted_done = sV.n

        def f(g, sorted_done=sorted_done):
            g.wait_ge(sV.s, sorted_done)
            sW.inc(g.dma_start(out=dbg_sort, in_=sorc[:]), 16)
        blk.gpsimd(f)

        def f(v, sorted_done=sorted_done):
            v.wait_ge(sV.s, sorted_done)
            v.memset(hmsk[:, :], 0.0)
            v.memset(hmsk[0:64, 0:1], 1.0)
            sV.inc(v.memset(hmsk[64:128, 1:2], 1.0))
        blk.vector(f)
        hm = sV.n

        def f(t, hm=hm):
            t.wait_ge(sV.s, hm)
            t.wait_ge(sSC.s, sSC.n)
            t.matmul(P1[0:2, 0:512], hmsk[:], sorc[:].rearrange("p c f -> p (c f)")[:, 0:512],
                     start=True, stop=True)
            sMM.inc(t.matmul(P6[0:2, 0:512], hmsk[:],
                             sorc[:].rearrange("p c f -> p (c f)")[:, 512:1024], start=True, stop=True))
        blk.tensor(f)
        gsm = sMM.n

        def f(v, gsm=gsm):
            v.wait_ge(sMM.s, gsm)
            v.tensor_copy(gssb[:, 0:512], P1[0:2, 0:512])
            sV.inc(v.tensor_copy(gssb[:, 512:1024], P6[0:2, 0:512]))
        blk.vector(f)
        gse = sV.n

        def f(g, gse=gse):
            g.wait_ge(sV.s, gse)
            sW.inc(g.dma_start(out=dbg_gs, in_=gssb[:]), 16)
        blk.gpsimd(f)

        def f(s, gse=gse):
            s.wait_ge(sV.s, gse)
            sRS.inc(s.dma_start(
                out=gs_in.rearrange("c (two f) -> two c f", two=2),
                in_=gssb[:].rearrange("h (c f) -> h c f", f=128)), 16)
        blk.sync(f)
        gsw = sRS.n

        def f(g, gsw=gsw):
            g.wait_ge(sRS.s, gsw)
            sCC.inc(g.collective_compute(
                "AllGather", Op.bypass,
                replica_groups=[list(range(NCORE))],
                ins=[gs_in.opt()], outs=[gs_out.opt()]))
        blk.gpsimd(f)
        ag2 = sCC.n

        def f(s, ag2=ag2):
            s.wait_ge(sCC.s, ag2)
            sIN.inc(s.dma_start(out=gso[:], in_=gs_out[:, :]), 16)
            sIN.inc(s.dma_start(out=sor2[:],
                                in_=gs_out.rearrange("e q -> (e q)").rearrange("(p f) -> p f", f=128)), 16)
        blk.sync(f)
        g2ld = sIN.n

        def sor2_negate_partitions(v, bbit):
            v.tensor_scalar(sor2[:], sor2[:], sgnc[:, bbit:bbit + 1], 0.0, Op.mult, Op.add)

        def sor2_negate_free(v, period_bit):
            half = 1 << period_bit
            vw = sor2[:].rearrange("p (n w) -> p n w", w=2 * half)
            v.tensor_scalar(vw[:, :, half:], vw[:, :, half:], -1.0, 0.0, Op.mult, Op.add)

        def sor2_cmpex(v, dfree):
            vw = sor2[:].rearrange("p (n two d) -> p n two d", two=2, d=dfree)
            ncn = vw.shape[1]
            tm = sor2t[:].rearrange("p (n d) -> p n d", d=dfree)[:, 0:ncn, :]
            v.tensor_tensor(tm, vw[:, :, 0, :], vw[:, :, 1, :], Op.min)
            v.tensor_tensor(vw[:, :, 1, :], vw[:, :, 0, :], vw[:, :, 1, :], Op.max)
            v.tensor_copy(vw[:, :, 0, :], tm)

        first2 = [True]

        def emit2_transpose(dep):
            def f(t, dep=dep):
                t.wait_ge(sV.s, dep)
                t.wait_ge(sSC.s, sSC.n)
                sMM.inc(t.transpose(P1[:, 0:128], sor2[:], eyef[:, :]))
            blk.tensor(f)
            trm = sMM.n

            def f(v, trm=trm):
                v.wait_ge(sMM.s, trm)
                sV.inc(v.tensor_copy(sor2[:], P1[:, 0:128]))
            blk.vector(f)

        def emit2_sweeps(k, js, form_now):
            def f(v, k=k, js=js, form_now=form_now):
                if first2[0]:
                    v.wait_ge(sIN.s, g2ld)
                    first2[0] = False
                else:
                    v.wait_ge(sMM.s, sMM.n)
                pp_bit, fs_bit = None, None
                if k < 14:
                    if form_now == "N":
                        if k >= 7:
                            fs_bit = k - 7
                        else:
                            pp_bit = k
                    else:
                        if k >= 7:
                            pp_bit = k - 7
                        else:
                            fs_bit = k
                if pp_bit is not None:
                    sor2_negate_partitions(v, pp_bit)
                if fs_bit is not None:
                    sor2_negate_free(v, fs_bit)
                for j in js:
                    d = (1 << (j - 7)) if form_now == "N" else (1 << j)
                    sor2_cmpex(v, d)
                if pp_bit is not None:
                    sor2_negate_partitions(v, pp_bit)
                if fs_bit is not None:
                    sor2_negate_free(v, fs_bit)
                sV.inc(v.engine_nop())
            blk.vector(f)

        form2 = "T"
        for k in range(1, 15):
            sweeps = list(range(k - 1, -1, -1))
            hi = [j for j in sweeps if j >= 7]
            lo = [j for j in sweeps if j < 7]
            if hi:
                if form2 != "N":
                    emit2_transpose(sV.n)
                    form2 = "N"
                emit2_sweeps(k, hi, "N")
            if lo:
                if form2 != "T":
                    emit2_transpose(sV.n)
                    form2 = "T"
                emit2_sweeps(k, lo, "T")
        if form2 != "N":
            emit2_transpose(sV.n)
            form2 = "N"
        g2sorted = sV.n

        def f(t, g2sorted=g2sorted):
            t.wait_ge(sV.s, g2sorted)
            t.wait_ge(sSC.s, sSC.n)
            sMM.inc(t.matmul(P1[0:64, 0:1], onrf[0:1, 0:64], sor2[0:1, 96:97], start=True, stop=True))
        blk.tensor(f)
        thm = sMM.n

        def f(v, thm=thm, g2ld=g2ld):
            v.wait_ge(sMM.s, thm)
            v.wait_ge(sIN.s, g2ld)
            sV.inc(v.tensor_copy(kth[:, 0:1], P1[0:64, 0:1]))
        blk.vector(f)

        def f(v, m=sV.n):
            v.wait_ge(sV.s, m)
            sV.inc(v.tensor_scalar(gso[:], gso[:], kth[:, 0:1], 0.0, Op.is_ge, Op.add))
        blk.vector(f)

        def f(v, m=sV.n):
            v.wait_ge(sV.s, m)
            sV.inc(v.tensor_reduce(kth[:, 1:2], gso[:], mybir.AxisListType.X, Op.add))
        blk.vector(f)

        def f(v, m=sV.n):
            v.wait_ge(sV.s, m)
            sV.inc(v.tensor_scalar(kth[:, 2:3], kth[:, 1:2], -64.0, 16384.0, Op.mult, Op.add))
        blk.vector(f)
        kd = sV.n

        def f(g, kd=kd):
            g.wait_ge(sV.s, kd)
            sW.inc(g.dma_start(out=dbg_kth, in_=kth[:]), 16)
        blk.gpsimd(f)

        def f(s):
            sIN.inc(s.dma_start(out=seltc[:], in_=seltau[:, :]), 16)
        blk.sync(f)
        stau_ld = sIN.n

        def f(t, kd=kd, stau_ld=stau_ld):
            t.wait_ge(sV.s, kd)
            t.wait_ge(sIN.s, stau_ld)
            t.matmul(P6[0:1, 0:8], kth[:, 2:3], seltc[:], start=True, stop=False)
            sMM.inc(t.matmul(P6[0:1, 8:16], kth[:, 1:2], seltc[:], start=False, stop=True))
        blk.tensor(f)
        rm = sMM.n

        def f(v, rm=rm):
            v.wait_ge(sMM.s, rm)
            v.tensor_copy(tfix[:, 0, :], P6[0:1, 0:8])
            sV.inc(v.tensor_copy(tfix[:, 1, :], P6[0:1, 8:16]))
        blk.vector(f)
        rrow = sV.n

        def f(t, rrow=rrow):
            t.wait_ge(sV.s, rrow)
            sMM.inc(t.matmul(P6[:, 16:24], onrf[0:1, :], tfix[:, 0, :], start=True, stop=True))
        blk.tensor(f)
        rbm = sMM.n

        def f(v, rbm=rbm):
            v.wait_ge(sMM.s, rbm)
            sV.inc(v.tensor_copy(rb128[:], P6[:, 16:24]))
        blk.vector(f)
        rbd = sV.n

        for s_ in range(8):
            def f(v, s_=s_, rbd=rbd):
                if s_ == 0:
                    v.wait_ge(sV.s, rbd)
                else:
                    v.wait_ge(sMM.s, sMM.n)
                sV.inc(v.tensor_scalar(eqt[:], iotc[:], rb128[:, s_:s_ + 1], 0.0, Op.is_equal, Op.add))
            blk.vector(f)

            def f(v, s_=s_, m=sV.n):
                v.wait_ge(sV.s, m)
                sV.inc(v.tensor_tensor(eqt[:], eqt[:], sorc[:, s_, :], Op.mult))
            blk.vector(f)
            eqd = sV.n

            def f(t, s_=s_, eqd=eqd):
                t.wait_ge(sV.s, eqd)
                sMM.inc(t.matmul(P7[0:1, 0:128], onc[:, 0:1], eqt[:], start=True, stop=True))
            blk.tensor(f)
            tmm = sMM.n

            def f(v, s_=s_, tmm=tmm):
                v.wait_ge(sMM.s, tmm)
                sV.inc(v.tensor_reduce(tfix[:, 2, s_:s_ + 1], P7[0:1, 0:128], mybir.AxisListType.X, Op.add))
            blk.vector(f)
        taud = sV.n

        def f(v, taud=taud):
            v.wait_ge(sV.s, taud)
            sV.inc(v.tensor_scalar(tfix[:, 1, :], tfix[:, 1, :], 0.5, 0.0, Op.is_ge, Op.add))
        blk.vector(f)

        def f(v, m=sV.n):
            v.wait_ge(sV.s, m)
            sV.inc(v.tensor_tensor(tfix[:, 2, :], tfix[:, 2, :], tfix[:, 1, :], Op.mult))
        blk.vector(f)

        def f(v, m=sV.n):
            v.wait_ge(sV.s, m)
            sV.inc(v.tensor_scalar(tfix[:, 1, :], tfix[:, 1, :], -1e30, 1e30, Op.mult, Op.add))
        blk.vector(f)

        def f(v, m=sV.n):
            v.wait_ge(sV.s, m)
            sV.inc(v.tensor_tensor(tfix[:, 2, :], tfix[:, 2, :], tfix[:, 1, :], Op.add))
        blk.vector(f)
        tfd = sV.n

        def f(g, tfd=tfd):
            g.wait_ge(sV.s, tfd)
            sW.inc(g.dma_start(out=dbg_tau, in_=tfix[:]), 16)
        blk.gpsimd(f)

        def f(s, tfd=tfd):
            s.wait_ge(sV.s, tfd)
            sRS.inc(s.dma_start(out=tau_in[0:1, 0:8], in_=tfix[:, 2, :]), 16)
        blk.sync(f)
        tw = sRS.n

        def f(g, tw=tw):
            g.wait_ge(sRS.s, tw)
            sCC.inc(g.collective_compute(
                "AllGather", Op.bypass,
                replica_groups=[list(range(NCORE))],
                ins=[tau_in.opt()], outs=[tau_out.opt()]))
        blk.gpsimd(f)
        ag3 = sCC.n

        def f(s, ag3=ag3):
            s.wait_ge(sCC.s, ag3)
            sIN.inc(s.dma_start(out=tausb[:].rearrange("o (c s) -> o c s", c=8),
                                in_=tau_out.rearrange("(c r) s -> r c s", r=8)[0:1]), 16)
        blk.sync(f)
        tld = sIN.n

        def f(t, tld=tld):
            t.wait_ge(sIN.s, tld)
            t.wait_ge(sSC.s, sSC.n)
            sMM.inc(t.matmul(P1[:, 0:64], onrf[:], tausb[:], start=True, stop=True))
        blk.tensor(f)
        tbm = sMM.n

        def f(v, tbm=tbm):
            v.wait_ge(sMM.s, tbm)
            v.tensor_tensor(ltm[:], stmt[:], P1[:, 0:64].unsqueeze(1).broadcast_to((128, 16, NE)), Op.is_ge)
            v.tensor_tensor(ltm[:], ltm[:], stmt[:], Op.mult)
            sV.inc(v.tensor_reduce(lrow[:, :, 0], ltm[:], mybir.AxisListType.X, Op.add))
        blk.vector(f)

        def f(v, m=sV.n):
            v.wait_ge(sV.s, m)
            sV.inc(v.tensor_scalar(lrow[:, :, 0], lrow[:, :, 0], 1.0, 1e-6, Op.mult, Op.add))
        blk.vector(f)

        def f(v, m=sV.n):
            v.wait_ge(sV.s, m)
            sV.inc(v.reciprocal(lrow[:, :, 1], lrow[:, :, 0]))
        blk.vector(f)

        def f(v, m=sV.n):
            v.wait_ge(sV.s, m)
            sV.inc(v.tensor_tensor(ltm[:], ltm[:],
                                   lrow[:, :, 1:2].broadcast_to((128, 16, NE)), Op.mult))
        blk.vector(f)
        fsd = sV.n

        fse_marks = {}
        for tc in range(16):
            def f(t, tc=tc, fsd=fsd):
                if tc == 0:
                    t.wait_ge(sV.s, fsd)
                    t.wait_ge(sSC.s, sSC.n)
                if tc >= 2:
                    t.wait_ge(sSC.s, fse_marks[tc - 2])
                pb = P0 if tc % 2 == 0 else P2
                sMM.inc(t.transpose(pb[0:64, 0:128], ltm[:, tc, :], eyef[:, :]))
            blk.tensor(f)
            trm = sMM.n

            def f(s, tc=tc, trm=trm):
                s.wait_ge(sMM.s, trm)
                pb = P0 if tc % 2 == 0 else P2
                s.copy(fsem[:, tc * 128:tc * 128 + 128], pb[0:64, 0:128])
                sSC.inc(s.copy(fseb[:, tc * 128:tc * 128 + 128], pb[0:64, 0:128]))
            blk.scalar(f)
            fse_marks[tc] = sSC.n
        fs_done = sSC.n

        if tap == "scores":
            def f(g):
                g.wait_ge(sSC.s, sSC.n)
                g.wait_ge(sV.s, sV.n)
                sW.inc(g.dma_start(out=tapt, in_=semj[:]), 16)
            blk.gpsimd(f)
        if tap == "sorted":
            def f(g):
                g.wait_ge(sV.s, sV.n)
                sW.inc(g.dma_start(out=tapt, in_=sorc[:]), 16)
            blk.gpsimd(f)
        if tap == "gs":
            def f(g):
                g.wait_ge(sV.s, sV.n)
                g.wait_ge(sRS.s, sRS.n)
                sW.inc(g.dma_start(out=tapt, in_=gs_in[:, :]), 16)
            blk.gpsimd(f)
        if tap == "tau":
            def f(g):
                g.wait_ge(sV.s, sV.n)
                g.wait_ge(sIN.s, sIN.n)
                sW.inc(g.dma_start(out=tapt, in_=tau_out[0:8, :]), 16)
            blk.gpsimd(f)
        if tap == "fs":
            def f(g):
                g.wait_ge(sSC.s, sSC.n)
                sW.inc(g.dma_start(out=tapt, in_=fsem[:]), 16)
            blk.gpsimd(f)

        stD.close()

        stE = contextlib.ExitStack()
        sbufE = lambda name, shape, dt=F32: stE.enter_context(nc.sbuf_tensor(name, shape, dt))
        hg = sbufE("hg", [128, 8, 2, 1024], BF16)
        wdnb = sbufE("wdnb", [128, 8, 2, 8, 128], BF16)
        wupb = [sbufE(f"wupb{i}", [128, 8, HID], BF16) for i in range(2)]
        wgtb = [sbufE(f"wgtb{i}", [128, 8, HID], BF16) for i in range(2)]
        fsbb = sbufE("fsbb", [128, 1024], BF16)
        bdnb = sbufE("bdnb", [64, 8, 128], BF16)
        bupc = sbufE("bupc", [128, 2, NE])
        bgtc = sbufE("bgtc", [128, 2, NE])
        upt2 = [sbufE(f"upt{i}", [128, 512], BF16) for i in range(2)]
        gtt = sbufE("gtt", [128, 512], BF16)
        fsrow = sbufE("fsrow", [1, 1024], BF16)

        def f(s, fs_done=fs_done):
            s.wait_ge(sSC.s, fs_done)
            sIN.inc(s.dma_start(out=bupc[:], in_=bupT[:, :, :]), 16)
            sIN.inc(s.dma_start(out=bgtc[:], in_=bgtT[:, :, :]), 16)
        blk.sync(f)
        bupc_ld = sIN.n

        def f(g, fs_done=fs_done):
            g.wait_ge(sSC.s, fs_done)
            sW.inc(g.dma_start(out=bdnb[:], in_=bdn.rearrange("e (d f) -> e d f", f=128)), 16)
        blk.gpsimd(f)
        bldw = sW.n

        def f(v):
            v.wait_ge(sV.s, sV.n)
            sV.inc(v.memset(outb[:], 0.0))
        blk.vector(f)
        ob0 = sV.n

        wld_marks = {}
        upgate_marks = {}
        ffn_it = 0
        pair_last_vec = [0, 0]
        prev_fbe = 0
        dn_p6_mark = 0
        for tp in range(2):
            toff = tp * 1024
            for gr in range(8):
                for e8 in range(8):
                    e = gr * 8 + e8
                    slot = e8 % 2

                    def f(g, e=e, slot=slot, tp=tp, gr=gr, e8=e8, vmark=sV.n):
                        g.wait_ge(sV.s, vmark)
                        if tp == 0 and gr == 0 and e8 == 0:
                            g.wait_ge(sV.s, ob0)
                            g.wait_ge(sW.s, bldw)
                            g.wait_ge(sSC.s, sSC.n)
                            g.wait_ge(sMM.s, sMM.n)
                        prev = (tp, gr, e8 - 2)
                        if prev in upgate_marks:
                            g.wait_ge(sMM.s, upgate_marks[prev])
                        elif e8 < 2 and (tp, gr) != (0, 0):
                            pg = (tp, gr - 1) if gr > 0 else (tp - 1, 7)
                            g.wait_ge(sMM.s, upgate_marks[(pg[0], pg[1], 6 + e8)])
                        sW.inc(g.dma_start(out=wupb[slot][:],
                                           in_=wup.rearrange("e (c p) h -> e p c h", p=128)[e]), 16)
                        sW.inc(g.dma_start(out=wgtb[slot][:],
                                           in_=wgt.rearrange("e (c p) h -> e p c h", p=128)[e]), 16)
                        sW.inc(g.dma_start(out=wdnb[:, e8, :, :, :],
                                           in_=wdn.rearrange("e (c p) (d f) -> e p c d f", p=128, f=128)[e]), 16)
                    blk.gpsimd(f)
                    wld_marks[(tp, e)] = sW.n

                    first_e = (tp == 0 and gr == 0 and e8 == 0)

                    def f(t, e=e, tp=tp, toff=toff, pfbe=prev_fbe, pdn=dn_p6_mark, first_e=first_e):
                        if first_e:
                            t.wait_ge(sSC.s, sSC.n)
                            t.wait_ge(sV.s, sV.n)
                        else:
                            t.wait_ge(sSC.s, pfbe)
                            t.wait_ge(sV.s, pdn)
                        t.matmul(P6[0:1, 0:512], eyeb[0:64, e:e + 1], fseb[:, toff:toff + 512],
                                 start=True, stop=True)
                        sMM.inc(t.matmul(P7[0:1, 0:512], eyeb[0:64, e:e + 1],
                                         fseb[:, toff + 512:toff + 1024], start=True, stop=True))
                    blk.tensor(f)
                    fbm = sMM.n

                    def f(s, fbm=fbm):
                        s.wait_ge(sMM.s, fbm)
                        s.copy(fsrow[:, 0:512], P6[0:1, 0:512])
                        sSC.inc(s.copy(fsrow[:, 512:1024], P7[0:1, 0:512]))
                    blk.scalar(f)
                    fbr = sSC.n

                    def f(t, fbr=fbr):
                        t.wait_ge(sSC.s, fbr)
                        t.matmul(P6[:, 0:512], onrb[:], fsrow[:, 0:512], start=True, stop=True)
                        sMM.inc(t.matmul(P7[:, 0:512], onrb[:], fsrow[:, 512:1024], start=True, stop=True))
                    blk.tensor(f)
                    fbm2 = sMM.n

                    def f(s, fbm2=fbm2):
                        s.wait_ge(sMM.s, fbm2)
                        s.copy(fsbb[:, 0:512], P6[:, 0:512])
                        sSC.inc(s.copy(fsbb[:, 512:1024], P7[:, 0:512]))
                    blk.scalar(f)
                    fbe = sSC.n
                    prev_fbe = fbe

                    for h2 in range(2):
                        for sub in range(2):
                            pr = ffn_it % 2
                            PU = (P0, P1)[pr]
                            PG = (P2, P3)[pr]
                            plv = pair_last_vec[pr]
                            blanket = ffn_it < 2

                            def f(t, e=e, slot=slot, h2=h2, sub=sub, toff=toff,
                                  wldm=wld_marks[(tp, e)], PU=PU, PG=PG, plv=plv, blanket=blanket):
                                t.wait_ge(sW.s, wldm)
                                if blanket:
                                    t.wait_ge(sSC.s, sSC.n)
                                    t.wait_ge(sV.s, sV.n)
                                else:
                                    t.wait_ge(sV.s, plv)
                                for dc in range(8):
                                    t.matmul(PU[:, 0:512], wupb[slot][:, dc, h2 * 128:h2 * 128 + 128],
                                             x2b[:, dc, toff + sub * 512:toff + sub * 512 + 512],
                                             start=(dc == 0), stop=(dc == 7))
                                for dc in range(8):
                                    mi = t.matmul(PG[:, 0:512], wgtb[slot][:, dc, h2 * 128:h2 * 128 + 128],
                                                  x2b[:, dc, toff + sub * 512:toff + sub * 512 + 512],
                                                  start=(dc == 0), stop=(dc == 7))
                                sMM.inc(mi)
                            blk.tensor(f)
                            ugm = sMM.n

                            def f(s, e=e, h2=h2, ugm=ugm, PU=PU, pr=pr, plv=plv, first=(ffn_it == 0)):
                                if first:
                                    s.wait_ge(sIN.s, bupc_ld)
                                s.wait_ge(sMM.s, ugm)
                                s.wait_ge(sV.s, plv)
                                sSC.inc(s.activation(upt2[pr][:], PU[:, 0:512], AF.Silu,
                                                     bias=bupc[:, h2, e:e + 1]))
                            blk.scalar(f)
                            uge = sSC.n

                            def f(v, e=e, e8=e8, h2=h2, sub=sub, uge=uge, ugm=ugm, PG=PG, pr=pr):
                                v.wait_ge(sSC.s, uge)
                                v.wait_ge(sMM.s, ugm)
                                v.tensor_scalar(gtt[:], PG[:, 0:512], bgtc[:, h2, e:e + 1], 0.0,
                                                Op.add, Op.add)
                                v.tensor_tensor(upt2[pr][:], upt2[pr][:], gtt[:], Op.mult)
                                sV.inc(v.tensor_tensor(hg[:, e8, h2, sub * 512:sub * 512 + 512],
                                                       upt2[pr][:], fsbb[:, sub * 512:sub * 512 + 512], Op.mult))
                            blk.vector(f)
                            pair_last_vec[pr] = sV.n
                            ffn_it += 1
                    upgate_marks[(tp, gr, e8)] = sMM.n
                hg_done = sV.n

                if tp == 1 and gr == 0:
                    def f(g, hg_done=hg_done):
                        g.wait_ge(sV.s, hg_done)
                        sW.inc(g.dma_start(out=dbg_hg, in_=hg[:]), 16)
                    blk.gpsimd(f)

                dn_marks = {}
                for dt_ in range(8):
                    for sub in range(2):
                        def f(t, gr=gr, dt_=dt_, sub=sub, tp=tp, toff=toff, hg_done=hg_done):
                            if dt_ == 0 and sub == 0:
                                t.wait_ge(sV.s, hg_done)
                            if (dt_ * 2 + sub) >= 2:
                                t.wait_ge(sV.s, dn_marks[dt_ * 2 + sub - 2])
                            pb = P4 if (dt_ * 2 + sub) % 2 == 0 else P6
                            for e8 in range(8):
                                for h2 in range(2):
                                    last = (e8 == 7 and h2 == 1 and gr != 7)
                                    mi = t.matmul(pb[:, 0:512], wdnb[:, e8, h2, dt_, :],
                                                  hg[:, e8, h2, sub * 512:sub * 512 + 512],
                                                  start=(e8 == 0 and h2 == 0), stop=last)
                            if gr == 7:
                                mi = t.matmul(pb[:, 0:512], bdnb[:, dt_, :],
                                              fseb[:, toff + sub * 512:toff + sub * 512 + 512],
                                              start=False, stop=True)
                            sMM.inc(mi)
                        blk.tensor(f)
                        dnm = sMM.n

                        def f(v, dt_=dt_, sub=sub, tp=tp, toff=toff, dnm=dnm):
                            v.wait_ge(sMM.s, dnm)
                            pb = P4 if (dt_ * 2 + sub) % 2 == 0 else P6
                            sV.inc(v.tensor_tensor(outb[:, dt_, toff + sub * 512:toff + sub * 512 + 512],
                                                   outb[:, dt_, toff + sub * 512:toff + sub * 512 + 512],
                                                   pb[:, 0:512], Op.add))
                        blk.vector(f)
                        dn_marks[dt_ * 2 + sub] = sV.n
                        if (dt_ * 2 + sub) % 2 == 1:
                            dn_p6_mark = sV.n
                if tp == 1 and gr == 0:
                    def f(g, m=sV.n):
                        g.wait_ge(sV.s, m)
                        sW.inc(g.dma_start(out=dbg_ob, in_=outb[:, :, 1024:1536]), 16)
                    blk.gpsimd(f)
        ffn_done = sV.n

        if tap == "ffnout":
            def f(g, ffn_done=ffn_done):
                g.wait_ge(sV.s, ffn_done)
                sW.inc(g.dma_start(out=tapt, in_=outb[:]), 16)
            blk.gpsimd(f)

        stE.close()

        stF = contextlib.ExitStack()
        sbufF = lambda name, shape, dt=F32: stF.enter_context(nc.sbuf_tensor(name, shape, dt))
        fout = sbufF("fout", [128, 8, OC])
        xf2 = sbufF("xf2", [128, 8, OC])
        sqf = sbufF("sqf", [128, 8, OC], BF16)
        rwf2 = sbufF("rwf2", [1, 5, OC])
        rwb2 = sbufF("rwb2", [1, 2, OC], BF16)

        for pn in range(4):
            c0 = pn * OC

            def f(s, pn=pn, c0=c0, ffn_done=ffn_done, outp=sOUT.n):
                if pn == 0:
                    s.wait_ge(sV.s, ffn_done)
                else:
                    s.wait_ge(sV.s, sV.n)
                    s.wait_ge(sOUT.s, outp)
                sIN.inc(s.dma_start(out=xf2[:], in_=x2_spill.rearrange("(c p) t -> p c t", p=128)[:, :, c0:c0 + OC]), 16)
            blk.sync(f)
            ld = sIN.n

            def f(s, pn=pn, c0=c0, ffn_done=ffn_done):
                s.wait_ge(sSC.s, sSC.n)
                if pn == 0:
                    s.wait_ge(sV.s, ffn_done)
                sSC.inc(s.activation(sqf[:], outb[:, :, c0:c0 + OC], AF.Square))
            blk.scalar(f)
            sq = sSC.n

            def f(t, sq=sq, c0=c0, pn=pn):
                t.wait_ge(sSC.s, sq)
                if pn > 0:
                    t.wait_ge(sV.s, sV.n)
                for dc in range(8):
                    t.matmul(P1[0:1, 0:OC], oncb[:], outb[:, dc, c0:c0 + OC], start=(dc == 0), stop=(dc == 7))
                for dc in range(8):
                    mi = t.matmul(P6[0:1, 0:OC], oncb[:], sqf[:, dc, :], start=(dc == 0), stop=(dc == 7))
                sMM.inc(mi)
            blk.tensor(f)
            stt = sMM.n

            def f(v, stt=stt):
                v.wait_ge(sMM.s, stt)
                v.tensor_scalar(rwf2[:, 0, :], P1[0:1, 0:OC], 1.0 / DM, 0.0, Op.mult, Op.add)
                v.tensor_scalar(rwf2[:, 1, :], P6[0:1, 0:OC], 1.0 / DM, 0.0, Op.mult, Op.add)
                v.tensor_tensor(rwf2[:, 2, :], rwf2[:, 0, :], rwf2[:, 0, :], Op.mult)
                sV.inc(v.tensor_tensor(rwf2[:, 2, :], rwf2[:, 1, :], rwf2[:, 2, :], Op.subtract))
            blk.vector(f)
            vr = sV.n

            def f(s, vr=vr):
                s.wait_ge(sV.s, vr)
                sSC.inc(s.activation(rwf2[:, 3, :], rwf2[:, 2, :], AF.Sqrt, bias=EPS))
            blk.scalar(f)
            sd = sSC.n

            def f(v, sd=sd):
                v.wait_ge(sSC.s, sd)
                v.reciprocal(rwf2[:, 4, :], rwf2[:, 3, :])
                v.tensor_copy(rwb2[:, 0, :], rwf2[:, 0, :])
                sV.inc(v.tensor_copy(rwb2[:, 1, :], rwf2[:, 4, :]))
            blk.vector(f)
            rw = sV.n

            def f(t, rw=rw):
                t.wait_ge(sV.s, rw)
                t.matmul(P1[:, 0:OC], onrb[:], rwb2[:, 0, :], start=True, stop=True)
                sMM.inc(t.matmul(P6[:, 0:OC], onrb[:], rwb2[:, 1, :], start=True, stop=True))
            blk.tensor(f)
            bc = sMM.n

            def f(v, bc=bc, ld=ld, c0=c0):
                v.wait_ge(sMM.s, bc)
                v.wait_ge(sIN.s, ld)
                mb = P1[:, 0:OC].unsqueeze(1).broadcast_to((128, 8, OC))
                rb = P6[:, 0:OC].unsqueeze(1).broadcast_to((128, 8, OC))
                v.tensor_tensor(fout[:], outb[:, :, c0:c0 + OC], mb, Op.subtract)
                v.tensor_tensor(fout[:], fout[:], rb, Op.mult)
                sV.inc(v.tensor_tensor(fout[:], fout[:], xf2[:], Op.add))
            blk.vector(f)
            fd = sV.n

            def f(s, c0=c0, fd=fd):
                s.wait_ge(sV.s, fd)
                sOUT.inc(s.dma_start(
                    out=yout.rearrange("(c p) t -> p c t", p=128)[:, :, c0:c0 + OC],
                    in_=fout[:]), 16)
            blk.scalar(f)

        def f(s):
            s.wait_ge(sOUT.s, sOUT.n)
        blk.scalar(f)

        stF.close()
        stC.close()

    return nc



def _hostprep(inputs, n_oc=16):
    L = n_oc * OC
    x = np.asarray(inputs["L0_attn_x"], dtype=np.float32)
    W_in = np.asarray(inputs["L0_attn_W_in"], dtype=np.float32)
    conv_w = np.asarray(inputs["L0_attn_conv_w"], dtype=np.float32)
    conv_b = np.asarray(inputs["L0_attn_conv_b"], dtype=np.float32)
    A_log = np.asarray(inputs["L0_attn_A_log"], dtype=np.float32)
    Dv = np.asarray(inputs["L0_attn_D"], dtype=np.float32)
    dt_b = np.asarray(inputs["L0_attn_dt_bias"], dtype=np.float32)
    W_out = np.asarray(inputs["L0_attn_W_out"], dtype=np.float32)

    eye = np.eye(128, dtype=np.float32)
    hmk = np.zeros((8, 4, 128), np.float32)
    for b in range(4):
        for pp in range(128):
            hmk[2 * b + pp // 64, b, pp] = 1.0
    mones = np.zeros((128, 128), np.float32)
    mones[0:64, 0:64] = 1.0
    mones[64:128, 64:128] = 1.0
    gm = np.zeros((4, 128, 128), np.float32)
    for li, mh in enumerate((8, 16, 32, 64)):
        bu = np.arange(128) // mh
        gm[li] = ((bu[:, None] % 2 == 0) & (bu[None, :] == bu[:, None] + 1)).astype(np.float32)
    bm = np.ones((128, OC), np.float32)
    bm[:, ::8] = 0.0
    sgn = np.ones((128, 7), np.float32)
    for bbit in range(7):
        sgn[:, bbit] = np.where(((np.arange(128) >> bbit) & 1) == 1, -1.0, 1.0)
    iog = (np.arange(128)[None, :] * 128 + np.arange(128)[:, None]).astype(np.float32)

    in_maps = []
    for c in range(NCORE):
        b, g = c // 4, c % 4
        hs = slice(8 * g, 8 * g + 8)
        ch = slice(512 * g, 512 * g + 512)
        xb = np.ascontiguousarray(x[b].T[:, :L])
        tok = slice(2048 * g, 2048 * g + 2048)
        xr = np.ascontiguousarray(x[b].T[:, tok])
        rows = [W_in[ch], W_in[2048 + 512 * g: 2048 + 512 * (g + 1)]]
        for h in range(8 * g, 8 * g + 8):
            rows.append(W_in[4096 + 129 * h: 4096 + 129 * h + 64])
        for h in range(8 * g, 8 * g + 8):
            rows.append(W_in[4096 + 129 * h + 64: 4096 + 129 * h + 128])
        rows.append(np.stack([W_in[4096 + 129 * h + 128] for h in range(8 * g, 8 * g + 8)]))
        wcat = np.concatenate(rows, axis=0)
        wpad = np.zeros((FOP, DM), np.float32)
        wpad[:wcat.shape[0]] = wcat
        w_in_c = np.ascontiguousarray(wpad.T)
        w_out_c = np.ascontiguousarray(W_out[:, ch].T)
        pidx = np.arange(128)
        cw = np.zeros((128, 4, 4), np.float32)
        cb = np.zeros((128, 4), np.float32)
        al = np.zeros((128, 4), np.float32)
        dc = np.zeros((128, 4), np.float32)
        for bi in range(4):
            chan = 512 * g + 128 * bi + pidx
            cw[:, bi] = conv_w[chan, 0]
            cb[:, bi] = conv_b[chan]
            dc[:, bi] = Dv[chan // 64]
            al[:, bi] = A_log[chan // 64, pidx % 64]
        dtb = np.zeros((128, 1), np.float32)
        dtb[0:8, 0] = dt_b[hs]
        sel = np.zeros((128, 4, 16), np.float32)
        for i in range(4):
            for jj in range(2):
                src_core = 2 * i + jj
                for s_ in range(8):
                    e_global = 8 * c + s_
                    sel[64 * jj + e_global, i, jj * 8 + s_] = 1.0
        m = dict(
            xf=xb, xres=xr, w_in=w_in_c, w_out=w_out_c, convw=cw, convb=cb,
            alog=al, dvec=dc, dtbias=dtb,
            wr=np.ascontiguousarray(np.asarray(inputs["L0_ffn_Wr"], dtype=np.float32)),
            brv=np.ascontiguousarray(np.asarray(inputs["L0_ffn_br"], dtype=np.float32).reshape(NE, 1)),
            wup=np.ascontiguousarray(np.asarray(inputs["L0_ffn_Wup"], dtype=np.float32)),
            wgt=np.ascontiguousarray(np.asarray(inputs["L0_ffn_Wgate"], dtype=np.float32)),
            wdn=np.ascontiguousarray(np.asarray(inputs["L0_ffn_Wdown"], dtype=np.float32)),
            bupT=np.ascontiguousarray(
                np.asarray(inputs["L0_ffn_bup"], dtype=np.float32).T.reshape(2, 128, NE).transpose(1, 0, 2)),
            bgtT=np.ascontiguousarray(
                np.asarray(inputs["L0_ffn_bgate"], dtype=np.float32).T.reshape(2, 128, NE).transpose(1, 0, 2)),
            bdn=np.ascontiguousarray(np.asarray(inputs["L0_ffn_bdown"], dtype=np.float32)),
            eyec=eye, hmk=hmk, mones=mones, gmaskc=gm, bmask=bm, sgnp=sgn,
            iotag=iog, selt=sel, seltau=np.eye(64, dtype=np.float32)[:, 8 * c:8 * c + 8].copy(),
            onescol=np.ones((128, 1), np.float32),
            onesrow=np.ones((1, 128), np.float32),
            constv=np.tile(np.array([[1e-5, 1.0, -1.0, 0.0]], np.float32), (128, 1)),
        )
        in_maps.append(m)
    return in_maps


_NC_CACHE = {}


def kernel(**inputs):
    in_maps = _hostprep(inputs)
    key = ("full", 16)
    if key not in _NC_CACHE:
        _NC_CACHE[key] = build_nc(n_oc=16, tap=None)
    nc = _NC_CACHE[key]
    res = run_bass_kernel_spmd(nc, in_maps, core_ids=list(range(NCORE)))
    out = np.zeros((BB, LSEQ, DM), np.float32)
    for c in range(NCORE):
        b, g = c // 4, c % 4
        out[b, 2048 * g:2048 * (g + 1), :] = res.results[c]["yout"].T
    return out


# version 11
# speedup vs baseline: 1.1583x; 1.0214x over previous
import contextlib
import numpy as np

import concourse.bass as bass
import concourse.mybir as mybir
from concourse.bass_utils import run_bass_kernel_spmd
from concourse.alu_op_type import AluOpType as Op

F32 = mybir.dt.float32
BF16 = mybir.dt.bfloat16
AF = mybir.ActivationFunctionType

NCORE = 8
BB, LSEQ, DM = 2, 8192, 1024
HPC = 8
CPC = 512
FOP = 2176
OC = 512
QC = 128
TLOC = 2048
NE, HID = 64, 256
EPS = 1e-5


class Sem:
    def __init__(self, st, nc, name):
        self.s = st.enter_context(nc.semaphore(name))
        self.n = 0

    def inc(self, instr, k=1):
        instr.then_inc(self.s, k)
        self.n += k
        return instr


def build_nc(n_oc=16, tap=None):
    nc = bass.Bass("TRN2", target_bir_lowering=False, debug=False,
                   detect_race_conditions=False)
    L = n_oc * OC

    def din(name, shape):
        return nc.dram_tensor(name, shape, F32, kind="ExternalInput").ap()

    xf = din("xf", [DM, L])
    xres = din("xres", [DM, TLOC])
    w_in = din("w_in", [DM, FOP])
    w_out = din("w_out", [CPC, DM])
    convw = din("convw", [128, 4, 4])
    convb = din("convb", [128, 4])
    alog = din("alog", [128, 4])
    dvec = din("dvec", [128, 4])
    dtbias = din("dtbias", [128, 1])
    wr = din("wr", [DM, NE])
    brv = din("brv", [NE, 1])
    wup = din("wup", [NE, DM, HID])
    wgt = din("wgt", [NE, DM, HID])
    wdn = din("wdn", [NE, HID, DM])
    bupT = din("bupT", [128, 2, NE])
    bgtT = din("bgtT", [128, 2, NE])
    bdn = din("bdn", [NE, DM])
    eyec = din("eyec", [128, 128])
    hmk = din("hmk", [8, 4, 128])
    mones = din("mones", [128, 128])
    gmaskc = din("gmaskc", [4, 128, 128])
    bmask = din("bmask", [128, OC])
    sgnp = din("sgnp", [128, 7])
    iotag = din("iotag", [128, 128])
    selt = din("selt", [128, 4, 16])
    onescol = din("onescol", [128, 1])
    constv = din("constv", [128, 4])
    onesrow = din("onesrow", [1, 128])
    seltau = din("seltau", [64, 8])

    idram = lambda name, shape: nc.dram_tensor(name, shape, F32).ap()
    rs_in = idram("rs_in", [4, DM, TLOC])
    rs_out = idram("rs_out", [DM, TLOC])
    sc_in = idram("sc_in", [NE, TLOC])
    sc_out = idram("sc_out", [NCORE * NE, TLOC])
    gs_in = idram("gs_in", [8, 256])
    gs_out = idram("gs_out", [64, 256])
    tau_in = idram("tau_in", [8, 8])
    tau_out = idram("tau_out", [64, 8])
    x2_spill = idram("x2_spill", [DM, TLOC])
    sorstg = idram("sorstg", [16, OC])

    yout = nc.dram_tensor("yout", [DM, TLOC], F32, kind="ExternalOutput").ap()
    dbg_sort = nc.dram_tensor("dbg_sort", [128, 8, 128], F32, kind="ExternalOutput").ap()
    dbg_kth = nc.dram_tensor("dbg_kth", [64, 3], F32, kind="ExternalOutput").ap()
    dbg_tau = nc.dram_tensor("dbg_tau", [1, 3, 8], F32, kind="ExternalOutput").ap()
    dbg_gs = nc.dram_tensor("dbg_gs", [2, 1024], F32, kind="ExternalOutput").ap()
    dbg_hg = nc.dram_tensor("dbg_hg", [128, 8, 2, 1024], F32, kind="ExternalOutput").ap()
    dbg_ob = nc.dram_tensor("dbg_ob", [128, 8, 512], F32, kind="ExternalOutput").ap()
    tapt = None
    if tap is not None:
        tshape = dict(proj=[128, 4, OC], xconv=[128, 4, OC + 8], dt=[128, OC],
                      yg=[128, 4, OC], mixpart=[128, 8, OC], x2=[DM, TLOC],
                      scores=[NE, TLOC], sorted=[128, 8, 128], gs=[8, 256],
                      tau=[8, 8], fs=[NE, TLOC], ybd=[128, 4, OC],
                      cl4=[128, 4, OC], bl4=[128, 4, OC], cl0=[128, 4, OC],
                      bl0=[128, 4, OC], lam=[128, 4, 4], dsum=[8, 128],
                      daux=[8, 3, OC // 8, 8], rsout=[DM, TLOC], rsin=[4, DM, TLOC],
                      ffnout=[128, 8, TLOC])[tap]
        tapt = nc.dram_tensor("tap_" + tap, tshape, F32, kind="ExternalOutput").ap()

    st = contextlib.ExitStack()
    stB = contextlib.ExitStack()
    sbufA = lambda name, shape, dt=F32: st.enter_context(nc.sbuf_tensor(name, shape, dt))
    sbuf = lambda name, shape, dt=F32: stB.enter_context(nc.sbuf_tensor(name, shape, dt))
    psum = lambda name, shape, dt=F32: st.enter_context(nc.psum_tensor(name, shape, dt))

    with st:
        cwc = sbufA("cwc", [128, 4, 4])
        cbc = sbufA("cbc", [128, 4])
        acol = sbufA("acol", [128, 4])
        dcolc = sbufA("dcolc", [128, 4])
        dtbc = sbufA("dtbc", [128, 1])
        eyeb = sbufA("eyeb", [128, 128], BF16)
        eyef = sbufA("eyef", [128, 128])
        hmkc = sbufA("hmkc", [8, 4, 128])
        monesb = sbufA("monesb", [128, 128], BF16)
        gmb = sbufA("gmb", [128, 4, 128], BF16)
        bmkc = sbufA("bmkc", [128, OC], BF16)
        onc = sbufA("onc", [128, 1])
        oncb = sbufA("oncb", [128, 1], BF16)
        onrb = sbufA("onrb", [1, 128], BF16)
        cstv = sbufA("cstv", [128, 4])
        nc.const_aps.register(F32, EPS, cstv[:, 0:1])
        nc.const_aps.register(F32, 1.0, cstv[:, 1:2])
        nc.const_aps.register(F32, -1.0, cstv[:, 2:3])
        nc.const_aps.register(F32, 0.0, cstv[:, 3:4])
        winr = sbuf("winr", [128, 8, 17, 128], BF16)
        wotr = sbuf("wotr", [128, 4, 8, 128], BF16)
        xcf = sbuf("xcf", [128, 8, OC])
        xcb = sbuf("xcb", [128, 8, OC], BF16)
        sqb = sbuf("sqb", [128, 8, OC], BF16)
        rowf = sbuf("rowf", [1, 5, OC])
        rowb = sbuf("rowb", [1, 2, OC], BF16)
        xiw = sbuf("xiw", [128, 4, OC + 8], BF16)
        zbt = sbuf("zbt", [128, 4, OC], BF16)
        btt = sbuf("btt", [128, 4, OC], BF16)
        ctt = sbuf("ctt", [128, 4, OC], BF16)
        dtsb = sbuf("dtsb", [128, OC])
        dtaux = sbuf("dtaux", [8, 3, OC // 8, 8])
        dsum = sbuf("dsum", [8, 128])
        lamt = sbuf("lamt", [128, 4, 4])
        cfac = sbuf("cfac", [128, 4, 32], BF16)
        cfacB = sbuf("cfacB", [128, 4, 32], BF16)
        xch = sbuf("xch", [128, 4, OC + 8], BF16)
        bdt = sbuf("bdt", [128, 4, OC + 8], BF16)
        dmk = sbuf("dmk", [128, 4, OC + 8], BF16)
        ebd = sbuf("ebd", [128, 4, OC], BF16)
        wbd = sbuf("wbd", [128, 4, OC], BF16)
        gbs = sbuf("gbs", [128, OC], BF16)
        gtp = sbuf("gtp", [128, OC], BF16)
        ybd = sbuf("ybd", [128, 4, OC], BF16)
        cl = [sbuf(f"cl{m}", [128, 4, OC], BF16) for m in (8, 16, 32, 64, 128)]
        blv = [sbuf(f"bl{m}", [128, 4, OC], BF16) for m in (8, 16, 32, 64, 128)]
        xts = sbuf("xts", [128, 64], BF16)
        bts = sbuf("bts", [128, 64], BF16)
        gsbA = sbuf("gsbA", [128, 128], BF16)
        gsbB = sbuf("gsbB", [128, 128], BF16)
        hst = sbuf("hst", [128, 4, 64])
        hsb = sbuf("hsb", [128, 4, 64], BF16)
        ygt = sbuf("ygt", [128, 4, OC])
        rsst = sbuf("rsst", [128, 4, OC])
        xnb = rsst[:].bitcast(BF16).rearrange("p c t -> p (c t)").rearrange("p (c t) -> p c t", c=8)

        P0 = psum("P0", [128, 512])
        P1 = psum("P1", [128, 512])
        P2 = psum("P2", [128, 512])
        P3 = psum("P3", [128, 512])
        P4 = psum("P4", [128, 512])
        P5 = psum("P5", [128, 1024], BF16)
        P6 = psum("P6", [128, 512])
        P7 = psum("P7", [128, 512])

        sIN = Sem(st, nc, "sIN")
        sW = Sem(st, nc, "sW")
        sMM = Sem(st, nc, "sMM")
        sV = Sem(st, nc, "sV")
        sSC = Sem(st, nc, "sSC")
        sG = Sem(st, nc, "sG")
        sCC = Sem(st, nc, "sCC")
        sOUT = Sem(st, nc, "sOUT")
        sRS = Sem(st, nc, "sRS")

        blk = st.enter_context(nc.Block())

        def f(s):
            for o_, i_ in ((eyef, eyec), (hmkc, hmk), (cwc, convw), (cbc, convb),
                           (dcolc, dvec), (dtbc, dtbias), (onc, onescol), (acol, alog),
                           (cstv, constv)):
                sIN.inc(s.dma_start(out=o_[:], in_=i_[(slice(None),) * len(i_.shape)]), 16)
        blk.sync(f)

        def f(g):
            sW.inc(g.dma_start(out=winr[:], in_=w_in.rearrange("(c p) (t f) -> p c t f", p=128, f=128)), 16)
            sW.inc(g.dma_start(out=wotr[:], in_=w_out.rearrange("(c p) (t f) -> p c t f", p=128, f=128)), 16)
            sW.inc(g.dma_start(out=eyeb[:], in_=eyec[:, :]), 16)
            sW.inc(g.dma_start(out=monesb[:], in_=mones[:, :]), 16)
            sW.inc(g.dma_start(out=gmb[:], in_=gmaskc.rearrange("m p f -> p m f")), 16)
            sW.inc(g.dma_start(out=bmkc[:], in_=bmask[:, :]), 16)
            sW.inc(g.dma_start(out=oncb[:], in_=onescol[:, :]), 16)
            sW.inc(g.dma_start(out=onrb[:], in_=onesrow[:, :]), 16)
        blk.gpsimd(f)

        def f(s):
            s.wait_ge(sIN.s, sIN.n)
            s.activation(acol[:], acol[:], AF.Exp)
            sSC.inc(s.mul(acol[:], acol[:], -1.0))
        blk.scalar(f)

        def f(v):
            v.memset(hst[:], 0.0)
            sV.inc(v.memset(hsb[:], 0.0))
        blk.vector(f)

        scan_done_prev = sV.n
        rs_dma_prev = 0

        for oc in range(n_oc):
            t0 = oc * OC

            def f(s, t0=t0, oc=oc):
                if oc > 0:
                    s.wait_ge(sV.s, sV.n)
                sIN.inc(s.dma_start(
                    out=xcf[:], in_=xf.rearrange("(c p) t -> p c t", p=128)[:, :, t0:t0 + OC]), 16)
            blk.sync(f)
            ld32 = sIN.n

            def f(g, t0=t0, oc=oc):
                if oc > 0:
                    g.wait_ge(sSC.s, sSC.n)
                sW.inc(g.dma_start(
                    out=xcb[:], in_=xf.rearrange("(c p) t -> p c t", p=128)[:, :, t0:t0 + OC]), 16)
            blk.gpsimd(f)
            ldbf = sW.n

            def f(v, oc=oc, sdp=scan_done_prev):
                v.wait_ge(sG.s, sG.n)
                v.wait_ge(sV.s, sdp)
                if oc == 0:
                    v.memset(xiw[:, :, 0:8], 0.0)
                    v.memset(xch[:, :, 0:8], 0.0)
                    v.memset(bdt[:, :, 0:8], 0.0)
                    sV.inc(v.memset(dmk[:, :, 0:8], 0.0))
                else:
                    v.tensor_copy(xiw[:, :, 0:8], xiw[:, :, OC:OC + 8])
                    v.tensor_copy(xch[:, :, 0:8], xch[:, :, OC:OC + 8])
                    v.tensor_copy(bdt[:, :, 0:8], bdt[:, :, OC:OC + 8])
                    sV.inc(v.tensor_copy(dmk[:, :, 0:8], dmk[:, :, OC:OC + 8]))
            blk.vector(f)
            tails_done = sV.n

            def f(s, ldbf=ldbf):
                s.wait_ge(sW.s, ldbf)
                sSC.inc(s.activation(sqb[:], xcb[:], AF.Square))
            blk.scalar(f)
            sq_done = sSC.n

            def f(t, ldbf=ldbf, sq_done=sq_done):
                t.wait_ge(sW.s, ldbf)
                t.wait_ge(sSC.s, sq_done)
                t.wait_ge(sV.s, sV.n)
                for dc in range(8):
                    t.matmul(P1[0:1, 0:OC], oncb[:], xcb[:, dc, :], start=(dc == 0), stop=(dc == 7))
                for dc in range(8):
                    mi = t.matmul(P6[0:1, 0:OC], oncb[:], sqb[:, dc, :], start=(dc == 0), stop=(dc == 7))
                sMM.inc(mi)
            blk.tensor(f)
            stat_done = sMM.n

            def f(v, stat_done=stat_done):
                v.wait_ge(sMM.s, stat_done)
                v.tensor_scalar(rowf[:, 0, :], P1[0:1, 0:OC], 1.0 / DM, 0.0, Op.mult, Op.add)
                v.tensor_scalar(rowf[:, 1, :], P6[0:1, 0:OC], 1.0 / DM, 0.0, Op.mult, Op.add)
                v.tensor_tensor(rowf[:, 2, :], rowf[:, 0, :], rowf[:, 0, :], Op.mult)
                sV.inc(v.tensor_tensor(rowf[:, 2, :], rowf[:, 1, :], rowf[:, 2, :], Op.subtract))
            blk.vector(f)
            var_done = sV.n

            def f(s, var_done=var_done):
                s.wait_ge(sV.s, var_done)
                sSC.inc(s.activation(rowf[:, 3, :], rowf[:, 2, :], AF.Sqrt, bias=EPS))
            blk.scalar(f)
            std_done = sSC.n

            def f(v, std_done=std_done):
                v.wait_ge(sSC.s, std_done)
                v.reciprocal(rowf[:, 4, :], rowf[:, 3, :])
                v.tensor_copy(rowb[:, 0, :], rowf[:, 0, :])
                sV.inc(v.tensor_copy(rowb[:, 1, :], rowf[:, 4, :]))
            blk.vector(f)
            rows_done = sV.n

            def f(t, rows_done=rows_done):
                t.wait_ge(sV.s, rows_done)
                t.matmul(P1[:, 0:OC], onrb[:], rowb[:, 0, :], start=True, stop=True)
                sMM.inc(t.matmul(P6[:, 0:OC], onrb[:], rowb[:, 1, :], start=True, stop=True))
            blk.tensor(f)
            bc_done = sMM.n

            def f(v, bc_done=bc_done, ld32=ld32, rdp=rs_dma_prev):
                v.wait_ge(sMM.s, bc_done)
                v.wait_ge(sIN.s, ld32)
                v.wait_ge(sRS.s, rdp)
                mb = P1[:, 0:OC].unsqueeze(1).broadcast_to((128, 8, OC))
                rb = P6[:, 0:OC].unsqueeze(1).broadcast_to((128, 8, OC))
                v.tensor_tensor(xcf[:], xcf[:], mb, Op.subtract)
                sV.inc(v.tensor_tensor(xnb, xcf[:], rb, Op.mult))
            blk.vector(f)
            xn_done = sV.n

            evac_marks = {}
            for ft in range(17):
                def f(t, ft=ft, xn_done=xn_done, tails_done=tails_done):
                    if ft == 0:
                        t.wait_ge(sV.s, xn_done)
                        t.wait_ge(sV.s, tails_done)
                        t.wait_ge(sSC.s, sSC.n)
                    if ft >= 2:
                        t.wait_ge(sSC.s, evac_marks[ft - 2])
                    pb = P0 if ft % 2 == 0 else P2
                    for dc in range(8):
                        mi = t.matmul(pb[:, 0:OC], winr[:, dc, ft, :], xnb[:, dc:dc+1, :].opt(),
                                      start=(dc == 0), stop=(dc == 7))
                    sMM.inc(mi)
                blk.tensor(f)
                mm_mark = sMM.n

                def f(s, ft=ft, mm_mark=mm_mark):
                    s.wait_ge(sMM.s, mm_mark)
                    pb = P0 if ft % 2 == 0 else P2
                    if ft < 4:
                        out = xiw[:, ft, 8:8 + OC]
                    elif ft < 8:
                        out = zbt[:, ft - 4, :]
                    elif ft < 12:
                        out = btt[:, ft - 8, :]
                    elif ft < 16:
                        out = ctt[:, ft - 12, :]
                    else:
                        out = dtsb[0:128, :]
                    sSC.inc(s.copy(out, pb[:, 0:OC]))
                blk.scalar(f)
                evac_marks[ft] = sSC.n
            proj_done = sSC.n

            def f(v, proj_done=proj_done):
                v.wait_ge(sSC.s, proj_done)
                for b in range(4):
                    for k in range(4):
                        xv = xiw[:, b, 5 + k:5 + k + OC]
                        wcol = cwc[:, b:b + 1, k]
                        if k == 0:
                            mi = v.tensor_scalar(ygt[:, b, :], xv, wcol, 0.0, Op.mult, Op.add)
                        else:
                            mi = v.scalar_tensor_tensor(ygt[:, b, :], xv, wcol, ygt[:, b, :],
                                                        Op.mult, Op.add)
                sV.inc(mi)
            blk.vector(f)
            conv_done = sV.n

            def f(s, conv_done=conv_done):
                s.wait_ge(sV.s, conv_done)
                for b in range(4):
                    mi = s.activation(xch[:, b, 8:8 + OC], ygt[:, b, :], AF.Silu, bias=cbc[:, b:b + 1])
                sSC.inc(mi)
            blk.scalar(f)
            xconv_done = sSC.n

            def f(s, proj_done=proj_done):
                s.wait_ge(sSC.s, proj_done)
                s.activation(dtaux[:, 2, :, :].rearrange('h b e -> h (b e)'), dtsb[0:8, :], AF.Exp, bias=dtbc[0:8, :])
                sSC.inc(s.activation(dtsb[0:8, :], dtaux[:, 2, :, :].rearrange('h b e -> h (b e)'), AF.Ln, bias=1.0))
            blk.scalar(f)
            dt_done = sSC.n

            def f(v, dt_done=dt_done):
                v.wait_ge(sSC.s, dt_done)
                dt8 = dtsb[0:8, :].rearrange("h (b e) -> h b e", e=8)
                inc = dtaux[:, 0, :, :]
                rex = dtaux[:, 1, :, :]
                tmp = dtaux[:, 2, :, :]
                v.tensor_copy(inc[:], dt8)
                v.tensor_tensor(tmp[:, :, 1:8], inc[:, :, 0:7], inc[:, :, 1:8], Op.add)
                v.tensor_copy(tmp[:, :, 0:1], inc[:, :, 0:1])
                v.tensor_tensor(inc[:, :, 2:8], tmp[:, :, 0:6], tmp[:, :, 2:8], Op.add)
                v.tensor_copy(inc[:, :, 0:2], tmp[:, :, 0:2])
                v.tensor_tensor(rex[:, :, 4:8], inc[:, :, 0:4], inc[:, :, 4:8], Op.add)
                v.tensor_copy(rex[:, :, 0:4], inc[:, :, 0:4])
                v.tensor_copy(inc[:], rex[:])
                v.tensor_tensor(rex[:], inc[:, :, 7:8].broadcast_to((8, OC // 8, 8)), inc[:], Op.subtract)
                v.tensor_copy(dsum[:, 0:64], inc[:, :, 7])
                v.tensor_reduce(dsum[:, 64:96], dtsb[0:8, :].rearrange("h (b e) -> h b e", e=16),
                                mybir.AxisListType.X, Op.add)
                v.tensor_reduce(dsum[:, 96:112], dtsb[0:8, :].rearrange("h (b e) -> h b e", e=32),
                                mybir.AxisListType.X, Op.add)
                v.tensor_reduce(dsum[:, 112:120], dtsb[0:8, :].rearrange("h (b e) -> h b e", e=64),
                                mybir.AxisListType.X, Op.add)
                sV.inc(v.tensor_reduce(dsum[:, 120:124],
                                       dtsb[0:8, :].rearrange("h (b e) -> h b e", e=128),
                                       mybir.AxisListType.X, Op.add))
            blk.vector(f)
            aux_done = sV.n

            base_marks = {}
            for b in range(4):
                def f(t, b=b, aux_done=aux_done, tails_done=tails_done):
                    if b == 0:
                        t.wait_ge(sV.s, aux_done)
                        t.wait_ge(sSC.s, sSC.n)
                    if b >= 2:
                        t.wait_ge(sSC.s, base_marks[b - 2])
                    pd, pi, pr = (P4, P0, P1) if b % 2 == 0 else (P6, P2, P7)
                    t.matmul(pd[:, 0:OC], hmkc[:, b, :], dtsb[0:8, :], start=True, stop=True)
                    t.matmul(pi[:, 0:OC], hmkc[:, b, :], dtaux[:, 0, :, :].rearrange("h b e -> h (b e)"), start=True, stop=True)
                    sMM.inc(t.matmul(pr[:, 0:OC], hmkc[:, b, :], dtaux[:, 1, :, :].rearrange("h b e -> h (b e)"),
                                     start=True, stop=True))
                blk.tensor(f)
                bmm = sMM.n

                def f(s, b=b, bmm=bmm, tails_done=tails_done):
                    s.wait_ge(sMM.s, bmm)
                    pd, pi, pr = (P4, P0, P1) if b % 2 == 0 else (P6, P2, P7)
                    s.activation(dmk[:, b, 8:8 + OC], pd[:, 0:OC], AF.Exp, scale=acol[:, b:b + 1])
                    s.activation(cl[0][:, b, :], pi[:, 0:OC], AF.Exp, scale=acol[:, b:b + 1])
                    s.activation(blv[0][:, b, :], pr[:, 0:OC], AF.Exp, scale=acol[:, b:b + 1])
                    sSC.inc(s.copy(bdt[:, b, 8:8 + OC], pd[:, 0:OC]))
                blk.scalar(f)
                base_marks[b] = sSC.n
            base_done = sSC.n

            def f(v, base_done=base_done):
                v.wait_ge(sSC.s, base_done)
                v.tensor_tensor(bdt[:, :, 8:8 + OC], bdt[:, :, 8:8 + OC], btt[:], Op.mult)
                v.tensor_tensor(dmk[:, :, 8:8 + OC], dmk[:, :, 8:8 + OC],
                                bmkc[:].unsqueeze(1).broadcast_to((128, 4, OC)), Op.mult)
                v.tensor_tensor(cl[0][:], cl[0][:], ctt[:], Op.mult)
                sV.inc(v.tensor_tensor(blv[0][:], blv[0][:], bdt[:, :, 8:8 + OC], Op.mult))
            blk.vector(f)
            lvl_prev = sV.n

            lvl_cfg = [(0, 64, 8), (64, 32, 16), (96, 16, 32), (112, 8, 64)]
            for li, (off, nb, half) in enumerate(lvl_cfg):
                nb2 = nb // 2

                def f(t, off=off, nb=nb, nb2=nb2, lvl_prev=lvl_prev):
                    t.wait_ge(sV.s, lvl_prev)
                    t.wait_ge(sSC.s, sSC.n)
                    dsv = dsum[:, off:off + nb].rearrange("h (b two) -> h b two", two=2)
                    for b in range(4):
                        t.matmul(P4[:, b * 32:b * 32 + nb2], hmkc[:, b, :], dsv[:, :, 0],
                                 start=(b == 0), stop=(b == 3))
                        mi = t.matmul(P6[:, b * 32:b * 32 + nb2], hmkc[:, b, :], dsv[:, :, 1],
                                      start=(b == 0), stop=(b == 3))
                    sMM.inc(mi)
                blk.tensor(f)
                lmm = sMM.n

                def f(s, nb2=nb2, lmm=lmm):
                    s.wait_ge(sMM.s, lmm)
                    for b in range(4):
                        s.activation(cfac[:, b, 0:nb2], P4[:, b * 32:b * 32 + nb2],
                                     AF.Exp, scale=acol[:, b:b + 1])
                        mi = s.activation(cfacB[:, b, 0:nb2], P6[:, b * 32:b * 32 + nb2],
                                          AF.Exp, scale=acol[:, b:b + 1])
                    sSC.inc(mi)
                blk.scalar(f)
                lact = sSC.n

                def f(v, li=li, nb2=nb2, half=half, lact=lact):
                    v.wait_ge(sSC.s, lact)
                    v.tensor_copy(cl[li + 1][:], cl[li][:])
                    v.tensor_copy(blv[li + 1][:], blv[li][:])
                    cv = cl[li + 1][:].rearrange("p b (n w) -> p b n w", n=nb2)
                    bv = blv[li + 1][:].rearrange("p b (n w) -> p b n w", n=nb2)
                    for b in range(4):
                        cf = cfac[:, b, 0:nb2].unsqueeze(2)
                        bf = cfacB[:, b, 0:nb2].unsqueeze(2)
                        v.tensor_tensor(cv[:, b, :, half:], cv[:, b, :, half:],
                                        cf.broadcast_to((128, nb2, half)), Op.mult)
                        mi = v.tensor_tensor(bv[:, b, :, 0:half], bv[:, b, :, 0:half],
                                             bf.broadcast_to((128, nb2, half)), Op.mult)
                    sV.inc(mi)
                blk.vector(f)
                lvl_prev = sV.n
            chains_done = lvl_prev

            def f(t, chains_done=chains_done):
                t.wait_ge(sV.s, chains_done)
                t.wait_ge(sSC.s, sSC.n)
                for b in range(4):
                    mi = t.matmul(P4[:, b * 4:b * 4 + 4], hmkc[:, b, :], dsum[:, 120:124],
                                  start=(b == 0), stop=(b == 3))
                sMM.inc(mi)
            blk.tensor(f)
            lamm = sMM.n

            def f(s, lamm=lamm):
                s.wait_ge(sMM.s, lamm)
                for b in range(4):
                    mi = s.activation(lamt[:, b, :], P4[:, b * 4:b * 4 + 4], AF.Exp,
                                      scale=acol[:, b:b + 1])
                sSC.inc(mi)
            blk.scalar(f)
            lam_done = sSC.n

            band_mm_prev = sMM.n
            gb_mark = sSC.n
            gp_read_mark = sG.n
            for k in range(8):
                def f(v, k=k, chains_done=chains_done, xconv_done=xconv_done,
                      band_mm_prev=band_mm_prev):
                    if k == 0:
                        v.wait_ge(sV.s, chains_done)
                        v.wait_ge(sSC.s, xconv_done)
                    else:
                        v.wait_ge(sMM.s, band_mm_prev)
                    if k == 0:
                        mi = v.tensor_tensor(wbd[:], ctt[:], bdt[:, :, 8:8 + OC], Op.mult)
                    elif k == 1:
                        v.tensor_copy(ebd[:], dmk[:, :, 8:8 + OC])
                        v.tensor_tensor(wbd[:], ctt[:], ebd[:], Op.mult)
                        mi = v.tensor_tensor(wbd[:], wbd[:], bdt[:, :, 7:7 + OC], Op.mult)
                    else:
                        v.tensor_tensor(ebd[:], ebd[:], dmk[:, :, 9 - k:9 - k + OC], Op.mult)
                        v.tensor_tensor(wbd[:], ctt[:], ebd[:], Op.mult)
                        mi = v.tensor_tensor(wbd[:], wbd[:], bdt[:, :, 8 - k:8 - k + OC], Op.mult)
                    sV.inc(mi)
                blk.vector(f)
                wmark = sV.n

                for b in range(4):
                    def f(t, k=k, b=b, wmark=wmark, gb_mark=gb_mark):
                        t.wait_ge(sV.s, wmark)
                        if not (k == 0 and b == 0):
                            t.wait_ge(sSC.s, gb_mark)
                        sMM.inc(t.matmul(P7[:, 0:OC], monesb[:], wbd[:, b, :], start=True, stop=True))
                    blk.tensor(f)
                    gmmk = sMM.n

                    def f(s, gmmk=gmmk, gp_read_mark=gp_read_mark):
                        s.wait_ge(sMM.s, gmmk)
                        s.wait_ge(sG.s, gp_read_mark)
                        sSC.inc(s.copy(gbs[:], P7[:, 0:OC]))
                    blk.scalar(f)
                    gb_mark = sSC.n

                    def f(g, k=k, b=b, gb_mark=gb_mark, sdp=scan_done_prev):
                        g.wait_ge(sSC.s, gb_mark)
                        if k == 0 and b == 0:
                            g.wait_ge(sV.s, sdp)
                        sG.inc(g.tensor_tensor(gtp[:], gbs[:], xch[:, b, 8 - k:8 - k + OC], Op.mult))
                        if k == 0:
                            sG.inc(g.tensor_copy(ybd[:, b, :], gtp[:]))
                        else:
                            sG.inc(g.tensor_tensor(ybd[:, b, :], ybd[:, b, :], gtp[:], Op.add))
                    blk.gpsimd(f)
                    gp_read_mark = sG.n - 1
                band_mm_prev = sMM.n
            bands_done = sG.n

            for qc in range(OC // QC):
                q0 = qc * QC
                for h in range(HPC):
                    b = h // 2
                    r0 = 64 * (h % 2)

                    def f(t, b=b, r0=r0, q0=q0, qc=qc, h=h, chains_done=chains_done,
                          lam_done=lam_done, xconv_done=xconv_done):
                        if qc == 0 and h == 0:
                            t.wait_ge(sV.s, chains_done)
                            t.wait_ge(sSC.s, lam_done)
                            t.wait_ge(sSC.s, xconv_done)
                        t.wait_ge(sV.s, sV.n)
                        t.transpose(P5[:, 0:64], xch[r0:r0 + 64, b, 8 + q0:8 + q0 + QC],
                                    eyeb[r0:r0 + 64, r0:r0 + 64])
                        sMM.inc(t.transpose(P5[:, 64:128], blv[4][r0:r0 + 64, b, q0:q0 + QC],
                                            eyeb[r0:r0 + 64, r0:r0 + 64]))
                    blk.tensor(f)
                    tr_mark = sMM.n

                    def f(v, tr_mark=tr_mark):
                        v.wait_ge(sMM.s, tr_mark)
                        v.tensor_copy(xts[:], P5[:, 0:64])
                        sV.inc(v.tensor_copy(bts[:], P5[:, 64:128]))
                    blk.vector(f)
                    trc_mark = sV.n

                    def f(t, b=b, r0=r0, q0=q0, trc_mark=trc_mark):
                        t.wait_ge(sV.s, trc_mark)
                        t.matmul(P4[r0:r0 + 64, 0:QC], hsb[r0:r0 + 64, b, :],
                                 cl[4][r0:r0 + 64, b, q0:q0 + QC], start=True, stop=False)
                        t.matmul(P0[:, 0:QC], blv[0][r0:r0 + 64, b, q0:q0 + QC],
                                 cl[0][r0:r0 + 64, b, q0:q0 + QC], start=True, stop=True)
                        sMM.inc(t.matmul(P2[:, 0:QC], blv[1][r0:r0 + 64, b, q0:q0 + QC],
                                         cl[1][r0:r0 + 64, b, q0:q0 + QC], start=True, stop=True))
                    blk.tensor(f)
                    g01 = sMM.n

                    def f(v, g01=g01):
                        v.wait_ge(sMM.s, g01)
                        v.tensor_tensor(gsbA[:], P0[:, 0:QC], gmb[:, 0, :], Op.mult)
                        sV.inc(v.tensor_tensor(gsbB[:], P2[:, 0:QC], gmb[:, 1, :], Op.mult))
                    blk.vector(f)
                    ev01 = sV.n

                    def f(t, b=b, r0=r0, q0=q0, ev01=ev01):
                        t.wait_ge(sV.s, ev01)
                        t.matmul(P4[r0:r0 + 64, 0:QC], xts[:], gsbA[:], start=False, stop=False)
                        t.matmul(P4[r0:r0 + 64, 0:QC], xts[:], gsbB[:], start=False, stop=False)
                        t.matmul(P0[:, 0:QC], blv[2][r0:r0 + 64, b, q0:q0 + QC],
                                 cl[2][r0:r0 + 64, b, q0:q0 + QC], start=True, stop=True)
                        sMM.inc(t.matmul(P2[:, 0:QC], blv[3][r0:r0 + 64, b, q0:q0 + QC],
                                         cl[3][r0:r0 + 64, b, q0:q0 + QC], start=True, stop=True))
                    blk.tensor(f)
                    g23 = sMM.n

                    def f(v, g23=g23):
                        v.wait_ge(sMM.s, g23)
                        v.tensor_tensor(gsbA[:], P0[:, 0:QC], gmb[:, 2, :], Op.mult)
                        sV.inc(v.tensor_tensor(gsbB[:], P2[:, 0:QC], gmb[:, 3, :], Op.mult))
                    blk.vector(f)
                    ev23 = sV.n

                    def f(t, b=b, r0=r0, ev23=ev23):
                        t.wait_ge(sV.s, ev23)
                        t.matmul(P4[r0:r0 + 64, 0:QC], xts[:], gsbA[:], start=False, stop=False)
                        t.matmul(P4[r0:r0 + 64, 0:QC], xts[:], gsbB[:], start=False, stop=True)
                        sMM.inc(t.matmul(P6[r0:r0 + 64, 0:64], bts[:], xts[:], start=True, stop=True))
                    blk.tensor(f)
                    smm = sMM.n

                    def f(v, b=b, r0=r0, q0=q0, qc=qc, smm=smm, bands_done=bands_done):
                        v.wait_ge(sMM.s, smm)
                        v.wait_ge(sG.s, bands_done)
                        v.scalar_tensor_tensor(ygt[r0:r0 + 64, b, q0:q0 + QC],
                                               xch[r0:r0 + 64, b, 8 + q0:8 + q0 + QC],
                                               dcolc[r0:r0 + 64, b:b + 1], P4[r0:r0 + 64, 0:QC],
                                               Op.mult, Op.add)
                        v.tensor_tensor(ygt[r0:r0 + 64, b, q0:q0 + QC],
                                        ygt[r0:r0 + 64, b, q0:q0 + QC],
                                        ybd[r0:r0 + 64, b, q0:q0 + QC], Op.add)
                        v.scalar_tensor_tensor(hst[r0:r0 + 64, b, :], hst[r0:r0 + 64, b, :],
                                               lamt[r0:r0 + 64, b, qc:qc + 1], P6[r0:r0 + 64, 0:64],
                                               Op.mult, Op.add)
                        sV.inc(v.tensor_copy(hsb[r0:r0 + 64, b, :], hst[r0:r0 + 64, b, :]))
                    blk.vector(f)
            scan_done = sV.n
            scan_done_prev = scan_done

            def f(s):
                s.wait_ge(sSC.s, sSC.n)
                sSC.inc(s.activation(zbt[:], zbt[:], AF.Silu))
            blk.scalar(f)
            zs_done = sSC.n

            def f(v, zs_done=zs_done, scan_done=scan_done):
                v.wait_ge(sSC.s, zs_done)
                v.wait_ge(sV.s, scan_done)
                sV.inc(v.tensor_tensor(sqb[:, 0:4, :], ygt[:], zbt[:], Op.mult))
            blk.vector(f)
            yg_done = sV.n

            wo_marks = {}
            half_marks = {}
            for dt_ in range(8):
                def f(t, dt_=dt_, yg_done=yg_done, rdp=rs_dma_prev):
                    if dt_ == 0:
                        t.wait_ge(sV.s, yg_done)
                        t.wait_ge(sSC.s, sSC.n)
                        t.wait_ge(sRS.s, rdp)
                    if dt_ >= 2:
                        t.wait_ge(sSC.s, wo_marks[dt_ - 2])
                    pb = P0 if dt_ % 2 == 0 else P2
                    for cb in range(4):
                        mi = t.matmul(pb[:, 0:OC], wotr[:, cb, dt_, :], sqb[:, cb, :],
                                      start=(cb == 0), stop=(cb == 3))
                    sMM.inc(mi)
                blk.tensor(f)
                womm = sMM.n

                def f(s, dt_=dt_, womm=womm, oc=oc):
                    s.wait_ge(sMM.s, womm)
                    if dt_ >= 4:
                        s.wait_ge(sRS.s, half_marks[0])
                    pb = P0 if dt_ % 2 == 0 else P2
                    sSC.inc(s.copy(rsst[:, dt_ % 4, :], pb[:, 0:OC]))
                blk.scalar(f)
                wo_marks[dt_] = sSC.n
                if dt_ in (3, 7):
                    def f(s, oc=oc, dt_=dt_, wo_done=sSC.n):
                        s.wait_ge(sSC.s, wo_done)
                        shard, col = oc // 4, (oc % 4) * OC
                        d0 = 0 if dt_ == 3 else 4
                        sRS.inc(s.dma_start(
                            out=rs_in.rearrange("s (c p) t -> s p c t", p=128)[shard, :, d0:d0 + 4, col:col + OC],
                            in_=rsst[:]), 16)
                    blk.sync(f)
                    half_marks[0 if dt_ == 3 else 1] = sRS.n
            rs_dma_prev = sRS.n

            if tap in ("proj", "xconv", "dt", "yg", "mixpart", "ybd", "cl4", "bl4",
                       "cl0", "bl0", "lam", "dsum", "daux") and oc == 0:
                def f(g, tap=tap):
                    g.wait_ge(sSC.s, sSC.n)
                    g.wait_ge(sV.s, sV.n)
                    g.wait_ge(sMM.s, sMM.n)
                    srcv = dict(proj=btt[:], xconv=xch[:], dt=dtsb[:], yg=ygt[:],
                                mixpart=rsst[:], ybd=ybd[:], cl4=cl[4][:],
                                bl4=blv[4][:], cl0=cl[0][:], bl0=blv[0][:],
                                lam=lamt[:], dsum=dsum[:], daux=dtaux[:])[tap]
                    sW.inc(g.dma_start(out=tapt, in_=srcv), 16)
                blk.gpsimd(f)

        def f(g):
            g.wait_ge(sRS.s, sRS.n)
            sCC.inc(g.collective_compute(
                "ReduceScatter", Op.add,
                replica_groups=[[0, 1, 2, 3], [4, 5, 6, 7]],
                ins=[rs_in.opt()], outs=[rs_out.opt()]))
        blk.gpsimd(f)
        rs_done = sCC.n

        if tap in ("rsout", "rsin"):
            def f(g, tap=tap):
                g.wait_ge(sCC.s, sCC.n)
                srcv = rs_out[:, :] if tap == "rsout" else rs_in[:, :, :]
                sW.inc(g.dma_start(out=tapt, in_=srcv), 16)
            blk.gpsimd(f)

        for pn in range(4):
            c0 = pn * OC

            def f(s, pn=pn, c0=c0, rs_done=rs_done, outp=sOUT.n):
                if pn == 0:
                    s.wait_ge(sCC.s, rs_done)
                else:
                    s.wait_ge(sV.s, sV.n)
                    s.wait_ge(sOUT.s, outp)
                sIN.inc(s.dma_start(out=xcf[:], in_=rs_out.rearrange("(c p) t -> p c t", p=128)[:, :, c0:c0 + OC]), 16)
                sIN.inc(s.dma_start(out=ygt[:],
                                    in_=xres.rearrange("(c p) t -> p c t", p=128)[:, 0:4, c0:c0 + OC]), 16)
            blk.sync(f)
            ld2 = sIN.n

            def f(v, ld2=ld2):
                v.wait_ge(sIN.s, ld2)
                sV.inc(v.tensor_copy(xcb[:], xcf[:]))
            blk.vector(f)
            cb2 = sV.n

            def f(s, cb2=cb2):
                s.wait_ge(sV.s, cb2)
                sSC.inc(s.activation(sqb[:], xcb[:], AF.Square))
            blk.scalar(f)
            sq2 = sSC.n

            def f(t, cb2=cb2, sq2=sq2):
                t.wait_ge(sV.s, cb2)
                t.wait_ge(sSC.s, sq2)
                for dc in range(8):
                    t.matmul(P1[0:1, 0:OC], oncb[:], xcb[:, dc, :], start=(dc == 0), stop=(dc == 7))
                for dc in range(8):
                    mi = t.matmul(P6[0:1, 0:OC], oncb[:], sqb[:, dc, :], start=(dc == 0), stop=(dc == 7))
                sMM.inc(mi)
            blk.tensor(f)
            st2 = sMM.n

            def f(v, st2=st2):
                v.wait_ge(sMM.s, st2)
                v.tensor_scalar(rowf[:, 0, :], P1[0:1, 0:OC], 1.0 / DM, 0.0, Op.mult, Op.add)
                v.tensor_scalar(rowf[:, 1, :], P6[0:1, 0:OC], 1.0 / DM, 0.0, Op.mult, Op.add)
                v.tensor_tensor(rowf[:, 2, :], rowf[:, 0, :], rowf[:, 0, :], Op.mult)
                sV.inc(v.tensor_tensor(rowf[:, 2, :], rowf[:, 1, :], rowf[:, 2, :], Op.subtract))
            blk.vector(f)
            var2 = sV.n

            def f(s, var2=var2):
                s.wait_ge(sV.s, var2)
                sSC.inc(s.activation(rowf[:, 3, :], rowf[:, 2, :], AF.Sqrt, bias=EPS))
            blk.scalar(f)
            std2 = sSC.n

            def f(v, std2=std2):
                v.wait_ge(sSC.s, std2)
                v.reciprocal(rowf[:, 4, :], rowf[:, 3, :])
                v.tensor_copy(rowb[:, 0, :], rowf[:, 0, :])
                sV.inc(v.tensor_copy(rowb[:, 1, :], rowf[:, 4, :]))
            blk.vector(f)
            rw2 = sV.n

            def f(t, rw2=rw2):
                t.wait_ge(sV.s, rw2)
                t.matmul(P1[:, 0:OC], onrb[:], rowb[:, 0, :], start=True, stop=True)
                sMM.inc(t.matmul(P6[:, 0:OC], onrb[:], rowb[:, 1, :], start=True, stop=True))
            blk.tensor(f)
            bc2 = sMM.n

            def f(v, bc2=bc2):
                v.wait_ge(sMM.s, bc2)
                mb = P1[:, 0:OC].unsqueeze(1).broadcast_to((128, 8, OC))
                rb = P6[:, 0:OC].unsqueeze(1).broadcast_to((128, 8, OC))
                v.tensor_tensor(xcf[:], xcf[:], mb, Op.subtract)
                v.tensor_tensor(xcf[:], xcf[:], rb, Op.mult)
                sV.inc(v.tensor_tensor(xcf[:, 0:4, :], xcf[:, 0:4, :], ygt[:], Op.add))
            blk.vector(f)
            h1 = sV.n

            def f(s, c0=c0, h1=h1):
                s.wait_ge(sV.s, h1)
                sIN.inc(s.dma_start(out=ygt[:],
                                    in_=xres.rearrange("(c p) t -> p c t", p=128)[:, 4:8, c0:c0 + OC]), 16)
            blk.sync(f)
            ld3 = sIN.n

            def f(v, ld3=ld3):
                v.wait_ge(sIN.s, ld3)
                sV.inc(v.tensor_tensor(xcf[:, 4:8, :], xcf[:, 4:8, :], ygt[:], Op.add))
            blk.vector(f)
            x2done = sV.n

            def f(s, c0=c0, x2done=x2done):
                s.wait_ge(sV.s, x2done)
                sOUT.inc(s.dma_start(
                    out=x2_spill.rearrange("(c p) t -> p c t", p=128)[:, :, c0:c0 + OC],
                    in_=xcf[:]), 16)
            blk.scalar(f)

        stB.close()

        stC = contextlib.ExitStack()
        sbufC = lambda name, shape, dt=F32: stC.enter_context(nc.sbuf_tensor(name, shape, dt))
        x2b = sbufC("x2b", [128, 8, TLOC], BF16)
        outb = sbufC("outb", [128, 8, TLOC], BF16)
        fsem = sbufC("fsem", [NE, TLOC])
        fseb = sbufC("fseb", [NE, TLOC], BF16)
        stmt = sbufC("stmt", [128, 16, NE])
        tausb = sbufC("tausb", [1, NE])
        onrf = sbufC("onrf", [1, 128])

        stD = contextlib.ExitStack()
        sbufD = lambda name, shape, dt=F32: stD.enter_context(nc.sbuf_tensor(name, shape, dt))
        x2p = sbufD("x2p", [128, 8, OC])
        sqp = sbufD("sqp", [128, 8, OC], BF16)
        x2pb = sbufD("x2pb", [128, 8, OC], BF16)
        rwf = sbufD("rwf", [1, 5, OC])
        rwb = sbufD("rwb", [1, 2, OC], BF16)
        wrb = sbufD("wrb", [128, 8, NE])
        brc = sbufD("brc", [NE, 1])
        lgt = sbufD("lgt", [NE, TLOC])
        ltm = sbufD("ltm", [128, 16, NE])
        lrow = sbufD("lrow", [128, 16, 2])
        semj = sbufD("semj", [NE, TLOC])
        gath = sbufD("gath", [128, 4, TLOC])
        selb = sbufD("selb", [128, 4, 16])
        stg = sbufD("stg", [16, OC])
        sorc = sbufD("sorc", [128, 8, 128])
        sortmp = sbufD("sortmp", [128, 512])
        sgnc = sbufD("sgnc", [128, 7])
        gssb = sbufD("gssb", [2, 1024])
        hmsk = sbufD("hmsk", [128, 2])
        gso = sbufD("gso", [64, 256])
        sor2 = sbufD("sor2", [128, 128])
        sor2t = sbufD("sor2t", [128, 64])
        iotc = sbufD("iotc", [128, 128])
        seltc = sbufD("seltc", [64, 8])
        kth = sbufD("kth", [64, 3])
        eqt = sbufD("eqt", [128, 128])
        rb128 = sbufD("rb128", [128, 8])
        tfix = sbufD("tfix", [1, 3, 8])

        for pn in range(4):
            c0 = pn * OC

            def f(s, pn=pn, c0=c0):
                if pn == 0:
                    s.wait_ge(sOUT.s, sOUT.n)
                    sIN.inc(s.dma_start(out=sgnc[:], in_=sgnp[:, :]), 16)
                    sIN.inc(s.dma_start(out=iotc[:], in_=iotag[:, :]), 16)
                    sIN.inc(s.dma_start(out=selb[:], in_=selt[:, :, :]), 16)
                    sIN.inc(s.dma_start(out=brc[:], in_=brv[:, :]), 16)
                    sIN.inc(s.dma_start(out=onrf[:], in_=onesrow[:, :]), 16)
                else:
                    s.wait_ge(sV.s, sV.n)
                sIN.inc(s.dma_start(out=x2p[:], in_=x2_spill.rearrange("(c p) t -> p c t", p=128)[:, :, c0:c0 + OC]), 16)
            blk.sync(f)
            ld = sIN.n

            def f(g, pn=pn, outp=sOUT.n):
                if pn == 0:
                    g.wait_ge(sOUT.s, outp)
                    sW.inc(g.dma_start(out=wrb[:], in_=wr.rearrange("(c p) e -> p c e", p=128)), 16)
            blk.gpsimd(f)
            wrld = sW.n

            def f(v, ld=ld):
                v.wait_ge(sIN.s, ld)
                sV.inc(v.tensor_copy(x2pb[:], x2p[:]))
            blk.vector(f)
            cb = sV.n

            def f(s, cb=cb):
                s.wait_ge(sV.s, cb)
                sSC.inc(s.activation(sqp[:], x2pb[:], AF.Square))
            blk.scalar(f)
            sq = sSC.n

            def f(t, cb=cb, sq=sq, pn=pn):
                t.wait_ge(sV.s, cb)
                t.wait_ge(sSC.s, sq)
                if pn > 0:
                    t.wait_ge(sV.s, sV.n)
                for dc in range(8):
                    t.matmul(P1[0:1, 0:OC], oncb[:], x2pb[:, dc, :], start=(dc == 0), stop=(dc == 7))
                for dc in range(8):
                    mi = t.matmul(P6[0:1, 0:OC], oncb[:], sqp[:, dc, :], start=(dc == 0), stop=(dc == 7))
                sMM.inc(mi)
            blk.tensor(f)
            stt = sMM.n

            def f(v, stt=stt):
                v.wait_ge(sMM.s, stt)
                v.tensor_scalar(rwf[:, 0, :], P1[0:1, 0:OC], 1.0 / DM, 0.0, Op.mult, Op.add)
                v.tensor_scalar(rwf[:, 1, :], P6[0:1, 0:OC], 1.0 / DM, 0.0, Op.mult, Op.add)
                v.tensor_tensor(rwf[:, 2, :], rwf[:, 0, :], rwf[:, 0, :], Op.mult)
                sV.inc(v.tensor_tensor(rwf[:, 2, :], rwf[:, 1, :], rwf[:, 2, :], Op.subtract))
            blk.vector(f)
            vr = sV.n

            def f(s, vr=vr):
                s.wait_ge(sV.s, vr)
                sSC.inc(s.activation(rwf[:, 3, :], rwf[:, 2, :], AF.Sqrt, bias=EPS))
            blk.scalar(f)
            sd = sSC.n

            def f(v, sd=sd):
                v.wait_ge(sSC.s, sd)
                v.reciprocal(rwf[:, 4, :], rwf[:, 3, :])
                v.tensor_copy(rwb[:, 0, :], rwf[:, 0, :])
                sV.inc(v.tensor_copy(rwb[:, 1, :], rwf[:, 4, :]))
            blk.vector(f)
            rw = sV.n

            def f(t, rw=rw):
                t.wait_ge(sV.s, rw)
                t.matmul(P1[:, 0:OC], onrb[:], rwb[:, 0, :], start=True, stop=True)
                sMM.inc(t.matmul(P6[:, 0:OC], onrb[:], rwb[:, 1, :], start=True, stop=True))
            blk.tensor(f)
            bc = sMM.n

            def f(v, bc=bc, pn=pn, c0=c0):
                v.wait_ge(sMM.s, bc)
                mb = P1[:, 0:OC].unsqueeze(1).broadcast_to((128, 8, OC))
                rb = P6[:, 0:OC].unsqueeze(1).broadcast_to((128, 8, OC))
                v.tensor_tensor(x2p[:], x2p[:], mb, Op.subtract)
                v.tensor_tensor(x2p[:], x2p[:], rb, Op.mult)
                sV.inc(v.tensor_copy(x2b[:, :, c0:c0 + OC], x2p[:]))
            blk.vector(f)
            xb_done = sV.n

            def f(t, c0=c0, xb_done=xb_done, wrld=wrld):
                t.wait_ge(sV.s, xb_done)
                t.wait_ge(sW.s, wrld)
                t.wait_ge(sSC.s, sSC.n)
                t.wait_ge(sV.s, sV.n)
                for dc in range(8):
                    mi = t.matmul(P0[0:64, 0:OC], wrb[:, dc, :], x2p[:, dc, :],
                                  start=(dc == 0), stop=(dc == 7))
                sMM.inc(mi)
            blk.tensor(f)
            lmm = sMM.n

            def f(v, c0=c0, lmm=lmm):
                v.wait_ge(sMM.s, lmm)
                sV.inc(v.tensor_scalar(lgt[0:64, c0:c0 + OC], P0[0:64, 0:OC], brc[:, 0:1], 0.0,
                                       Op.add, Op.add))
            blk.vector(f)
        logits_done = sV.n

        for tc in range(16):
            def f(t, tc=tc, logits_done=logits_done):
                if tc == 0:
                    t.wait_ge(sV.s, logits_done)
                else:
                    t.wait_ge(sV.s, sV.n)
                sMM.inc(t.transpose(P1[:, 0:64], lgt[0:64, tc * 128:tc * 128 + 128],
                                    eyef[0:64, 0:64]))
            blk.tensor(f)
            trm = sMM.n

            def f(v, tc=tc, trm=trm):
                v.wait_ge(sMM.s, trm)
                v.tensor_copy(ltm[:, tc, :], P1[:, 0:64])
                sV.inc(v.tensor_reduce(lrow[:, tc, 0:1], ltm[:, tc, :], mybir.AxisListType.X, Op.max))
            blk.vector(f)

            def f(v, tc=tc, m=sV.n):
                v.wait_ge(sV.s, m)
                sV.inc(v.tensor_scalar(lrow[:, tc, 1:2], lrow[:, tc, 0:1], -1.0, 0.0, Op.mult, Op.add))
            blk.vector(f)
            mxd = sV.n

            def f(s, tc=tc, mxd=mxd):
                s.wait_ge(sV.s, mxd)
                sSC.inc(s.activation(ltm[:, tc, :], ltm[:, tc, :], AF.Exp, bias=lrow[:, tc, 1:2]))
            blk.scalar(f)
            exd = sSC.n

            def f(v, tc=tc, exd=exd):
                v.wait_ge(sSC.s, exd)
                sV.inc(v.tensor_reduce(lrow[:, tc, 0:1], ltm[:, tc, :], mybir.AxisListType.X, Op.add))
            blk.vector(f)

            def f(v, tc=tc, m=sV.n):
                v.wait_ge(sV.s, m)
                sV.inc(v.reciprocal(lrow[:, tc, 1:2], lrow[:, tc, 0:1]))
            blk.vector(f)

            def f(v, tc=tc, m=sV.n):
                v.wait_ge(sV.s, m)
                sV.inc(v.tensor_tensor(stmt[:, tc, :], ltm[:, tc, :],
                                       lrow[:, tc, 1:2].broadcast_to((128, NE)), Op.mult))
            blk.vector(f)
        smax_done = sV.n

        em_marks = {}
        for tc in range(16):
            def f(t, tc=tc, smax_done=smax_done):
                if tc == 0:
                    t.wait_ge(sV.s, smax_done)
                    t.wait_ge(sSC.s, sSC.n)
                if tc >= 2:
                    t.wait_ge(sSC.s, em_marks[tc - 2])
                pb = P0 if tc % 2 == 0 else P2
                sMM.inc(t.transpose(pb[0:64, 0:128], stmt[:, tc, :], eyef[:, :]))
            blk.tensor(f)
            trm = sMM.n

            def f(s, tc=tc, trm=trm):
                s.wait_ge(sMM.s, trm)
                pb = P0 if tc % 2 == 0 else P2
                sSC.inc(s.copy(semj[:, tc * 128:tc * 128 + 128], pb[0:64, 0:128]))
            blk.scalar(f)
            em_marks[tc] = sSC.n
        sem_done = sSC.n

        def f(s, sem_done=sem_done):
            s.wait_ge(sSC.s, sem_done)
            sRS.inc(s.dma_start(out=sc_in[:, :], in_=semj[:]), 16)
        blk.sync(f)
        scw = sRS.n

        def f(g, scw=scw):
            g.wait_ge(sRS.s, scw)
            sCC.inc(g.collective_compute(
                "AllGather", Op.bypass,
                replica_groups=[list(range(NCORE))],
                ins=[sc_in.opt()], outs=[sc_out.opt()]))
        blk.gpsimd(f)
        ag1 = sCC.n

        def f(s, ag1=ag1):
            s.wait_ge(sCC.s, ag1)
            sIN.inc(s.dma_start(out=gath[:], in_=sc_out.rearrange("(c p) t -> p c t", p=128)), 16)
        blk.sync(f)
        gld = sIN.n

        sel_marks = {}
        for i in range(4):
            for tc4 in range(4):
                def f(t, i=i, tc4=tc4, gld=gld):
                    if i == 0 and tc4 == 0:
                        t.wait_ge(sIN.s, gld)
                        t.wait_ge(sSC.s, sSC.n)
                    if (i * 4 + tc4) >= 2:
                        t.wait_ge(sSC.s, sel_marks[(i * 4 + tc4) - 2])
                    pb = P0 if (i * 4 + tc4) % 2 == 0 else P2
                    sMM.inc(t.matmul(pb[0:16, 0:OC], selb[:, i, :],
                                     gath[:, i, tc4 * OC:tc4 * OC + OC], start=True, stop=True))
                blk.tensor(f)
                smm = sMM.n

                def f(s, i=i, tc4=tc4, smm=smm):
                    s.wait_ge(sMM.s, smm)
                    s.wait_ge(sRS.s, sRS.n)
                    pb = P0 if (i * 4 + tc4) % 2 == 0 else P2
                    sSC.inc(s.copy(stg[:], pb[0:16, 0:OC]))
                blk.scalar(f)
                sel_marks[i * 4 + tc4] = sSC.n

                def f(s, i=i, tc4=tc4, mark=sSC.n):
                    s.wait_ge(sSC.s, mark)
                    sRS.inc(s.dma_start(out=sorstg[:, :], in_=stg[:]), 16)
                blk.sync(f)
                stgw = sRS.n

                def f(s, i=i, tc4=tc4, stgw=stgw):
                    s.wait_ge(sRS.s, stgw)
                    for jj in range(2):
                        j = 2 * i + jj
                        sRS.inc(s.dma_start(
                            out=sorc[j * 16 + 4 * tc4: j * 16 + 4 * tc4 + 4, :, :],
                            in_=sorstg.rearrange("(jj s) (p f) -> jj p s f", jj=2, f=128)[jj]), 16)
                blk.sync(f)
        sort_ready = sRS.n

        def negate_partitions(v, bbit):
            v.tensor_scalar(sorc[:], sorc[:], sgnc[:, bbit:bbit + 1], 0.0, Op.mult, Op.add)

        def negate_free(v, period_bit):
            half = 1 << period_bit
            vw = sorc[:].rearrange("p c (n w) -> p (c n) w", w=2 * half)
            v.tensor_scalar(vw[:, :, half:], vw[:, :, half:], -1.0, 0.0, Op.mult, Op.add)

        def cmpex(v, dfree):
            vw = sorc[:].rearrange("p c (n two d) -> p (c n) two d", two=2, d=dfree)
            ncn = vw.shape[1]
            tm = sortmp[:].rearrange("p (cn d) -> p cn d", d=dfree)[:, 0:ncn, :]
            v.tensor_tensor(tm, vw[:, :, 0, :], vw[:, :, 1, :], Op.min)
            v.tensor_tensor(vw[:, :, 1, :], vw[:, :, 0, :], vw[:, :, 1, :], Op.max)
            v.tensor_copy(vw[:, :, 0, :], tm)

        def pb_sel(c):
            return P0 if c % 2 == 0 else P2

        def emit_transpose_all(dep_v):
            for c in range(8):
                def f(t, c=c, dep_v=dep_v):
                    if c == 0:
                        t.wait_ge(sV.s, dep_v)
                        t.wait_ge(sSC.s, sSC.n)
                    else:
                        t.wait_ge(sV.s, sV.n)
                    sMM.inc(t.transpose(pb_sel(c)[:, 0:128], sorc[:, c, :], eyef[:, :]))
                blk.tensor(f)
                trm = sMM.n

                def f(v, c=c, trm=trm):
                    v.wait_ge(sMM.s, trm)
                    sV.inc(v.tensor_copy(sorc[:, c, :], pb_sel(c)[:, 0:128]))
                blk.vector(f)

        first_sort_op = [True]

        def emit_sweeps(ks, js, form_now):
            def f(v, ks=ks, js=js, form_now=form_now):
                if first_sort_op[0]:
                    v.wait_ge(sRS.s, sort_ready)
                    first_sort_op[0] = False
                else:
                    v.wait_ge(sMM.s, sMM.n)
                k = ks
                pp_bit, fs_bit = None, None
                if k < 14:
                    if form_now == "N":
                        if k >= 7:
                            fs_bit = k - 7
                        else:
                            pp_bit = k
                    else:
                        if k >= 7:
                            pp_bit = k - 7
                        else:
                            fs_bit = k
                if pp_bit is not None:
                    negate_partitions(v, pp_bit)
                if fs_bit is not None:
                    negate_free(v, fs_bit)
                for j in js:
                    d = (1 << (j - 7)) if form_now == "N" else (1 << j)
                    cmpex(v, d)
                if pp_bit is not None:
                    negate_partitions(v, pp_bit)
                if fs_bit is not None:
                    negate_free(v, fs_bit)
                sV.inc(v.engine_nop())
            blk.vector(f)

        form = "T"
        for k in range(1, 15):
            sweeps = list(range(k - 1, -1, -1))
            hi = [j for j in sweeps if j >= 7]
            lo = [j for j in sweeps if j < 7]
            if hi:
                if form != "N":
                    emit_transpose_all(sV.n)
                    form = "N"
                emit_sweeps(k, hi, "N")
            if lo:
                if form != "T":
                    emit_transpose_all(sV.n)
                    form = "T"
                emit_sweeps(k, lo, "T")
        if form != "N":
            emit_transpose_all(sV.n)
            form = "N"
        sorted_done = sV.n

        def f(g, sorted_done=sorted_done):
            g.wait_ge(sV.s, sorted_done)
            sW.inc(g.dma_start(out=dbg_sort, in_=sorc[:]), 16)
        blk.gpsimd(f)

        def f(v, sorted_done=sorted_done):
            v.wait_ge(sV.s, sorted_done)
            v.memset(hmsk[:, :], 0.0)
            v.memset(hmsk[0:64, 0:1], 1.0)
            sV.inc(v.memset(hmsk[64:128, 1:2], 1.0))
        blk.vector(f)
        hm = sV.n

        def f(t, hm=hm):
            t.wait_ge(sV.s, hm)
            t.wait_ge(sSC.s, sSC.n)
            t.matmul(P1[0:2, 0:512], hmsk[:], sorc[:].rearrange("p c f -> p (c f)")[:, 0:512],
                     start=True, stop=True)
            sMM.inc(t.matmul(P6[0:2, 0:512], hmsk[:],
                             sorc[:].rearrange("p c f -> p (c f)")[:, 512:1024], start=True, stop=True))
        blk.tensor(f)
        gsm = sMM.n

        def f(v, gsm=gsm):
            v.wait_ge(sMM.s, gsm)
            v.tensor_copy(gssb[:, 0:512], P1[0:2, 0:512])
            sV.inc(v.tensor_copy(gssb[:, 512:1024], P6[0:2, 0:512]))
        blk.vector(f)
        gse = sV.n

        def f(g, gse=gse):
            g.wait_ge(sV.s, gse)
            sW.inc(g.dma_start(out=dbg_gs, in_=gssb[:]), 16)
        blk.gpsimd(f)

        def f(s, gse=gse):
            s.wait_ge(sV.s, gse)
            sRS.inc(s.dma_start(
                out=gs_in.rearrange("c (two f) -> two c f", two=2),
                in_=gssb[:].rearrange("h (c f) -> h c f", f=128)), 16)
        blk.sync(f)
        gsw = sRS.n

        def f(g, gsw=gsw):
            g.wait_ge(sRS.s, gsw)
            sCC.inc(g.collective_compute(
                "AllGather", Op.bypass,
                replica_groups=[list(range(NCORE))],
                ins=[gs_in.opt()], outs=[gs_out.opt()]))
        blk.gpsimd(f)
        ag2 = sCC.n

        def f(s, ag2=ag2):
            s.wait_ge(sCC.s, ag2)
            sIN.inc(s.dma_start(out=gso[:], in_=gs_out[:, :]), 16)
            sIN.inc(s.dma_start(out=sor2[:],
                                in_=gs_out.rearrange("e q -> (e q)").rearrange("(p f) -> p f", f=128)), 16)
        blk.sync(f)
        g2ld = sIN.n

        def sor2_negate_partitions(v, bbit):
            v.tensor_scalar(sor2[:], sor2[:], sgnc[:, bbit:bbit + 1], 0.0, Op.mult, Op.add)

        def sor2_negate_free(v, period_bit):
            half = 1 << period_bit
            vw = sor2[:].rearrange("p (n w) -> p n w", w=2 * half)
            v.tensor_scalar(vw[:, :, half:], vw[:, :, half:], -1.0, 0.0, Op.mult, Op.add)

        def sor2_cmpex(v, dfree):
            vw = sor2[:].rearrange("p (n two d) -> p n two d", two=2, d=dfree)
            ncn = vw.shape[1]
            tm = sor2t[:].rearrange("p (n d) -> p n d", d=dfree)[:, 0:ncn, :]
            v.tensor_tensor(tm, vw[:, :, 0, :], vw[:, :, 1, :], Op.min)
            v.tensor_tensor(vw[:, :, 1, :], vw[:, :, 0, :], vw[:, :, 1, :], Op.max)
            v.tensor_copy(vw[:, :, 0, :], tm)

        first2 = [True]

        def emit2_transpose(dep):
            def f(t, dep=dep):
                t.wait_ge(sV.s, dep)
                t.wait_ge(sSC.s, sSC.n)
                sMM.inc(t.transpose(P1[:, 0:128], sor2[:], eyef[:, :]))
            blk.tensor(f)
            trm = sMM.n

            def f(v, trm=trm):
                v.wait_ge(sMM.s, trm)
                sV.inc(v.tensor_copy(sor2[:], P1[:, 0:128]))
            blk.vector(f)

        def emit2_sweeps(k, js, form_now):
            def f(v, k=k, js=js, form_now=form_now):
                if first2[0]:
                    v.wait_ge(sIN.s, g2ld)
                    first2[0] = False
                else:
                    v.wait_ge(sMM.s, sMM.n)
                pp_bit, fs_bit = None, None
                if k < 14:
                    if form_now == "N":
                        if k >= 7:
                            fs_bit = k - 7
                        else:
                            pp_bit = k
                    else:
                        if k >= 7:
                            pp_bit = k - 7
                        else:
                            fs_bit = k
                if pp_bit is not None:
                    sor2_negate_partitions(v, pp_bit)
                if fs_bit is not None:
                    sor2_negate_free(v, fs_bit)
                for j in js:
                    d = (1 << (j - 7)) if form_now == "N" else (1 << j)
                    sor2_cmpex(v, d)
                if pp_bit is not None:
                    sor2_negate_partitions(v, pp_bit)
                if fs_bit is not None:
                    sor2_negate_free(v, fs_bit)
                sV.inc(v.engine_nop())
            blk.vector(f)

        form2 = "T"
        for k in range(1, 15):
            sweeps = list(range(k - 1, -1, -1))
            hi = [j for j in sweeps if j >= 7]
            lo = [j for j in sweeps if j < 7]
            if hi:
                if form2 != "N":
                    emit2_transpose(sV.n)
                    form2 = "N"
                emit2_sweeps(k, hi, "N")
            if lo:
                if form2 != "T":
                    emit2_transpose(sV.n)
                    form2 = "T"
                emit2_sweeps(k, lo, "T")
        if form2 != "N":
            emit2_transpose(sV.n)
            form2 = "N"
        g2sorted = sV.n

        def f(t, g2sorted=g2sorted):
            t.wait_ge(sV.s, g2sorted)
            t.wait_ge(sSC.s, sSC.n)
            sMM.inc(t.matmul(P1[0:64, 0:1], onrf[0:1, 0:64], sor2[0:1, 96:97], start=True, stop=True))
        blk.tensor(f)
        thm = sMM.n

        def f(v, thm=thm, g2ld=g2ld):
            v.wait_ge(sMM.s, thm)
            v.wait_ge(sIN.s, g2ld)
            sV.inc(v.tensor_copy(kth[:, 0:1], P1[0:64, 0:1]))
        blk.vector(f)

        def f(v, m=sV.n):
            v.wait_ge(sV.s, m)
            sV.inc(v.tensor_scalar(gso[:], gso[:], kth[:, 0:1], 0.0, Op.is_ge, Op.add))
        blk.vector(f)

        def f(v, m=sV.n):
            v.wait_ge(sV.s, m)
            sV.inc(v.tensor_reduce(kth[:, 1:2], gso[:], mybir.AxisListType.X, Op.add))
        blk.vector(f)

        def f(v, m=sV.n):
            v.wait_ge(sV.s, m)
            sV.inc(v.tensor_scalar(kth[:, 2:3], kth[:, 1:2], -64.0, 16384.0, Op.mult, Op.add))
        blk.vector(f)
        kd = sV.n

        def f(g, kd=kd):
            g.wait_ge(sV.s, kd)
            sW.inc(g.dma_start(out=dbg_kth, in_=kth[:]), 16)
        blk.gpsimd(f)

        def f(s):
            sIN.inc(s.dma_start(out=seltc[:], in_=seltau[:, :]), 16)
        blk.sync(f)
        stau_ld = sIN.n

        def f(t, kd=kd, stau_ld=stau_ld):
            t.wait_ge(sV.s, kd)
            t.wait_ge(sIN.s, stau_ld)
            t.matmul(P6[0:1, 0:8], kth[:, 2:3], seltc[:], start=True, stop=False)
            sMM.inc(t.matmul(P6[0:1, 8:16], kth[:, 1:2], seltc[:], start=False, stop=True))
        blk.tensor(f)
        rm = sMM.n

        def f(v, rm=rm):
            v.wait_ge(sMM.s, rm)
            v.tensor_copy(tfix[:, 0, :], P6[0:1, 0:8])
            sV.inc(v.tensor_copy(tfix[:, 1, :], P6[0:1, 8:16]))
        blk.vector(f)
        rrow = sV.n

        def f(t, rrow=rrow):
            t.wait_ge(sV.s, rrow)
            sMM.inc(t.matmul(P6[:, 16:24], onrf[0:1, :], tfix[:, 0, :], start=True, stop=True))
        blk.tensor(f)
        rbm = sMM.n

        def f(v, rbm=rbm):
            v.wait_ge(sMM.s, rbm)
            sV.inc(v.tensor_copy(rb128[:], P6[:, 16:24]))
        blk.vector(f)
        rbd = sV.n

        for s_ in range(8):
            def f(v, s_=s_, rbd=rbd):
                if s_ == 0:
                    v.wait_ge(sV.s, rbd)
                else:
                    v.wait_ge(sMM.s, sMM.n)
                sV.inc(v.tensor_scalar(eqt[:], iotc[:], rb128[:, s_:s_ + 1], 0.0, Op.is_equal, Op.add))
            blk.vector(f)

            def f(v, s_=s_, m=sV.n):
                v.wait_ge(sV.s, m)
                sV.inc(v.tensor_tensor(eqt[:], eqt[:], sorc[:, s_, :], Op.mult))
            blk.vector(f)
            eqd = sV.n

            def f(t, s_=s_, eqd=eqd):
                t.wait_ge(sV.s, eqd)
                sMM.inc(t.matmul(P7[0:1, 0:128], onc[:, 0:1], eqt[:], start=True, stop=True))
            blk.tensor(f)
            tmm = sMM.n

            def f(v, s_=s_, tmm=tmm):
                v.wait_ge(sMM.s, tmm)
                sV.inc(v.tensor_reduce(tfix[:, 2, s_:s_ + 1], P7[0:1, 0:128], mybir.AxisListType.X, Op.add))
            blk.vector(f)
        taud = sV.n

        def f(v, taud=taud):
            v.wait_ge(sV.s, taud)
            sV.inc(v.tensor_scalar(tfix[:, 1, :], tfix[:, 1, :], 0.5, 0.0, Op.is_ge, Op.add))
        blk.vector(f)

        def f(v, m=sV.n):
            v.wait_ge(sV.s, m)
            sV.inc(v.tensor_tensor(tfix[:, 2, :], tfix[:, 2, :], tfix[:, 1, :], Op.mult))
        blk.vector(f)

        def f(v, m=sV.n):
            v.wait_ge(sV.s, m)
            sV.inc(v.tensor_scalar(tfix[:, 1, :], tfix[:, 1, :], -1e30, 1e30, Op.mult, Op.add))
        blk.vector(f)

        def f(v, m=sV.n):
            v.wait_ge(sV.s, m)
            sV.inc(v.tensor_tensor(tfix[:, 2, :], tfix[:, 2, :], tfix[:, 1, :], Op.add))
        blk.vector(f)
        tfd = sV.n

        def f(g, tfd=tfd):
            g.wait_ge(sV.s, tfd)
            sW.inc(g.dma_start(out=dbg_tau, in_=tfix[:]), 16)
        blk.gpsimd(f)

        def f(s, tfd=tfd):
            s.wait_ge(sV.s, tfd)
            sRS.inc(s.dma_start(out=tau_in[0:1, 0:8], in_=tfix[:, 2, :]), 16)
        blk.sync(f)
        tw = sRS.n

        def f(g, tw=tw):
            g.wait_ge(sRS.s, tw)
            sCC.inc(g.collective_compute(
                "AllGather", Op.bypass,
                replica_groups=[list(range(NCORE))],
                ins=[tau_in.opt()], outs=[tau_out.opt()]))
        blk.gpsimd(f)
        ag3 = sCC.n

        def f(s, ag3=ag3):
            s.wait_ge(sCC.s, ag3)
            sIN.inc(s.dma_start(out=tausb[:].rearrange("o (c s) -> o c s", c=8),
                                in_=tau_out.rearrange("(c r) s -> r c s", r=8)[0:1]), 16)
        blk.sync(f)
        tld = sIN.n

        def f(t, tld=tld):
            t.wait_ge(sIN.s, tld)
            t.wait_ge(sSC.s, sSC.n)
            sMM.inc(t.matmul(P1[:, 0:64], onrf[:], tausb[:], start=True, stop=True))
        blk.tensor(f)
        tbm = sMM.n

        def f(v, tbm=tbm):
            v.wait_ge(sMM.s, tbm)
            v.tensor_tensor(ltm[:], stmt[:], P1[:, 0:64].unsqueeze(1).broadcast_to((128, 16, NE)), Op.is_ge)
            v.tensor_tensor(ltm[:], ltm[:], stmt[:], Op.mult)
            sV.inc(v.tensor_reduce(lrow[:, :, 0], ltm[:], mybir.AxisListType.X, Op.add))
        blk.vector(f)

        def f(v, m=sV.n):
            v.wait_ge(sV.s, m)
            sV.inc(v.tensor_scalar(lrow[:, :, 0], lrow[:, :, 0], 1.0, 1e-6, Op.mult, Op.add))
        blk.vector(f)

        def f(v, m=sV.n):
            v.wait_ge(sV.s, m)
            sV.inc(v.reciprocal(lrow[:, :, 1], lrow[:, :, 0]))
        blk.vector(f)

        def f(v, m=sV.n):
            v.wait_ge(sV.s, m)
            sV.inc(v.tensor_tensor(ltm[:], ltm[:],
                                   lrow[:, :, 1:2].broadcast_to((128, 16, NE)), Op.mult))
        blk.vector(f)
        fsd = sV.n

        fse_marks = {}
        for tc in range(16):
            def f(t, tc=tc, fsd=fsd):
                if tc == 0:
                    t.wait_ge(sV.s, fsd)
                    t.wait_ge(sSC.s, sSC.n)
                if tc >= 2:
                    t.wait_ge(sSC.s, fse_marks[tc - 2])
                pb = P0 if tc % 2 == 0 else P2
                sMM.inc(t.transpose(pb[0:64, 0:128], ltm[:, tc, :], eyef[:, :]))
            blk.tensor(f)
            trm = sMM.n

            def f(s, tc=tc, trm=trm):
                s.wait_ge(sMM.s, trm)
                pb = P0 if tc % 2 == 0 else P2
                s.copy(fsem[:, tc * 128:tc * 128 + 128], pb[0:64, 0:128])
                sSC.inc(s.copy(fseb[:, tc * 128:tc * 128 + 128], pb[0:64, 0:128]))
            blk.scalar(f)
            fse_marks[tc] = sSC.n
        fs_done = sSC.n

        if tap == "scores":
            def f(g):
                g.wait_ge(sSC.s, sSC.n)
                g.wait_ge(sV.s, sV.n)
                sW.inc(g.dma_start(out=tapt, in_=semj[:]), 16)
            blk.gpsimd(f)
        if tap == "sorted":
            def f(g):
                g.wait_ge(sV.s, sV.n)
                sW.inc(g.dma_start(out=tapt, in_=sorc[:]), 16)
            blk.gpsimd(f)
        if tap == "gs":
            def f(g):
                g.wait_ge(sV.s, sV.n)
                g.wait_ge(sRS.s, sRS.n)
                sW.inc(g.dma_start(out=tapt, in_=gs_in[:, :]), 16)
            blk.gpsimd(f)
        if tap == "tau":
            def f(g):
                g.wait_ge(sV.s, sV.n)
                g.wait_ge(sIN.s, sIN.n)
                sW.inc(g.dma_start(out=tapt, in_=tau_out[0:8, :]), 16)
            blk.gpsimd(f)
        if tap == "fs":
            def f(g):
                g.wait_ge(sSC.s, sSC.n)
                sW.inc(g.dma_start(out=tapt, in_=fsem[:]), 16)
            blk.gpsimd(f)

        stD.close()

        stE = contextlib.ExitStack()
        sbufE = lambda name, shape, dt=F32: stE.enter_context(nc.sbuf_tensor(name, shape, dt))
        hg = sbufE("hg", [128, 8, 2, 1024], BF16)
        wdnb = sbufE("wdnb", [128, 8, 2, 8, 128], BF16)
        wupb = [sbufE(f"wupb{i}", [128, 8, HID], BF16) for i in range(3)]
        wgtb = [sbufE(f"wgtb{i}", [128, 8, HID], BF16) for i in range(3)]
        fsbb = sbufE("fsbb", [128, 1024], BF16)
        bdnb = sbufE("bdnb", [64, 8, 128], BF16)
        bupc = sbufE("bupc", [128, 2, NE])
        bgtc = sbufE("bgtc", [128, 2, NE])
        upt2 = [sbufE(f"upt{i}", [128, 512], BF16) for i in range(2)]
        gtt = sbufE("gtt", [128, 512], BF16)
        fsrow = sbufE("fsrow", [1, 1024], BF16)

        def f(s, fs_done=fs_done):
            s.wait_ge(sSC.s, fs_done)
            sIN.inc(s.dma_start(out=bupc[:], in_=bupT[:, :, :]), 16)
            sIN.inc(s.dma_start(out=bgtc[:], in_=bgtT[:, :, :]), 16)
        blk.sync(f)
        bupc_ld = sIN.n

        def f(g, fs_done=fs_done):
            g.wait_ge(sSC.s, fs_done)
            sW.inc(g.dma_start(out=bdnb[:], in_=bdn.rearrange("e (d f) -> e d f", f=128)), 16)
        blk.gpsimd(f)
        bldw = sW.n

        def f(v):
            v.wait_ge(sV.s, sV.n)
            sV.inc(v.memset(outb[:], 0.0))
        blk.vector(f)
        ob0 = sV.n

        wld_marks = {}
        upgate_marks = {}
        ffn_it = 0
        pair_last_vec = [0, 0]
        prev_fbe = 0
        dn_p6_mark = 0
        for tp in range(2):
            toff = tp * 1024
            for gr in range(8):
                for e8 in range(8):
                    e = gr * 8 + e8
                    slot = (tp * 64 + gr * 8 + e8) % 3

                    def f(g, e=e, slot=slot, tp=tp, gr=gr, e8=e8, vmark=sV.n):
                        g.wait_ge(sV.s, vmark)
                        if tp == 0 and gr == 0 and e8 == 0:
                            g.wait_ge(sV.s, ob0)
                            g.wait_ge(sW.s, bldw)
                            g.wait_ge(sSC.s, sSC.n)
                            g.wait_ge(sMM.s, sMM.n)
                        gi = tp * 64 + gr * 8 + e8 - 3
                        if gi >= 0:
                            pt, pr_ = divmod(gi, 64)
                            pgr, pe8 = divmod(pr_, 8)
                            g.wait_ge(sMM.s, upgate_marks[(pt, pgr, pe8)])
                        sW.inc(g.dma_start(out=wupb[slot][:],
                                           in_=wup.rearrange("e (c p) h -> e p c h", p=128)[e]), 16)
                        sW.inc(g.dma_start(out=wgtb[slot][:],
                                           in_=wgt.rearrange("e (c p) h -> e p c h", p=128)[e]), 16)
                        sW.inc(g.dma_start(out=wdnb[:, e8, :, :, :],
                                           in_=wdn.rearrange("e (c p) (d f) -> e p c d f", p=128, f=128)[e]), 16)
                    blk.gpsimd(f)
                    wld_marks[(tp, e)] = sW.n

                    first_e = (tp == 0 and gr == 0 and e8 == 0)

                    def f(t, e=e, tp=tp, toff=toff, pfbe=prev_fbe, pdn=dn_p6_mark, first_e=first_e):
                        if first_e:
                            t.wait_ge(sSC.s, sSC.n)
                            t.wait_ge(sV.s, sV.n)
                        else:
                            t.wait_ge(sSC.s, pfbe)
                            t.wait_ge(sV.s, pdn)
                        t.matmul(P6[0:1, 0:512], eyeb[0:64, e:e + 1], fseb[:, toff:toff + 512],
                                 start=True, stop=True)
                        sMM.inc(t.matmul(P7[0:1, 0:512], eyeb[0:64, e:e + 1],
                                         fseb[:, toff + 512:toff + 1024], start=True, stop=True))
                    blk.tensor(f)
                    fbm = sMM.n

                    def f(s, fbm=fbm):
                        s.wait_ge(sMM.s, fbm)
                        s.copy(fsrow[:, 0:512], P6[0:1, 0:512])
                        sSC.inc(s.copy(fsrow[:, 512:1024], P7[0:1, 0:512]))
                    blk.scalar(f)
                    fbr = sSC.n

                    def f(t, fbr=fbr):
                        t.wait_ge(sSC.s, fbr)
                        t.matmul(P6[:, 0:512], onrb[:], fsrow[:, 0:512], start=True, stop=True)
                        sMM.inc(t.matmul(P7[:, 0:512], onrb[:], fsrow[:, 512:1024], start=True, stop=True))
                    blk.tensor(f)
                    fbm2 = sMM.n

                    def f(s, fbm2=fbm2):
                        s.wait_ge(sMM.s, fbm2)
                        s.copy(fsbb[:, 0:512], P6[:, 0:512])
                        sSC.inc(s.copy(fsbb[:, 512:1024], P7[:, 0:512]))
                    blk.scalar(f)
                    fbe = sSC.n
                    prev_fbe = fbe

                    for h2 in range(2):
                        for sub in range(2):
                            pr = ffn_it % 2
                            PU = (P0, P1)[pr]
                            PG = (P2, P3)[pr]
                            plv = pair_last_vec[pr]
                            blanket = ffn_it < 2

                            def f(t, e=e, slot=slot, h2=h2, sub=sub, toff=toff,
                                  wldm=wld_marks[(tp, e)], PU=PU, PG=PG, plv=plv, blanket=blanket):
                                t.wait_ge(sW.s, wldm)
                                if blanket:
                                    t.wait_ge(sSC.s, sSC.n)
                                    t.wait_ge(sV.s, sV.n)
                                else:
                                    t.wait_ge(sV.s, plv)
                                for dc in range(8):
                                    t.matmul(PU[:, 0:512], wupb[slot][:, dc, h2 * 128:h2 * 128 + 128],
                                             x2b[:, dc, toff + sub * 512:toff + sub * 512 + 512],
                                             start=(dc == 0), stop=(dc == 7))
                                for dc in range(8):
                                    mi = t.matmul(PG[:, 0:512], wgtb[slot][:, dc, h2 * 128:h2 * 128 + 128],
                                                  x2b[:, dc, toff + sub * 512:toff + sub * 512 + 512],
                                                  start=(dc == 0), stop=(dc == 7))
                                sMM.inc(mi)
                            blk.tensor(f)
                            ugm = sMM.n

                            def f(s, e=e, h2=h2, ugm=ugm, PU=PU, pr=pr, plv=plv, first=(ffn_it == 0)):
                                if first:
                                    s.wait_ge(sIN.s, bupc_ld)
                                s.wait_ge(sMM.s, ugm)
                                s.wait_ge(sV.s, plv)
                                sSC.inc(s.activation(upt2[pr][:], PU[:, 0:512], AF.Silu,
                                                     bias=bupc[:, h2, e:e + 1]))
                            blk.scalar(f)
                            uge = sSC.n

                            def f(v, e=e, e8=e8, h2=h2, sub=sub, uge=uge, ugm=ugm, PG=PG, pr=pr):
                                v.wait_ge(sSC.s, uge)
                                v.wait_ge(sMM.s, ugm)
                                v.tensor_scalar(gtt[:], PG[:, 0:512], bgtc[:, h2, e:e + 1], 0.0,
                                                Op.add, Op.add)
                                v.tensor_tensor(upt2[pr][:], upt2[pr][:], gtt[:], Op.mult)
                                sV.inc(v.tensor_tensor(hg[:, e8, h2, sub * 512:sub * 512 + 512],
                                                       upt2[pr][:], fsbb[:, sub * 512:sub * 512 + 512], Op.mult))
                            blk.vector(f)
                            pair_last_vec[pr] = sV.n
                            ffn_it += 1
                    upgate_marks[(tp, gr, e8)] = sMM.n
                hg_done = sV.n

                dn_marks = {}
                for dt_ in range(8):
                    for sub in range(2):
                        def f(t, gr=gr, dt_=dt_, sub=sub, tp=tp, toff=toff, hg_done=hg_done):
                            if dt_ == 0 and sub == 0:
                                t.wait_ge(sV.s, hg_done)
                            if (dt_ * 2 + sub) >= 2:
                                t.wait_ge(sV.s, dn_marks[dt_ * 2 + sub - 2])
                            pb = P4 if (dt_ * 2 + sub) % 2 == 0 else P6
                            for e8 in range(8):
                                for h2 in range(2):
                                    last = (e8 == 7 and h2 == 1 and gr != 7)
                                    mi = t.matmul(pb[:, 0:512], wdnb[:, e8, h2, dt_, :],
                                                  hg[:, e8, h2, sub * 512:sub * 512 + 512],
                                                  start=(e8 == 0 and h2 == 0), stop=last)
                            if gr == 7:
                                mi = t.matmul(pb[:, 0:512], bdnb[:, dt_, :],
                                              fseb[:, toff + sub * 512:toff + sub * 512 + 512],
                                              start=False, stop=True)
                            sMM.inc(mi)
                        blk.tensor(f)
                        dnm = sMM.n

                        def f(v, dt_=dt_, sub=sub, tp=tp, toff=toff, dnm=dnm):
                            v.wait_ge(sMM.s, dnm)
                            pb = P4 if (dt_ * 2 + sub) % 2 == 0 else P6
                            sV.inc(v.tensor_tensor(outb[:, dt_, toff + sub * 512:toff + sub * 512 + 512],
                                                   outb[:, dt_, toff + sub * 512:toff + sub * 512 + 512],
                                                   pb[:, 0:512], Op.add))
                        blk.vector(f)
                        dn_marks[dt_ * 2 + sub] = sV.n
                        if (dt_ * 2 + sub) % 2 == 1:
                            dn_p6_mark = sV.n
        ffn_done = sV.n

        if tap == "ffnout":
            def f(g, ffn_done=ffn_done):
                g.wait_ge(sV.s, ffn_done)
                sW.inc(g.dma_start(out=tapt, in_=outb[:]), 16)
            blk.gpsimd(f)

        stE.close()

        stF = contextlib.ExitStack()
        sbufF = lambda name, shape, dt=F32: stF.enter_context(nc.sbuf_tensor(name, shape, dt))
        fout = sbufF("fout", [128, 8, OC])
        xf2 = sbufF("xf2", [128, 8, OC])
        sqf = sbufF("sqf", [128, 8, OC], BF16)
        rwf2 = sbufF("rwf2", [1, 5, OC])
        rwb2 = sbufF("rwb2", [1, 2, OC], BF16)

        for pn in range(4):
            c0 = pn * OC

            def f(s, pn=pn, c0=c0, ffn_done=ffn_done, outp=sOUT.n):
                if pn == 0:
                    s.wait_ge(sV.s, ffn_done)
                else:
                    s.wait_ge(sV.s, sV.n)
                    s.wait_ge(sOUT.s, outp)
                sIN.inc(s.dma_start(out=xf2[:], in_=x2_spill.rearrange("(c p) t -> p c t", p=128)[:, :, c0:c0 + OC]), 16)
            blk.sync(f)
            ld = sIN.n

            def f(s, pn=pn, c0=c0, ffn_done=ffn_done):
                s.wait_ge(sSC.s, sSC.n)
                if pn == 0:
                    s.wait_ge(sV.s, ffn_done)
                sSC.inc(s.activation(sqf[:], outb[:, :, c0:c0 + OC], AF.Square))
            blk.scalar(f)
            sq = sSC.n

            def f(t, sq=sq, c0=c0, pn=pn):
                t.wait_ge(sSC.s, sq)
                if pn > 0:
                    t.wait_ge(sV.s, sV.n)
                for dc in range(8):
                    t.matmul(P1[0:1, 0:OC], oncb[:], outb[:, dc, c0:c0 + OC], start=(dc == 0), stop=(dc == 7))
                for dc in range(8):
                    mi = t.matmul(P6[0:1, 0:OC], oncb[:], sqf[:, dc, :], start=(dc == 0), stop=(dc == 7))
                sMM.inc(mi)
            blk.tensor(f)
            stt = sMM.n

            def f(v, stt=stt):
                v.wait_ge(sMM.s, stt)
                v.tensor_scalar(rwf2[:, 0, :], P1[0:1, 0:OC], 1.0 / DM, 0.0, Op.mult, Op.add)
                v.tensor_scalar(rwf2[:, 1, :], P6[0:1, 0:OC], 1.0 / DM, 0.0, Op.mult, Op.add)
                v.tensor_tensor(rwf2[:, 2, :], rwf2[:, 0, :], rwf2[:, 0, :], Op.mult)
                sV.inc(v.tensor_tensor(rwf2[:, 2, :], rwf2[:, 1, :], rwf2[:, 2, :], Op.subtract))
            blk.vector(f)
            vr = sV.n

            def f(s, vr=vr):
                s.wait_ge(sV.s, vr)
                sSC.inc(s.activation(rwf2[:, 3, :], rwf2[:, 2, :], AF.Sqrt, bias=EPS))
            blk.scalar(f)
            sd = sSC.n

            def f(v, sd=sd):
                v.wait_ge(sSC.s, sd)
                v.reciprocal(rwf2[:, 4, :], rwf2[:, 3, :])
                v.tensor_copy(rwb2[:, 0, :], rwf2[:, 0, :])
                sV.inc(v.tensor_copy(rwb2[:, 1, :], rwf2[:, 4, :]))
            blk.vector(f)
            rw = sV.n

            def f(t, rw=rw):
                t.wait_ge(sV.s, rw)
                t.matmul(P1[:, 0:OC], onrb[:], rwb2[:, 0, :], start=True, stop=True)
                sMM.inc(t.matmul(P6[:, 0:OC], onrb[:], rwb2[:, 1, :], start=True, stop=True))
            blk.tensor(f)
            bc = sMM.n

            def f(v, bc=bc, ld=ld, c0=c0):
                v.wait_ge(sMM.s, bc)
                v.wait_ge(sIN.s, ld)
                mb = P1[:, 0:OC].unsqueeze(1).broadcast_to((128, 8, OC))
                rb = P6[:, 0:OC].unsqueeze(1).broadcast_to((128, 8, OC))
                v.tensor_tensor(fout[:], outb[:, :, c0:c0 + OC], mb, Op.subtract)
                v.tensor_tensor(fout[:], fout[:], rb, Op.mult)
                sV.inc(v.tensor_tensor(fout[:], fout[:], xf2[:], Op.add))
            blk.vector(f)
            fd = sV.n

            def f(s, c0=c0, fd=fd):
                s.wait_ge(sV.s, fd)
                sOUT.inc(s.dma_start(
                    out=yout.rearrange("(c p) t -> p c t", p=128)[:, :, c0:c0 + OC],
                    in_=fout[:]), 16)
            blk.scalar(f)

        def f(s):
            s.wait_ge(sOUT.s, sOUT.n)
        blk.scalar(f)

        stF.close()
        stC.close()

    return nc



def _hostprep(inputs, n_oc=16):
    L = n_oc * OC
    x = np.asarray(inputs["L0_attn_x"], dtype=np.float32)
    W_in = np.asarray(inputs["L0_attn_W_in"], dtype=np.float32)
    conv_w = np.asarray(inputs["L0_attn_conv_w"], dtype=np.float32)
    conv_b = np.asarray(inputs["L0_attn_conv_b"], dtype=np.float32)
    A_log = np.asarray(inputs["L0_attn_A_log"], dtype=np.float32)
    Dv = np.asarray(inputs["L0_attn_D"], dtype=np.float32)
    dt_b = np.asarray(inputs["L0_attn_dt_bias"], dtype=np.float32)
    W_out = np.asarray(inputs["L0_attn_W_out"], dtype=np.float32)

    eye = np.eye(128, dtype=np.float32)
    hmk = np.zeros((8, 4, 128), np.float32)
    for b in range(4):
        for pp in range(128):
            hmk[2 * b + pp // 64, b, pp] = 1.0
    mones = np.zeros((128, 128), np.float32)
    mones[0:64, 0:64] = 1.0
    mones[64:128, 64:128] = 1.0
    gm = np.zeros((4, 128, 128), np.float32)
    for li, mh in enumerate((8, 16, 32, 64)):
        bu = np.arange(128) // mh
        gm[li] = ((bu[:, None] % 2 == 0) & (bu[None, :] == bu[:, None] + 1)).astype(np.float32)
    bm = np.ones((128, OC), np.float32)
    bm[:, ::8] = 0.0
    sgn = np.ones((128, 7), np.float32)
    for bbit in range(7):
        sgn[:, bbit] = np.where(((np.arange(128) >> bbit) & 1) == 1, -1.0, 1.0)
    iog = (np.arange(128)[None, :] * 128 + np.arange(128)[:, None]).astype(np.float32)

    in_maps = []
    for c in range(NCORE):
        b, g = c // 4, c % 4
        hs = slice(8 * g, 8 * g + 8)
        ch = slice(512 * g, 512 * g + 512)
        xb = np.ascontiguousarray(x[b].T[:, :L])
        tok = slice(2048 * g, 2048 * g + 2048)
        xr = np.ascontiguousarray(x[b].T[:, tok])
        rows = [W_in[ch], W_in[2048 + 512 * g: 2048 + 512 * (g + 1)]]
        for h in range(8 * g, 8 * g + 8):
            rows.append(W_in[4096 + 129 * h: 4096 + 129 * h + 64])
        for h in range(8 * g, 8 * g + 8):
            rows.append(W_in[4096 + 129 * h + 64: 4096 + 129 * h + 128])
        rows.append(np.stack([W_in[4096 + 129 * h + 128] for h in range(8 * g, 8 * g + 8)]))
        wcat = np.concatenate(rows, axis=0)
        wpad = np.zeros((FOP, DM), np.float32)
        wpad[:wcat.shape[0]] = wcat
        w_in_c = np.ascontiguousarray(wpad.T)
        w_out_c = np.ascontiguousarray(W_out[:, ch].T)
        pidx = np.arange(128)
        cw = np.zeros((128, 4, 4), np.float32)
        cb = np.zeros((128, 4), np.float32)
        al = np.zeros((128, 4), np.float32)
        dc = np.zeros((128, 4), np.float32)
        for bi in range(4):
            chan = 512 * g + 128 * bi + pidx
            cw[:, bi] = conv_w[chan, 0]
            cb[:, bi] = conv_b[chan]
            dc[:, bi] = Dv[chan // 64]
            al[:, bi] = A_log[chan // 64, pidx % 64]
        dtb = np.zeros((128, 1), np.float32)
        dtb[0:8, 0] = dt_b[hs]
        sel = np.zeros((128, 4, 16), np.float32)
        for i in range(4):
            for jj in range(2):
                src_core = 2 * i + jj
                for s_ in range(8):
                    e_global = 8 * c + s_
                    sel[64 * jj + e_global, i, jj * 8 + s_] = 1.0
        m = dict(
            xf=xb, xres=xr, w_in=w_in_c, w_out=w_out_c, convw=cw, convb=cb,
            alog=al, dvec=dc, dtbias=dtb,
            wr=np.ascontiguousarray(np.asarray(inputs["L0_ffn_Wr"], dtype=np.float32)),
            brv=np.ascontiguousarray(np.asarray(inputs["L0_ffn_br"], dtype=np.float32).reshape(NE, 1)),
            wup=np.ascontiguousarray(np.asarray(inputs["L0_ffn_Wup"], dtype=np.float32)),
            wgt=np.ascontiguousarray(np.asarray(inputs["L0_ffn_Wgate"], dtype=np.float32)),
            wdn=np.ascontiguousarray(np.asarray(inputs["L0_ffn_Wdown"], dtype=np.float32)),
            bupT=np.ascontiguousarray(
                np.asarray(inputs["L0_ffn_bup"], dtype=np.float32).T.reshape(2, 128, NE).transpose(1, 0, 2)),
            bgtT=np.ascontiguousarray(
                np.asarray(inputs["L0_ffn_bgate"], dtype=np.float32).T.reshape(2, 128, NE).transpose(1, 0, 2)),
            bdn=np.ascontiguousarray(np.asarray(inputs["L0_ffn_bdown"], dtype=np.float32)),
            eyec=eye, hmk=hmk, mones=mones, gmaskc=gm, bmask=bm, sgnp=sgn,
            iotag=iog, selt=sel, seltau=np.eye(64, dtype=np.float32)[:, 8 * c:8 * c + 8].copy(),
            onescol=np.ones((128, 1), np.float32),
            onesrow=np.ones((1, 128), np.float32),
            constv=np.tile(np.array([[1e-5, 1.0, -1.0, 0.0]], np.float32), (128, 1)),
        )
        in_maps.append(m)
    return in_maps


_NC_CACHE = {}


def kernel(**inputs):
    in_maps = _hostprep(inputs)
    key = ("full", 16)
    if key not in _NC_CACHE:
        _NC_CACHE[key] = build_nc(n_oc=16, tap=None)
    nc = _NC_CACHE[key]
    res = run_bass_kernel_spmd(nc, in_maps, core_ids=list(range(NCORE)))
    out = np.zeros((BB, LSEQ, DM), np.float32)
    for c in range(NCORE):
        b, g = c // 4, c % 4
        out[b, 2048 * g:2048 * (g + 1), :] = res.results[c]["yout"].T
    return out
